# version 03
# baseline (speedup 1.0000x reference)
import numpy as np
import ml_dtypes
import concourse.bass as bass
import concourse.mybir as mybir
from concourse.bass_utils import run_bass_kernel_spmd

F32 = mybir.dt.float32
BF16 = mybir.dt.bfloat16
U8 = mybir.dt.uint8
AF = mybir.ActivationFunctionType
ALU = mybir.AluOpType

DC = 32
IC = 86
C_EXP = 20.0
SM_SCALE = float(1.0 / np.sqrt(128.0))
EPS = 1e-5
RN = 1.0 / 4096.0
KTS = [[0, 1, 2, 3, 8, 9, 10, 11, 12, 13, 14, 15], list(range(16))]
QUARTS = [(0, 22), (22, 22), (44, 21), (65, 21)]
OCTS = [(8 * i, min(8, IC - 8 * i)) for i in range((IC + 7) // 8)]

SEMS = ("s0", "sx", "sx2", "w0", "w1", "u0", "u1", "la", "lb", "lc", "ld",
        "sa", "sb", "lq", "lm", "sy", "dz",
        "pe", "act", "dve", "pool", "ph")
WS = ("w0", "w1")
US = ("u0", "u1")
LS = ("la", "lb")
LS2 = ("lc", "ld")
SS = ("sa", "sb")


def _off():
    per = {}
    o = 0
    def put(name, size):
        nonlocal o
        per[name] = o
        o += (size + 31) // 32 * 32
    put("ropeC", 4096)
    put("ropeS", 4096)
    put("consts", 64)
    put("ones_bf", 32)
    put("ones_f", 512)
    put("pswap", 256)
    put("ident", 256)
    put("agt", 344)
    put("aut", 344)
    put("adt", 128)
    put("S1", 8192)
    put("S2", 8192)
    put("att1", 2048)
    ab = (o + 31) // 32 * 32
    ar = {
        "sq0": 0, "sq1": 4096,
        "r_sb": 0, "mr_sb": 4096, "t_dve": 8192, "t_pool": 12288,
        "LN1F": 40960,
        "wsl0": 0, "wsl1": 8192,
        "ksb0": 16384, "ksb1": 20480,
        "t1b": 24576, "t2b": 28672,
        "ko0": 32768, "ko1": 36864,
        "vt0": 0, "vt1": 4096, "v40": 8192, "v41": 12288,
        "qsb": 0,
        "mkb": 32768,
        "kb0": 49152, "kb1": 53248,
        "vb0": 57344, "vb1": 61440,
        "pt0": 65536, "pt1": 66560,
        "pm0": 67584, "pm1": 68608,
        "ob0": 69632, "ob1": 71680,
        "OT": 106496,
        "ars0": 16384, "ars1": 20480,
        "arb0": 24576, "arb1": 26624,
        "sqa0": 28672, "sqa1": 30720,
        "xs0": 0, "xs1": 4096,
        "as0": 8192, "as1": 12288,
        "t6a0": 16384, "t6a1": 20480,
        "t6b0": 24576, "t6b1": 28672,
        "x1s0": 32768, "x1s1": 36864,
        "r2sb": 4096, "mr2sb": 6144,
        "sq60": 40960, "sq61": 43008,
        "XH1": 106496,
        "t7d": 0, "t7p": 2048,
        "LN2": 40960,
        "wu0": 16384, "wu1": 24576,
        "silu0": 32768, "silu1": 34816,
        "up0": 36864, "up1": 38912,
        "h80": 106496, "h81": 108544,
        "hq0": 0, "hq1": 45056,
        "wd0": 90112, "wd1": 98304,
        "frs0": 106496, "frs1": 108544,
        "fb0": 0, "fb1": 1024,
        "sf0": 2048, "sf1": 3072,
        "x1c0": 4096, "x1c1": 6144,
        "frc0": 8192, "frc1": 10240,
        "tA0": 12288, "tA1": 14336,
        "tB0": 16384, "tB1": 18432,
        "os0": 20480, "os1": 22528,
    }
    return per, ab, ar


def build(dbg=False):
    nc = bass.Bass("TRN2", target_bir_lowering=False)
    ik = "ExternalOutput" if dbg else "Internal"

    xbt = nc.dram_tensor("xbt", [4096, 2048], F32, kind="ExternalInput").ap()
    wq = nc.dram_tensor("wq", [4096, 4096], F32, kind="ExternalInput").ap()
    wk = nc.dram_tensor("wk", [4096, 4096], F32, kind="ExternalInput").ap()
    wv = nc.dram_tensor("wv", [4096, 4096], F32, kind="ExternalInput").ap()
    wo = nc.dram_tensor("wo", [4096, 4096], F32, kind="ExternalInput").ap()
    wgT = nc.dram_tensor("wgT", [4096, 11008], F32, kind="ExternalInput").ap()
    wuT = nc.dram_tensor("wuT", [4096, 11008], F32, kind="ExternalInput").ap()
    wdT = nc.dram_tensor("wdT", [11008, 4096], F32, kind="ExternalInput").ap()
    agt_d = nc.dram_tensor("agt", [128, 86], F32, kind="ExternalInput").ap()
    aut_d = nc.dram_tensor("aut", [128, 86], F32, kind="ExternalInput").ap()
    adt_d = nc.dram_tensor("adt", [128, 32], F32, kind="ExternalInput").ap()
    ropec_d = nc.dram_tensor("ropec", [128, 2048], F32, kind="ExternalInput").ap()
    ropes_d = nc.dram_tensor("ropes", [128, 2048], F32, kind="ExternalInput").ap()
    masks_d = nc.dram_tensor("masks", [28, 128, 512], F32, kind="ExternalInput").ap()
    pswap_d = nc.dram_tensor("pswap", [128, 128], F32, kind="ExternalInput").ap()
    ident_d = nc.dram_tensor("ident", [128, 128], F32, kind="ExternalInput").ap()
    consts_d = nc.dram_tensor("consts", [128, 16], F32, kind="ExternalInput").ap()
    y_t = nc.dram_tensor("y_t", [4096, 1024], F32, kind="ExternalOutput").ap()

    q_d = nc.dram_tensor("q_d", [32, 128, 1024], BF16, kind=ik).ap()
    k_d = nc.dram_tensor("k_d", [32, 128, 2048], BF16, kind=ik).ap()
    vT_d = nc.dram_tensor("vT_d", [32, 128, 2048], BF16, kind=ik).ap()
    v4_d = nc.dram_tensor("v4_d", [32, 128, 16, 128], BF16, kind=ik).ap()
    ar_d = nc.dram_tensor("ar_d", [32, 128, 1024], F32, kind=ik).ap()
    x1_d = nc.dram_tensor("x1_d", [32, 128, 1024], F32, kind=ik).ap()
    h_d = nc.dram_tensor("h_d", [2, 86, 128, 512], BF16, kind=ik).ap()
    fr_d = nc.dram_tensor("fr_d", [2, 32, 128, 512], F32, kind=ik).ap()
    if dbg:
        dbg_ln1 = nc.dram_tensor("dbg_ln1", [32, 128, 2048], BF16, kind="ExternalOutput").ap()
        dbg_ot = nc.dram_tensor("dbg_ot", [32, 128, 1024], BF16, kind="ExternalOutput").ap()
        dbg_ln2 = nc.dram_tensor("dbg_ln2", [32, 128, 1024], BF16, kind="ExternalOutput").ap()
        dbg_rsb = nc.dram_tensor("dbg_rsb", [128, 4096], BF16, kind="ExternalOutput").ap()

    per, ab, ar = _off()
    space = nc.alloc_sbuf_tensor("space", [128, nc.sbuf_bytes_remaining - 64], U8)
    A = nc.lookup_mloc(space).addr

    def at(name, shape, dtype, off):
        return nc.alloc_sbuf_tensor_at(name, shape, dtype, offset=A + off)

    ropeC = at("ropeC", [128, 2048], BF16, per["ropeC"])
    ropeS = at("ropeS", [128, 2048], BF16, per["ropeS"])
    consts = at("consts", [128, 16], F32, per["consts"])
    ones_bf = at("ones_bf", [128, 1], BF16, per["ones_bf"])
    ones_f = at("ones_f", [1, 128], F32, per["ones_f"])
    pswap = at("pswap", [128, 128], BF16, per["pswap"])
    ident = at("ident", [128, 128], BF16, per["ident"])
    agt = at("agt_s", [128, 86], F32, per["agt"])
    aut = at("aut_s", [128, 86], F32, per["aut"])
    adt = at("adt_s", [128, 32], F32, per["adt"])
    S1 = at("S1", [128, 2048], F32, per["S1"])
    S2 = at("S2", [128, 2048], F32, per["S2"])
    att1 = at("att1", [128, 512], F32, per["att1"])

    sq = [at("sq0", [128, 2048], BF16, ab + ar["sq0"]),
          at("sq1", [128, 2048], BF16, ab + ar["sq1"])]
    r_sb = at("r_sb", [128, 2048], BF16, ab + ar["r_sb"])
    mr_sb = at("mr_sb", [128, 2048], BF16, ab + ar["mr_sb"])
    t_dve = at("t_dve", [128, 2048], BF16, ab + ar["t_dve"])
    t_pool = at("t_pool", [128, 2048], BF16, ab + ar["t_pool"])
    LN1F = at("LN1F", [128, 32, 2048], BF16, ab + ar["LN1F"])
    wsl = [at("wsl0", [128, 32, 128], BF16, ab + ar["wsl0"]),
           at("wsl1", [128, 32, 128], BF16, ab + ar["wsl1"])]
    ksb = [at("ksb0", [128, 2048], BF16, ab + ar["ksb0"]),
           at("ksb1", [128, 2048], BF16, ab + ar["ksb1"])]
    t1b = at("t1b", [128, 2048], BF16, ab + ar["t1b"])
    t2b = at("t2b", [128, 2048], BF16, ab + ar["t2b"])
    ko = [at("ko0", [128, 2048], BF16, ab + ar["ko0"]),
          at("ko1", [128, 2048], BF16, ab + ar["ko1"])]
    vt = [at("vt0", [128, 2048], BF16, ab + ar["vt0"]),
          at("vt1", [128, 2048], BF16, ab + ar["vt1"])]
    v4s = [at("v40", [128, 2048], BF16, ab + ar["v40"]),
           at("v41", [128, 2048], BF16, ab + ar["v41"])]
    qsb = at("qsb", [128, 32, 512], BF16, ab + ar["qsb"])
    mkb = at("mkb", [128, 16, 512], BF16, ab + ar["mkb"])
    kb = [at("kb0", [128, 2048], BF16, ab + ar["kb0"]),
          at("kb1", [128, 2048], BF16, ab + ar["kb1"])]
    vb = [at("vb0", [128, 16, 128], BF16, ab + ar["vb0"]),
          at("vb1", [128, 16, 128], BF16, ab + ar["vb1"])]
    pt = [at("pt0", [128, 512], BF16, ab + ar["pt0"]),
          at("pt1", [128, 512], BF16, ab + ar["pt1"])]
    pm = [at("pm0", [128, 512], BF16, ab + ar["pm0"]),
          at("pm1", [128, 512], BF16, ab + ar["pm1"])]
    ob = [at("ob0", [128, 512], F32, ab + ar["ob0"]),
          at("ob1", [128, 512], F32, ab + ar["ob1"])]
    OT = at("OT", [128, 32, 1024], BF16, ab + ar["OT"])
    ars = [at("ars0", [128, 1024], F32, ab + ar["ars0"]),
           at("ars1", [128, 1024], F32, ab + ar["ars1"])]
    arb = [at("arb0", [128, 1024], BF16, ab + ar["arb0"]),
           at("arb1", [128, 1024], BF16, ab + ar["arb1"])]
    sqa = [at("sqa0", [128, 1024], BF16, ab + ar["sqa0"]),
           at("sqa1", [128, 1024], BF16, ab + ar["sqa1"])]
    xs = [at("xs0", [128, 1024], F32, ab + ar["xs0"]),
          at("xs1", [128, 1024], F32, ab + ar["xs1"])]
    as_ = [at("as0", [128, 1024], F32, ab + ar["as0"]),
           at("as1", [128, 1024], F32, ab + ar["as1"])]
    t6a = [at("t6a0", [128, 1024], F32, ab + ar["t6a0"]),
           at("t6a1", [128, 1024], F32, ab + ar["t6a1"])]
    t6b = [at("t6b0", [128, 1024], F32, ab + ar["t6b0"]),
           at("t6b1", [128, 1024], F32, ab + ar["t6b1"])]
    x1s = [at("x1s0", [128, 1024], F32, ab + ar["x1s0"]),
           at("x1s1", [128, 1024], F32, ab + ar["x1s1"])]
    r2sb = at("r2sb", [128, 1024], BF16, ab + ar["r2sb"])
    mr2sb = at("mr2sb", [128, 1024], BF16, ab + ar["mr2sb"])
    sq6 = [at("sq60", [128, 1024], BF16, ab + ar["sq60"]),
           at("sq61", [128, 1024], BF16, ab + ar["sq61"])]
    XH1 = at("XH1", [128, 32, 1024], BF16, ab + ar["XH1"])
    t7d = at("t7d", [128, 1024], BF16, ab + ar["t7d"])
    t7p = at("t7p", [128, 1024], BF16, ab + ar["t7p"])
    LN2 = at("LN2", [128, 32, 1024], BF16, ab + ar["LN2"])
    wu = [at("wu0", [128, 32, 128], BF16, ab + ar["wu0"]),
          at("wu1", [128, 32, 128], BF16, ab + ar["wu1"])]
    silu = [at("silu0", [128, 1024], BF16, ab + ar["silu0"]),
            at("silu1", [128, 1024], BF16, ab + ar["silu1"])]
    up = [at("up0", [128, 1024], BF16, ab + ar["up0"]),
          at("up1", [128, 1024], BF16, ab + ar["up1"])]
    h8 = [at("h80", [128, 1024], BF16, ab + ar["h80"]),
          at("h81", [128, 1024], BF16, ab + ar["h81"])]
    hq = [at("hq0", [128, 22, 1024], BF16, ab + ar["hq0"]),
          at("hq1", [128, 22, 1024], BF16, ab + ar["hq1"])]
    wd = [at("wd0", [128, 8, 512], BF16, ab + ar["wd0"]),
          at("wd1", [128, 8, 512], BF16, ab + ar["wd1"])]
    frs = [at("frs0", [128, 512], F32, ab + ar["frs0"]),
           at("frs1", [128, 512], F32, ab + ar["frs1"])]
    fb = [at("fb0", [128, 512], BF16, ab + ar["fb0"]),
          at("fb1", [128, 512], BF16, ab + ar["fb1"])]
    sf = [at("sf0", [128, 512], BF16, ab + ar["sf0"]),
          at("sf1", [128, 512], BF16, ab + ar["sf1"])]
    x1c = [at("x1c0", [128, 512], F32, ab + ar["x1c0"]),
           at("x1c1", [128, 512], F32, ab + ar["x1c1"])]
    frc = [at("frc0", [128, 512], F32, ab + ar["frc0"]),
           at("frc1", [128, 512], F32, ab + ar["frc1"])]
    tA = [at("tA0", [128, 512], F32, ab + ar["tA0"]),
          at("tA1", [128, 512], F32, ab + ar["tA1"])]
    tB = [at("tB0", [128, 512], F32, ab + ar["tB0"]),
          at("tB1", [128, 512], F32, ab + ar["tB1"])]
    osb = [at("os0", [128, 512], F32, ab + ar["os0"]),
           at("os1", [128, 512], F32, ab + ar["os1"])]

    ps_h = nc.alloc_psum_tensor("psall", [128, 4096], F32)

    def bank(b, p0=0, pn=128, n=512):
        return ps_h[p0:p0 + pn, 512 * b:512 * b + n]

    sems = {}

    def walk(E, e):
        cnt = {k: 0 for k in SEMS}

        def W(sem, val):
            if val and val > 0:
                e.wait_ge(sems[sem], val)

        def DMA(eng, sem, outf, inf, waits=()):
            cnt[sem] += 16
            if eng == E:
                for (s, v) in waits:
                    W(s, v)
                e.dma_start(out=outf(), in_=inf()).then_inc(sems[sem], 16)
            return cnt[sem]

        def OP(eng, fn, sem=None, waits=()):
            if sem:
                cnt[sem] += 1
            if eng == E:
                for (s, v) in waits:
                    W(s, v)
                ins = fn(e)
                if sem:
                    ins.then_inc(sems[sem], 1)
            return cnt[sem] if sem else None

        def BAR():
            cnt["ph"] += 5
            e.nop().then_inc(sems["ph"], 1)
            e.wait_ge(sems["ph"], cnt["ph"])

        def SSFULL():
            return [("sa", cnt["sa"]), ("sb", cnt["sb"])]

        DMA("gpsimd", "s0", lambda: ropeC[:, :], lambda: ropec_d)
        DMA("gpsimd", "s0", lambda: ropeS[:, :], lambda: ropes_d)
        DMA("gpsimd", "s0", lambda: consts[:, :], lambda: consts_d)
        DMA("gpsimd", "s0", lambda: pswap[:, :], lambda: pswap_d)
        DMA("gpsimd", "s0", lambda: ident[:, :], lambda: ident_d)
        DMA("gpsimd", "s0", lambda: agt[:, :], lambda: agt_d)
        DMA("gpsimd", "s0", lambda: aut[:, :], lambda: aut_d)
        DMA("gpsimd", "s0", lambda: adt[:, :], lambda: adt_d)
        OP("vector", lambda e: e.memset(ones_bf[:, :], 1.0), "dve")
        OP("vector", lambda e: e.memset(ones_f[:, :], 1.0), "dve")
        if E == "gpsimd":
            e.wait_ge(sems["s0"], cnt["s0"])
        BAR()

        for (a, b) in ((0, 8), (8, 16), (16, 24), (24, 32)):
            DMA("gpsimd", "sx", lambda a=a, b=b: LN1F[:, a:b, :],
                lambda a=a, b=b: xbt[128 * a:128 * b, :].rearrange(
                    "(c p) n -> p c n", p=128))
        xw = [("sx", cnt["sx"])]
        act_sq = []
        pe_chunk = []
        for c in range(32):
            s_ = sq[c % 2]
            wl = list(xw) if c == 0 else []
            if c >= 2:
                wl.append(("pe", pe_chunk[c - 2]))
            act_sq.append(OP("scalar", lambda e, c=c, s_=s_: e.activation(
                s_[:, :], LN1F[:, c, :], AF.Square), "act", waits=wl))
            for sl in range(4):
                OP("tensor", lambda e, c=c, sl=sl: e.matmul(
                    bank(sl, 0, 1), ones_bf[:, :], LN1F[:, c, 512 * sl:512 * (sl + 1)],
                    start=(c == 0), stop=(c == 31), skip_group_check=True),
                    waits=xw if (sl == 0 and c == 0) else ())
            for sl in range(4):
                v = OP("tensor", lambda e, c=c, sl=sl, s_=s_: e.matmul(
                    bank(4 + sl, 0, 1), ones_bf[:, :], s_[:, 512 * sl:512 * (sl + 1)],
                    start=(c == 0), stop=(c == 31), skip_group_check=True),
                    sem="pe" if sl == 3 else None,
                    waits=[("act", act_sq[c])] if sl == 0 else ())
            pe_chunk.append(v)

        def ln_strips(T, cs, cq, pe_wait):
            OP("vector", lambda e: e.tensor_scalar(
                ps_h[0:1, cs:cs + T], ps_h[0:1, cs:cs + T], RN, None, ALU.mult),
                "dve", waits=[("pe", pe_wait)])
            OP("vector", lambda e: e.tensor_scalar(
                ps_h[0:1, cq:cq + T], ps_h[0:1, cq:cq + T], RN, None, ALU.mult), "dve")
            OP("vector", lambda e: e.tensor_copy(
                S2[0:1, 0:T], ps_h[0:1, cs:cs + T]), "dve")
            OP("vector", lambda e: e.tensor_tensor(
                S1[0:1, 0:T], S2[0:1, 0:T], S2[0:1, 0:T], ALU.mult), "dve")
            d4 = OP("vector", lambda e: e.tensor_tensor(
                ps_h[0:1, cq:cq + T], ps_h[0:1, cq:cq + T], S1[0:1, 0:T], ALU.subtract), "dve")
            OP("scalar", lambda e: e.activation(
                ps_h[0:1, cq:cq + T], ps_h[0:1, cq:cq + T], AF.Ln,
                bias=consts[0:1, 1:2]), "act", waits=[("dve", d4)])
            a2 = OP("scalar", lambda e: e.activation(
                S1[0:1, 0:T], ps_h[0:1, cq:cq + T], AF.Exp, scale=-0.5), "act")
            return OP("vector", lambda e: e.tensor_tensor(
                S2[0:1, 0:T], S2[0:1, 0:T], S1[0:1, 0:T], ALU.mult),
                "dve", waits=[("act", a2)])

        d_mr = ln_strips(2048, 0, 2048, pe_chunk[31])
        for sl in range(4):
            OP("tensor", lambda e, sl=sl: e.matmul(
                bank(sl), ones_f[:, :], S1[0:1, 512 * sl:512 * (sl + 1)],
                start=True, stop=True, skip_group_check=True),
                waits=[("dve", d_mr)] if sl == 0 else ())
        pe_bc = None
        for sl in range(4):
            pe_bc = OP("tensor", lambda e, sl=sl: e.matmul(
                bank(4 + sl), ones_f[:, :], S2[0:1, 512 * sl:512 * (sl + 1)],
                start=True, stop=True, skip_group_check=True),
                sem="pe" if sl == 3 else None)
        OP("scalar", lambda e: e.copy(r_sb[:, :], ps_h[:, 0:2048]),
           "act", waits=[("pe", pe_bc)])
        a_mr = OP("scalar", lambda e: e.copy(mr_sb[:, :], ps_h[:, 2048:4096]), "act")
        if dbg:
            DMA("sync", "dz", lambda: dbg_rsb[:, 0:2048], lambda: r_sb[:, :],
                waits=[("act", a_mr)])
            DMA("sync", "dz", lambda: dbg_rsb[:, 2048:4096], lambda: mr_sb[:, :])
        dz_rsb = cnt["dz"]

        d_mul, p_sub = {}, {}
        for c in range(32):
            tb = [t_dve, t_pool][c % 2]
            wl = [("act", a_mr)] if c == 0 else []
            if c >= 2:
                wl.append(("pool", p_sub[c - 2]))
            d_mul[c] = OP("vector", lambda e, c=c, tb=tb: e.tensor_tensor(
                tb[:, :], LN1F[:, c, :], r_sb[:, :], ALU.mult), "dve", waits=wl)
            p_sub[c] = OP("gpsimd", lambda e, c=c, tb=tb: e.tensor_tensor(
                LN1F[:, c, :], tb[:, :], mr_sb[:, :], ALU.subtract), "pool",
                waits=[("dve", d_mul[c])])
        if dbg:
            for c in range(32):
                DMA("sync", "dz", lambda c=c: dbg_ln1[c], lambda c=c: LN1F[:, c, :],
                    waits=[("dve", cnt["dve"]), ("pool", cnt["pool"])] if c == 0 else ())
        dz_ln1 = cnt["dz"]
        BAR()

        def proj(w_src, nslab, out_d, rope, carry=None, carry_ksb=None,
                 carry_ko=None, carry_w=None, first_w_extra=()):
            TT = nslab * 512
            lw_s, pe_acc, pe_swap, act_ev, dve_ro, lx_ko = {}, {}, {}, {}, {}, {}

            def emit_swap(m):
                pst = m % 2
                base = 4 * pst + (2 if nslab == 2 else 0)
                wl = [("act", act_ev[m])]
                if m >= 2:
                    wl.append(("dve", dve_ro[m - 2]))
                v = None
                for j in range(nslab):
                    v = OP("tensor", lambda e, pst=pst, j=j, base=base: e.matmul(
                        bank(base + j), pswap[:, :],
                        ksb[pst][:, 512 * j:512 * (j + 1)],
                        start=True, stop=True, skip_group_check=True),
                        sem="pe" if j == nslab - 1 else None,
                        waits=wl if j == 0 else ())
                pe_swap[m] = v

            def emit_rope(m):
                st = m % 2
                base = 4 * st + (2 if nslab == 2 else 0)
                wl = [("act", act_ev[m]), ("pe", pe_swap[m])]
                if m < 2 and carry_ko is not None:
                    wl.append(carry_ko)
                if m >= 2:
                    wl.append((SS[m % 2], lx_ko[m - 2]))
                OP("vector", lambda e, st=st: e.tensor_tensor(
                    t1b[:, 0:TT], ksb[st][:, 0:TT], ropeC[:, 0:TT], ALU.mult),
                    waits=wl)
                OP("vector", lambda e, base=base: e.tensor_tensor(
                    t2b[:, 0:TT], ps_h[:, 512 * base:512 * base + TT],
                    ropeS[:, 0:TT], ALU.mult))
                dve_ro[m] = OP("vector", lambda e, st=st: e.tensor_add(
                    ko[st][:, 0:TT], t1b[:, 0:TT], t2b[:, 0:TT]), "dve")
                lx_ko[m] = DMA("sync", SS[m % 2], lambda m=m: out_d[m],
                               lambda st=st: ko[st][:, 0:TT],
                               waits=[("dve", dve_ro[m])])

            for m in range(32):
                st = m % 2
                wl = list(first_w_extra) if m == 0 else []
                if m < 2 and carry_w is not None:
                    wl.append(carry_w)
                if m >= 2:
                    wl = [("pe", pe_acc[m - 2])]
                lw_s[m] = DMA(
                    "gpsimd", WS[m % 2],
                    lambda st=st: wsl[st][:, :, :],
                    lambda m=m: w_src.rearrange("(c p) n -> p c n", p=128)[:, :, 128 * m:128 * (m + 1)],
                    waits=wl)
                for k in range(32):
                    if rope and k == 8 and m >= 1:
                        emit_swap(m - 1)
                    wl = []
                    if k == 0:
                        wl = [(WS[m % 2], lw_s[m])]
                        if m < 2 and carry is not None:
                            wl.append(carry)
                        if m >= 2:
                            if rope:
                                wl.append(("dve", dve_ro[m - 2]))
                            else:
                                wl.append(("act", act_ev[m - 2]))
                    v = None
                    for j in range(nslab):
                        v = OP("tensor", lambda e, m=m, k=k, j=j: e.matmul(
                            bank(4 * (m % 2) + j), wsl[m % 2][:, k, :],
                            LN1F[:, k, 512 * j:512 * (j + 1)],
                            start=(k == 0), stop=(k == 31), skip_group_check=True),
                            sem="pe" if (k == 31 and j == nslab - 1) else None,
                            waits=wl if (j == 0 and k == 0) else ())
                pe_acc[m] = v
                wl = [("pe", pe_acc[m])]
                if m < 2 and carry_ksb is not None:
                    wl.append(carry_ksb)
                if m >= 2:
                    if rope:
                        wl.append(("pe", pe_swap[m - 2]))
                        wl.append(("dve", dve_ro[m - 2]))
                    else:
                        wl.append((SS[m % 2], lx_ko[m - 2]))
                act_ev[m] = OP("scalar", lambda e, m=m: e.copy(
                    ksb[m % 2][:, 0:TT], ps_h[:, 2048 * (m % 2):2048 * (m % 2) + TT]),
                    "act", waits=wl)
                if rope:
                    if m >= 1:
                        emit_rope(m - 1)
                else:
                    lx_ko[m] = DMA("sync", SS[m % 2], lambda m=m: out_d[m],
                                   lambda m=m: ksb[m % 2][:, 0:TT],
                                   waits=[("act", act_ev[m])])
            if rope:
                emit_swap(31)
                emit_rope(31)
            if rope:
                g_ps = ("dve", dve_ro[31])
                g_ksb = ("dve", dve_ro[31])
                g_ko = (SS[31 % 2], lx_ko[31])
            else:
                g_ps = ("act", act_ev[31])
                g_ksb = (SS[31 % 2], lx_ko[31])
                g_ko = None
            return lx_ko, (g_ps, g_ksb, g_ko, ("pe", pe_acc[31]))

        first_w = [("dz", dz_rsb)] if dbg else []
        lx_k, (k_ps, k_ksb, k_ko, k_w) = proj(wk, 4, k_d, True,
                                              first_w_extra=first_w)
        lx_v, (v_ps, v_ksb, v_ko, v_w) = proj(wv, 4, vT_d, False,
                                              carry=k_ps, carry_ksb=k_ksb,
                                              carry_w=k_w)
        lx_q, _ = proj(wq, 2, q_d, True,
                       carry=v_ps, carry_ksb=v_ksb, carry_ko=k_ko,
                       carry_w=v_w)
        BAR()

        pe_tr, act_v4, lx_v4 = {}, {}, {}
        for c in range(32):
            st = c % 2
            wl = SSFULL() if c == 0 else []
            if c >= 2:
                wl.append(("pe", pe_tr[c - 2]))
            lx_in = DMA("sync", LS[c % 2], lambda st=st: vt[st][:, :],
                        lambda c=c: vT_d[c], waits=wl)
            wl = [(LS[c % 2], lx_in)]
            if c >= 2:
                wl.append(("act", act_v4[c - 2]))
            v = None
            for kt in range(16):
                v = OP("tensor", lambda e, st=st, kt=kt: e.matmul(
                    ps_h[:, 2048 * st + 128 * kt:2048 * st + 128 * (kt + 1)],
                    vt[st][:, 128 * kt:128 * (kt + 1)], ident[:, :],
                    start=True, stop=True, skip_group_check=True),
                    sem="pe" if kt == 15 else None, waits=wl if kt == 0 else ())
            pe_tr[c] = v
            wl = [("pe", pe_tr[c])]
            if c >= 2:
                wl.append((SS[c % 2], lx_v4[c - 2]))
            act_v4[c] = OP("scalar", lambda e, st=st: e.copy(
                v4s[st][:, :], ps_h[:, 2048 * st:2048 * (st + 1)]), "act", waits=wl)
            lx_v4[c] = DMA("sync", SS[c % 2],
                           lambda c=c: v4_d[c].rearrange("p a b -> p (a b)"),
                           lambda st=st: v4s[st][:, :],
                           waits=[("act", act_v4[c])])
        BAR()

        s_last = {}
        pv_last = {}
        recip_t = {}
        oev_t = {}
        bcast_t = {}
        norm_t = {}
        mask_last_slot = [None, None]
        lpv_last_slot = [None, None]
        exp_last_bank = [None, None]
        rb_of = {}

        def finish_head(gh):
            rb = rb_of[gh]
            h = gh % 32
            wl = [("dve", recip_t[gh])]
            if gh >= 1:
                wl.append(("dve", norm_t.get(gh - 1)))
            if dbg and gh == 0:
                wl.append(("dz", dz_ln1))
            bcast_t[gh] = OP("tensor", lambda e, rb=rb: e.matmul(
                bank(rb), ones_f[:, :], att1[0:1, 0:512],
                start=True, stop=True, skip_group_check=True), "pe", waits=wl)
            norm_t[gh] = OP("vector", lambda e, rb=rb, h=h, gh=gh: e.tensor_tensor(
                OT[:, h, 512 * rb:512 * (rb + 1)], ob[gh % 2][:, :],
                bank(rb), ALU.mult), "dve",
                waits=[("act", oev_t[gh]), ("pe", bcast_t[gh])])

        lx_kb = [None, None]
        lx_vb = [None, None]
        for rb in range(2):
            kts = KTS[rb]
            n = len(kts)
            wl = SSFULL() if rb == 0 else []
            if rb == 1:
                wl.append(("pe", s_last[31]))
            lx_q4 = DMA("sync", "lq", lambda: qsb[:, :, :],
                        lambda rb=rb: q_d.rearrange("h p t -> p h t")[:, :, 512 * rb:512 * (rb + 1)],
                        waits=wl)
            i0 = 0 if rb == 0 else 12
            wl = SSFULL() if rb == 0 else []
            if rb == 1:
                wl.append(("dve", mask_last_slot[0]))
                wl.append(("dve", mask_last_slot[1]))
            lx_m = DMA("gpsimd", "lm", lambda n=n: mkb[:, 0:n, :],
                       lambda i0=i0, n=n: masks_d[i0:i0 + n].rearrange("n p t -> p n t"),
                       waits=wl)
            for h in range(32):
                gh = rb * 32 + h
                rb_of[gh] = rb
                wl = []
                if gh >= 2:
                    wl.append(("pe", s_last[gh - 2]))
                lx_kb[gh % 2] = DMA("sync", LS[gh % 2], lambda gh=gh: kb[gh % 2][:, :],
                                    lambda h=h: k_d[h], waits=wl)
                wl = []
                if gh >= 2:
                    wl.append(("pe", pv_last[gh - 2]))
                lx_vb[gh % 2] = DMA("sync", LS2[gh % 2],
                                    lambda gh=gh: vb[gh % 2][:, :, :],
                                    lambda h=h: v4_d[h],
                                    waits=wl)
                s_t = {}

                def S_mm(i):
                    kt = kts[i]
                    wl = [(LS[gh % 2], lx_kb[gh % 2]), ("lq", lx_q4)]
                    if exp_last_bank[i % 2] is not None:
                        wl.append(("act", exp_last_bank[i % 2]))
                    s_t[i] = OP("tensor", lambda e, kt=kt, i=i, h=h, gh=gh: e.matmul(
                        bank(2 + (i % 2)), kb[gh % 2][:, 128 * kt:128 * (kt + 1)],
                        qsb[:, h, :], start=True, stop=True, skip_group_check=True),
                        "pe", waits=wl)

                S_mm(0)
                if n > 1:
                    S_mm(1)
                if gh >= 1:
                    finish_head(gh - 1)
                for i in range(n):
                    wl = [("pe", s_t[i])]
                    if mask_last_slot[i % 2] is not None:
                        wl.append(("dve", mask_last_slot[i % 2]))
                    a_e = OP("scalar", lambda e, i=i: e.activation(
                        pt[i % 2][:, :], bank(2 + (i % 2)), AF.Exp,
                        bias=consts[:, 0:1], scale=SM_SCALE), "act", waits=wl)
                    exp_last_bank[i % 2] = a_e
                    wl = [("act", a_e), ("lm", lx_m)]
                    if lpv_last_slot[i % 2] is not None:
                        wl.append(("pe", lpv_last_slot[i % 2]))
                    d_m = OP("vector", lambda e, i=i: e.tensor_tensor(
                        pm[i % 2][:, :], pt[i % 2][:, :],
                        mkb[:, i, :], ALU.mult), "dve", waits=wl)
                    mask_last_slot[i % 2] = d_m
                    if i + 2 < n:
                        S_mm(i + 2)
                    wl = [("dve", d_m)]
                    if i == 0:
                        wl.append((LS2[gh % 2], lx_vb[gh % 2]))
                        if gh >= 1:
                            wl.append(("dve", recip_t[gh - 1]))
                            wl.append(("act", oev_t[gh - 1]))
                    OP("tensor", lambda e, rb=rb, i=i, n=n: e.matmul(
                        bank(6 + rb, 0, 1), ones_bf[:, :], pm[i % 2][:, :],
                        start=(i == 0), stop=(i == n - 1), skip_group_check=True),
                        waits=wl)
                    kt = kts[i]
                    v = OP("tensor", lambda e, rb=rb, i=i, n=n, kt=kt, gh=gh: e.matmul(
                        bank(4 + rb), vb[gh % 2][:, kt, :], pm[i % 2][:, :],
                        start=(i == 0), stop=(i == n - 1), skip_group_check=True), "pe")
                    lpv_last_slot[i % 2] = v
                pv_last[gh] = v
                s_last[gh] = s_t[n - 1]
                wl = [("pe", pv_last[gh])]
                if gh >= 1:
                    wl.append(("pe", bcast_t.get(gh - 1)))
                recip_t[gh] = OP("vector", lambda e, rb=rb: e.reciprocal(
                    att1[0:1, 0:512], bank(6 + rb, 0, 1)), "dve", waits=wl)
                wl = [("pe", pv_last[gh])]
                if gh >= 2:
                    wl.append(("dve", norm_t.get(gh - 2)))
                oev_t[gh] = OP("scalar", lambda e, rb=rb, gh=gh: e.copy(
                    ob[gh % 2][:, :], bank(4 + rb)), "act", waits=wl)
        finish_head(63)
        if dbg:
            DMA("sync", "dz", lambda: dbg_ot.rearrange("h p t -> p h t"), lambda: OT[:, :, :],
                waits=[("dve", norm_t[63])])
        dz_ot = cnt["dz"]
        BAR()

        lw_s, pe_acc, act_ev, dve_cs, act_sq5, pe_st, lx_ar = {}, {}, {}, {}, {}, {}, {}
        for m in range(32):
            st = m % 2
            wl = []
            if m >= 2:
                wl.append(("pe", pe_acc[m - 2]))
            lw_s[m] = DMA(
                "gpsimd", WS[m % 2], lambda st=st: wsl[st][:, :, :],
                lambda m=m: wo.rearrange("(c p) n -> p c n", p=128)[:, :, 128 * m:128 * (m + 1)],
                waits=wl)
            v = None
            for k in range(32):
                wl = []
                if k == 0:
                    wl = [(WS[m % 2], lw_s[m])]
                    if m >= 2:
                        wl.append(("act", act_ev[m - 2]))
                for j in range(2):
                    v = OP("tensor", lambda e, m=m, k=k, j=j: e.matmul(
                        bank(2 * (m % 2) + j), wsl[m % 2][:, k, :],
                        OT[:, k, 512 * j:512 * (j + 1)],
                        start=(k == 0), stop=(k == 31), skip_group_check=True),
                        sem="pe" if (k == 31 and j == 1) else None,
                        waits=wl if (j == 0 and k == 0) else ())
            pe_acc[m] = v
            wl = [("pe", pe_acc[m])]
            if m >= 2:
                wl.append((SS[m % 2], lx_ar[m - 2]))
                wl.append(("dve", dve_cs[m - 2]))
            act_ev[m] = OP("scalar", lambda e, m=m: e.copy(
                ars[m % 2][:, :], ps_h[:, 1024 * (m % 2):1024 * (m % 2) + 1024]),
                "act", waits=wl)
            lx_ar[m] = DMA("sync", SS[m % 2], lambda m=m: ar_d[m],
                           lambda m=m: ars[m % 2][:, :],
                           waits=[("act", act_ev[m])])
            wl = [("act", act_ev[m])]
            if m >= 2:
                wl.append(("pe", pe_st[m - 2]))
            dve_cs[m] = OP("vector", lambda e, m=m: e.tensor_copy(
                arb[m % 2][:, :], ars[m % 2][:, :]), "dve", waits=wl)
            act_sq5[m] = OP("scalar", lambda e, m=m: e.activation(
                sqa[m % 2][:, :], arb[m % 2][:, :], AF.Square), "act",
                waits=[("dve", dve_cs[m])])
            for sl2 in range(2):
                OP("tensor", lambda e, m=m, sl2=sl2: e.matmul(
                    bank(4 + sl2, 0, 1), ones_bf[:, :],
                    arb[m % 2][:, 512 * sl2:512 * (sl2 + 1)],
                    start=(m == 0), stop=(m == 31), skip_group_check=True),
                    waits=[("dve", dve_cs[m])] if sl2 == 0 else ())
            for sl2 in range(2):
                v = OP("tensor", lambda e, m=m, sl2=sl2: e.matmul(
                    bank(6 + sl2, 0, 1), ones_bf[:, :],
                    sqa[m % 2][:, 512 * sl2:512 * (sl2 + 1)],
                    start=(m == 0), stop=(m == 31), skip_group_check=True),
                    sem="pe" if sl2 == 1 else None,
                    waits=[("act", act_sq5[m])] if sl2 == 0 else ())
            pe_st[m] = v
        d_mr = ln_strips(1024, 2048, 3072, pe_st[31])
        for sl2 in range(2):
            OP("tensor", lambda e, sl2=sl2: e.matmul(
                bank(sl2), ones_f[:, :], S1[0:1, 512 * sl2:512 * (sl2 + 1)],
                start=True, stop=True, skip_group_check=True),
                waits=[("dve", d_mr), ("act", act_ev[31])] if sl2 == 0 else ())
        pe_bc1 = None
        for sl2 in range(2):
            pe_bc1 = OP("tensor", lambda e, sl2=sl2: e.matmul(
                bank(2 + sl2), ones_f[:, :], S2[0:1, 512 * sl2:512 * (sl2 + 1)],
                start=True, stop=True, skip_group_check=True),
                sem="pe" if sl2 == 1 else None)
        BAR()

        lx_x, lx_a, dve_t2, pool_x1, act_c, act_s6, pe_st6, lx_x1 = \
            {}, {}, {}, {}, {}, {}, {}, {}
        ss_ar = SSFULL()
        for c in range(32):
            st = c % 2
            wl = []
            if c >= 2:
                wl.append(("pool", pool_x1[c - 2]))
            lx_x[c] = DMA("sync", LS[c % 2], lambda st=st: xs[st][:, :],
                          lambda c=c: xbt[128 * c:128 * (c + 1), 0:1024], waits=wl)
            wl = list(ss_ar) if c < 2 else []
            if c >= 2:
                wl.append(("dve", dve_t2[c - 2]))
            lx_a[c] = DMA("sync", LS2[c % 2], lambda st=st: as_[st][:, :],
                          lambda c=c: ar_d[c], waits=wl)
            wl = [(LS2[c % 2], lx_a[c]), ("pe", pe_bc1)]
            if c == 0:
                wl += ss_ar
            if c >= 2:
                wl.append(("dve", dve_t2[c - 2]))
            OP("vector", lambda e, st=st: e.tensor_tensor(
                t6a[st][:, :], as_[st][:, :], ps_h[:, 0:1024], ALU.mult), waits=wl)
            wl = []
            if c >= 2:
                wl.append(("pool", pool_x1[c - 2]))
            dve_t2[c] = OP("vector", lambda e, st=st: e.tensor_tensor(
                t6b[st][:, :], t6a[st][:, :], ps_h[:, 1024:2048], ALU.subtract),
                "dve", waits=wl)
            wl = [("dve", dve_t2[c]), (LS[c % 2], lx_x[c])]
            if c >= 2:
                wl.append((SS[c % 2], lx_x1[c - 2]))
                wl.append(("act", act_c[c - 2]))
            pool_x1[c] = OP("gpsimd", lambda e, st=st: e.tensor_add(
                x1s[st][:, :], t6b[st][:, :], xs[st][:, :]), "pool", waits=wl)
            lx_x1[c] = DMA("sync", SS[c % 2], lambda c=c: x1_d[c],
                           lambda st=st: x1s[st][:, :],
                           waits=[("pool", pool_x1[c])])
            wl = [("pool", pool_x1[c])]
            if dbg and c == 0:
                wl.append(("dz", dz_ot))
            act_c[c] = OP("scalar", lambda e, c=c, st=st: e.copy(
                XH1[:, c, :], x1s[st][:, :]), "act", waits=wl)
            wl = [("act", act_c[c])]
            if c >= 2:
                wl.append(("pe", pe_st6[c - 2]))
            act_s6[c] = OP("scalar", lambda e, c=c, st=st: e.activation(
                sq6[st][:, :], XH1[:, c, :], AF.Square), "act", waits=wl)
            for sl2 in range(2):
                OP("tensor", lambda e, c=c, sl2=sl2: e.matmul(
                    bank(4 + sl2, 0, 1), ones_bf[:, :],
                    XH1[:, c, 512 * sl2:512 * (sl2 + 1)],
                    start=(c == 0), stop=(c == 31), skip_group_check=True),
                    waits=[("act", act_c[c])] if sl2 == 0 else ())
            v = None
            for sl2 in range(2):
                v = OP("tensor", lambda e, c=c, st=st, sl2=sl2: e.matmul(
                    bank(6 + sl2, 0, 1), ones_bf[:, :],
                    sq6[st][:, 512 * sl2:512 * (sl2 + 1)],
                    start=(c == 0), stop=(c == 31), skip_group_check=True),
                    sem="pe" if sl2 == 1 else None,
                    waits=[("act", act_s6[c])] if sl2 == 0 else ())
            pe_st6[c] = v
        d_mr = ln_strips(1024, 2048, 3072, pe_st6[31])
        for sl2 in range(2):
            OP("tensor", lambda e, sl2=sl2: e.matmul(
                bank(4 + sl2), ones_f[:, :], S1[0:1, 512 * sl2:512 * (sl2 + 1)],
                start=True, stop=True, skip_group_check=True),
                waits=[("dve", d_mr)] if sl2 == 0 else ())
        pe_bc2 = None
        for sl2 in range(2):
            pe_bc2 = OP("tensor", lambda e, sl2=sl2: e.matmul(
                bank(6 + sl2), ones_f[:, :], S2[0:1, 512 * sl2:512 * (sl2 + 1)],
                start=True, stop=True, skip_group_check=True),
                sem="pe" if sl2 == 1 else None)
        OP("scalar", lambda e: e.copy(r2sb[:, :], ps_h[:, 2048:3072]),
           "act", waits=[("pe", pe_bc2)])
        a_mr2 = OP("scalar", lambda e: e.copy(mr2sb[:, :], ps_h[:, 3072:4096]), "act")
        BAR()

        d_mul, p_sub = {}, {}
        for c in range(32):
            tb = [t7d, t7p][c % 2]
            wl = [("act", a_mr2)] if c == 0 else []
            if c >= 2:
                wl.append(("pool", p_sub[c - 2]))
            d_mul[c] = OP("vector", lambda e, c=c, tb=tb: e.tensor_tensor(
                tb[:, :], XH1[:, c, :], r2sb[:, :], ALU.mult), "dve", waits=wl)
            p_sub[c] = OP("gpsimd", lambda e, c=c, tb=tb: e.tensor_tensor(
                LN2[:, c, :], tb[:, :], mr2sb[:, :], ALU.subtract), "pool",
                waits=[("dve", d_mul[c])])
        BAR()

        if dbg:
            for c in range(32):
                DMA("sync", "dz", lambda c=c: dbg_ln2[c], lambda c=c: LN2[:, c, :])
        dz_ln2 = cnt["dz"]
        lw_g, lw_u, pe_g, pe_u, act_si, dve_up, dve_h, lx_h = \
            {}, {}, {}, {}, {}, {}, {}, {}
        ss_x1 = SSFULL()
        for mt in range(IC):
            st = mt % 2
            wl = []
            if mt >= 2:
                wl.append(("pe", pe_g[mt - 2]))
            lw_g[mt] = DMA(
                "gpsimd", WS[mt % 2], lambda st=st: wsl[st][:, :, :],
                lambda mt=mt: wgT.rearrange("(c p) n -> p c n", p=128)[:, :, 128 * mt:128 * (mt + 1)],
                waits=wl)
            wl = []
            if mt >= 2:
                wl.append(("pe", pe_u[mt - 2]))
            lw_u[mt] = DMA(
                "gpsimd", US[mt % 2], lambda st=st: wu[st][:, :, :],
                lambda mt=mt: wuT.rearrange("(c p) n -> p c n", p=128)[:, :, 128 * mt:128 * (mt + 1)],
                waits=wl)
            v = None
            for k in range(32):
                wl = []
                if k == 0:
                    wl = [(WS[mt % 2], lw_g[mt])]
                    if mt >= 2:
                        wl.append(("act", act_si[mt - 2]))
                for j in range(2):
                    v = OP("tensor", lambda e, mt=mt, k=k, j=j: e.matmul(
                        bank(4 * (mt % 2) + j), wsl[mt % 2][:, k, :],
                        LN2[:, k, 512 * j:512 * (j + 1)],
                        start=(k == 0), stop=(k == 31), skip_group_check=True),
                        sem="pe" if (k == 31 and j == 1) else None,
                        waits=wl if (j == 0 and k == 0) else ())
            pe_g[mt] = v
            for k in range(32):
                wl = []
                if k == 0:
                    wl = [(US[mt % 2], lw_u[mt])]
                    if mt >= 2:
                        wl.append(("dve", dve_up[mt - 2]))
                for j in range(2):
                    v = OP("tensor", lambda e, mt=mt, k=k, j=j: e.matmul(
                        bank(4 * (mt % 2) + 2 + j), wu[mt % 2][:, k, :],
                        LN2[:, k, 512 * j:512 * (j + 1)],
                        start=(k == 0), stop=(k == 31), skip_group_check=True),
                        sem="pe" if (k == 31 and j == 1) else None,
                        waits=wl if (j == 0 and k == 0) else ())
            pe_u[mt] = v
            wl = [("pe", pe_g[mt])]
            if mt == 0:
                wl += ss_x1
            if mt >= 2:
                wl.append(("dve", dve_h[mt - 2]))
            act_si[mt] = OP("scalar", lambda e, mt=mt: e.activation(
                silu[mt % 2][:, :], ps_h[:, 2048 * (mt % 2):2048 * (mt % 2) + 1024],
                AF.Silu, scale=agt[:, mt:mt + 1]), "act", waits=wl)
            wl = [("pe", pe_u[mt])]
            if mt == 0:
                wl += ss_x1
            if mt >= 2:
                wl.append(("dve", dve_h[mt - 2]))
            dve_up[mt] = OP("vector", lambda e, mt=mt: e.tensor_scalar(
                up[mt % 2][:, :], ps_h[:, 2048 * (mt % 2) + 1024:2048 * (mt % 2) + 2048],
                aut[:, mt:mt + 1], None, ALU.mult), "dve", waits=wl)
            wl = [("act", act_si[mt]), ("dve", dve_up[mt])]
            if mt >= 2:
                wl.append((SS[mt % 2], lx_h[mt - 2]))
            dve_h[mt] = OP("vector", lambda e, mt=mt: e.tensor_tensor(
                h8[mt % 2][:, :], silu[mt % 2][:, :], up[mt % 2][:, :], ALU.mult),
                "dve", waits=wl)
            DMA("sync", SS[mt % 2], lambda mt=mt: h_d[0, mt],
                lambda mt=mt: h8[mt % 2][:, 0:512],
                waits=[("dve", dve_h[mt])])
            lx_h[mt] = DMA("sync", SS[mt % 2], lambda mt=mt: h_d[1, mt],
                           lambda mt=mt: h8[mt % 2][:, 512:1024])
        BAR()

        dve_fr_last = None
        lx_fr = {}
        ss_h = SSFULL()
        first_hq = True
        for g in range(8):
            base = cnt["pe"]
            lx_hq = {}
            lw_wd = {}
            for qi, (q0, nk) in enumerate(QUARTS):
                wl = []
                if first_hq:
                    wl += ss_h
                    if dbg:
                        wl.append(("dz", dz_ln2))
                if qi >= 2:
                    p0, pn = QUARTS[qi - 2]
                    wl.append(("pe", base + p0 + pn))
                elif g >= 1:
                    wl.append(("pe", base))
                first_hq = False
                v = None
                for rb in range(2):
                    v = DMA("sync", LS[qi % 2],
                            lambda qi=qi, nk=nk, rb=rb: hq[qi % 2][:, 0:nk, 512 * rb:512 * (rb + 1)],
                            lambda q0=q0, nk=nk, rb=rb: h_d[rb, q0:q0 + nk].rearrange("k p t -> p k t"),
                            waits=wl if rb == 0 else ())
                lx_hq[qi] = v
            for oi, (o0, no) in enumerate(OCTS):
                wl = []
                if oi >= 2:
                    p0, pn = OCTS[oi - 2]
                    wl.append(("pe", base + p0 + pn))
                elif g >= 1:
                    wl.append(("pe", base))
                lw_wd[oi] = DMA(
                    "gpsimd", WS[oi % 2], lambda oi=oi, no=no: wd[oi % 2][:, 0:no, :],
                    lambda o0=o0, no=no, g=g: wdT.rearrange(
                        "(c p) n -> p c n", p=128)[:, o0:o0 + no, 512 * g:512 * (g + 1)],
                    waits=wl)
            qi = 0
            v = None
            for k in range(IC):
                while qi + 1 < 4 and k >= QUARTS[qi][0] + QUARTS[qi][1]:
                    qi += 1
                oi = k // 8
                q0 = QUARTS[qi][0]
                wl = []
                if k == q0:
                    wl.append((LS[qi % 2], lx_hq[qi]))
                if k == OCTS[oi][0]:
                    wl.append((WS[oi % 2], lw_wd[oi]))
                if k == 0 and g >= 1:
                    wl.append(("dve", dve_fr_last))
                first = True
                for m in range(4):
                    for rb in range(2):
                        last = (m == 3 and rb == 1)
                        v = OP("tensor", lambda e, k=k, m=m, rb=rb, oi=oi, q0=q0, qi=qi: e.matmul(
                            bank(2 * m + rb), wd[oi % 2][:, k - 8 * oi, 128 * m:128 * (m + 1)],
                            hq[qi % 2][:, k - q0, 512 * rb:512 * (rb + 1)],
                            start=(k == 0), stop=(k == IC - 1), skip_group_check=True),
                            sem="pe" if last else None,
                            waits=wl if first else ())
                        first = False
            pe_done = v
            assert cnt["pe"] == base + IC
            ei = 0
            for m in range(4):
                for rb in range(2):
                    mg = g * 4 + m
                    st = ei % 2
                    wl = [("pe", pe_done)]
                    prev = lx_fr.get((g, ei - 2))
                    if prev:
                        wl.append((SS[ei % 2], prev))
                    dve_fr_last = OP("vector", lambda e, m=m, rb=rb, mg=mg, st=st: e.tensor_scalar(
                        frs[st][:, :], bank(2 * m + rb), adt[:, mg:mg + 1], None, ALU.mult),
                        "dve", waits=wl)
                    lx_fr[(g, ei)] = DMA("sync", SS[ei % 2],
                                         lambda rb=rb, mg=mg: fr_d[rb, mg],
                                         lambda st=st: frs[st][:, :],
                                         waits=[("dve", dve_fr_last)])
                    ei += 1
        BAR()

        ss_fr = SSFULL()
        d_apply_prev = None
        d_mr_prev = None
        first_gp = True
        first_sy = True
        for rb in range(2):
            lw_f, act_sf, pe_sf = {}, {}, {}
            for c in range(32):
                st = c % 2
                wl = []
                if first_gp:
                    wl += ss_fr
                    first_gp = False
                if c >= 2:
                    wl.append(("pe", pe_sf[c - 2]))
                lw_f[c] = DMA("gpsimd", US[c % 2], lambda st=st: fb[st][:, :],
                              lambda rb=rb, c=c: fr_d[rb, c], waits=wl)
                wl = [(US[c % 2], lw_f[c])]
                if c >= 2:
                    wl.append(("pe", pe_sf[c - 2]))
                act_sf[c] = OP("scalar", lambda e, st=st: e.activation(
                    sf[st][:, :], fb[st][:, :], AF.Square), "act", waits=wl)
                wl = [(US[c % 2], lw_f[c])]
                if rb == 1 and c == 0:
                    wl.append(("dve", d_mr_prev))
                OP("tensor", lambda e, c=c, st=st: e.matmul(
                    bank(4, 0, 1), ones_bf[:, :], fb[st][:, :],
                    start=(c == 0), stop=(c == 31), skip_group_check=True),
                    waits=wl)
                pe_sf[c] = OP("tensor", lambda e, c=c, st=st: e.matmul(
                    bank(5, 0, 1), ones_bf[:, :], sf[st][:, :],
                    start=(c == 0), stop=(c == 31), skip_group_check=True),
                    sem="pe", waits=[("act", act_sf[c])])
            d_mr_prev = ln_strips(512, 2048, 2560, pe_sf[31])
            wl = [("dve", d_mr_prev)]
            if rb == 1:
                wl.append(("dve", d_apply_prev))
            OP("tensor", lambda e: e.matmul(
                bank(6), ones_f[:, :], S1[0:1, 0:512],
                start=True, stop=True, skip_group_check=True), waits=wl)
            pe_bc3 = OP("tensor", lambda e: e.matmul(
                bank(7), ones_f[:, :], S2[0:1, 0:512],
                start=True, stop=True, skip_group_check=True), "pe")
            lx_xc, lx_fc, dve_tb, pool_o, lx_o = {}, {}, {}, {}, {}
            for c in range(32):
                st = c % 2
                wl = []
                if first_sy:
                    wl += ss_fr
                    first_sy = False
                if c >= 2:
                    wl.append(("pool", pool_o[c - 2]))
                lx_xc[c] = DMA("sync", LS[c % 2], lambda st=st: x1c[st][:, :],
                               lambda c=c, rb=rb: x1_d[c][:, 512 * rb:512 * (rb + 1)],
                               waits=wl)
                wl = []
                if c >= 2:
                    wl.append(("dve", dve_tb[c - 2]))
                lx_fc[c] = DMA("sync", LS2[c % 2], lambda st=st: frc[st][:, :],
                               lambda c=c, rb=rb: fr_d[rb, c], waits=wl)
                wl = [(LS2[c % 2], lx_fc[c]), ("pe", pe_bc3)]
                if c >= 2:
                    wl.append(("dve", dve_tb[c - 2]))
                OP("vector", lambda e, st=st: e.tensor_tensor(
                    tA[st][:, :], frc[st][:, :], bank(6), ALU.mult), waits=wl)
                wl = []
                if c >= 2:
                    wl.append(("pool", pool_o[c - 2]))
                dve_tb[c] = OP("vector", lambda e, st=st: e.tensor_tensor(
                    tB[st][:, :], tA[st][:, :], bank(7), ALU.subtract), "dve", waits=wl)
                wl = [("dve", dve_tb[c]), (LS[c % 2], lx_xc[c])]
                if c >= 2:
                    wl.append(("sy", lx_o[c - 2]))
                pool_o[c] = OP("gpsimd", lambda e, st=st: e.tensor_add(
                    osb[st][:, :], tB[st][:, :], x1c[st][:, :]), "pool", waits=wl)
                lx_o[c] = DMA("sync", "sy",
                              lambda c=c, rb=rb: y_t[128 * c:128 * (c + 1), 512 * rb:512 * (rb + 1)],
                              lambda st=st: osb[st][:, :],
                              waits=[("pool", pool_o[c])])
            d_apply_prev = dve_tb[31]
        if E == "sync":
            e.wait_ge(sems["sy"], cnt["sy"])
            if dbg:
                e.wait_ge(sems["dz"], cnt["dz"])
        return cnt

    for s in SEMS:
        sems[s] = nc.semaphore(s).__enter__()
    with (
        nc.Block() as blk,
    ):
        @blk.sync
        def _(e):
            walk("sync", e)

        @blk.gpsimd
        def _(e):
            walk("gpsimd", e)

        @blk.tensor
        def _(e):
            walk("tensor", e)

        @blk.scalar
        def _(e):
            walk("scalar", e)

        @blk.vector
        def _(e):
            walk("vector", e)

    return nc


def stage_inputs(inputs):
    x = np.ascontiguousarray(np.asarray(inputs["L0_attn_x"], dtype=np.float32))
    ws = {n: np.ascontiguousarray(np.asarray(inputs[k], dtype=np.float32))
          for n, k in (("wq", "L0_attn_wq"), ("wk", "L0_attn_wk"),
                       ("wv", "L0_attn_wv"), ("wo", "L0_attn_wo"))}
    wgT = np.ascontiguousarray(np.asarray(inputs["L0_ffn_Wg"], dtype=np.float32).T)
    wuT = np.ascontiguousarray(np.asarray(inputs["L0_ffn_Wu"], dtype=np.float32).T)
    wdT = np.ascontiguousarray(np.asarray(inputs["L0_ffn_Wd"], dtype=np.float32).T)
    agt = np.ascontiguousarray(np.asarray(inputs["L0_ffn_ag"], dtype=np.float32).reshape(86, 128).T)
    aut = np.ascontiguousarray(np.asarray(inputs["L0_ffn_au"], dtype=np.float32).reshape(86, 128).T)
    adt = np.ascontiguousarray(np.asarray(inputs["L0_ffn_ad"], dtype=np.float32).reshape(32, 128).T)
    pswap = np.zeros((128, 128), np.float32)
    for i in range(64):
        pswap[2 * i, 2 * i + 1] = 1.0
        pswap[2 * i + 1, 2 * i] = 1.0
    ident = np.eye(128, dtype=np.float32)
    consts = np.zeros((128, 16), np.float32)
    consts[:, 0] = -C_EXP
    consts[:, 1] = EPS

    inv = 1.0 / 10000.0 ** (np.arange(0, 128, 2)[:64].astype(np.float32) / 128.0)
    cls = {}
    for h in (0, 1):
        pos = np.concatenate([np.arange(h * 1024, (h + 1) * 1024),
                              np.arange((1 - h) * 1024, (2 - h) * 1024)]).astype(np.float32)
        f = np.outer(inv, pos)
        ropec = np.repeat(np.cos(f), 2, axis=0).astype(np.float32)
        ropes = np.repeat(np.sin(f), 2, axis=0).astype(np.float32)
        ropes[0::2] *= -1.0
        gq = h * 1024 + np.arange(1024)
        masks = np.zeros((28, 128, 512), np.float32)
        idx = 0
        for rb in range(2):
            rows = gq[512 * rb:512 * (rb + 1)]
            for kt in KTS[rb]:
                keys = pos[128 * kt:128 * (kt + 1)]
                masks[idx] = (keys[:, None] <= rows[None, :]).astype(np.float32)
                idx += 1
        cls[h] = (ropec, ropes, masks)

    in_maps = []
    for c in range(8):
        b, h = c // 2, c % 2
        own = x[b, h * 1024:(h + 1) * 1024]
        par = x[b, (1 - h) * 1024:(2 - h) * 1024]
        xbt = np.ascontiguousarray(np.concatenate([own, par], 0).T)
        ropec, ropes, masks = cls[h]
        in_maps.append({
            "xbt": xbt, "wq": ws["wq"], "wk": ws["wk"], "wv": ws["wv"],
            "wo": ws["wo"], "wgT": wgT, "wuT": wuT, "wdT": wdT,
            "agt": agt, "aut": aut, "adt": adt,
            "ropec": ropec, "ropes": ropes, "masks": masks,
            "pswap": pswap, "ident": ident, "consts": consts,
        })
    return in_maps


def kernel(**inputs):
    nc = build(dbg=False)
    in_maps = stage_inputs(inputs)
    res = run_bass_kernel_spmd(nc, in_maps, core_ids=list(range(8)))
    out = np.empty((4, 2048, 4096), np.float32)
    for c in range(8):
        b, h = c // 2, c % 2
        out[b, h * 1024:(h + 1) * 1024, :] = res.results[c]["y_t"].T
    return out


# version 04
# speedup vs baseline: 1.0537x; 1.0537x over previous
import numpy as np
import ml_dtypes
import concourse.bass as bass
import concourse.mybir as mybir
from concourse.bass_utils import run_bass_kernel_spmd

F32 = mybir.dt.float32
BF16 = mybir.dt.bfloat16
U8 = mybir.dt.uint8
AF = mybir.ActivationFunctionType
ALU = mybir.AluOpType

DC = 32
IC = 86
C_EXP = 20.0
SM_SCALE = float(1.0 / np.sqrt(128.0))
EPS = 1e-5
RN = 1.0 / 4096.0
KTS = [[0, 1, 2, 3, 8, 9, 10, 11, 12, 13, 14, 15], list(range(16))]
QUARTS = [(0, 22), (22, 22), (44, 21), (65, 21)]
OCTS = [(8 * i, min(8, IC - 8 * i)) for i in range((IC + 7) // 8)]

SEMS = ("s0", "sx", "sx2", "w0", "w1", "u0", "u1", "la", "lb", "lc", "ld",
        "sa", "sb", "lq", "lm", "sy", "dz",
        "pe", "act", "dve", "pool", "ph")
WS = ("w0", "w1")
US = ("u0", "u1")
LS = ("la", "lb")
LS2 = ("lc", "ld")
SS = ("sa", "sb")


def _off():
    per = {}
    o = 0
    def put(name, size):
        nonlocal o
        per[name] = o
        o += (size + 31) // 32 * 32
    put("ropeC", 4096)
    put("ropeS", 4096)
    put("consts", 64)
    put("ones_bf", 32)
    put("ones_f", 512)
    put("pswap", 256)
    put("ident", 256)
    put("agt", 344)
    put("aut", 344)
    put("adt", 128)
    put("S1", 8192)
    put("S2", 8192)
    put("att1", 2048)
    ab = (o + 31) // 32 * 32
    ar = {
        "sq0": 0, "sq1": 4096,
        "r_sb": 172032, "mr_sb": 176128,
        "LN1F": 40960,
        "wsl0": 0, "wsl1": 8192,
        "ksb0": 16384, "ksb1": 20480,
        "t1b": 24576, "t2b": 28672,
        "ko0": 32768, "ko1": 36864,
        "vt0": 0, "vt1": 4096, "v40": 8192, "v41": 12288,
        "qsb": 0,
        "mkb": 32768,
        "kb0": 49152, "kb1": 53248,
        "vb0": 57344, "vb1": 61440,
        "pt0": 65536, "pt1": 66560,
        "pm0": 67584, "pm1": 68608,
        "ob0": 69632, "ob1": 71680,
        "OT": 106496,
        "ars0": 16384, "ars1": 20480,
        "arb0": 24576, "arb1": 26624,
        "sqa0": 28672, "sqa1": 30720,
        "xs0": 0, "xs1": 4096,
        "as0": 8192, "as1": 12288,
        "t6a0": 16384, "t6a1": 20480,
        "t6b0": 24576, "t6b1": 28672,
        "x1s0": 32768, "x1s1": 36864,
        "r2sb": 4096, "mr2sb": 6144,
        "sq60": 40960, "sq61": 43008,
        "XH1": 106496,
        "t7d": 0, "t7p": 2048,
        "LN2": 40960,
        "wu0": 16384, "wu1": 24576,
        "silu0": 32768, "silu1": 34816,
        "up0": 36864, "up1": 38912,
        "h80": 106496, "h81": 108544,
        "hq0": 0, "hq1": 45056,
        "wd0": 90112, "wd1": 98304,
        "frs0": 106496, "frs1": 108544,
        "fb0": 0, "fb1": 1024,
        "sf0": 2048, "sf1": 3072,
        "x1c0": 4096, "x1c1": 6144,
        "frc0": 8192, "frc1": 10240,
        "tA0": 12288, "tA1": 14336,
        "tB0": 16384, "tB1": 18432,
        "os0": 20480, "os1": 22528,
    }
    return per, ab, ar


def build(dbg=False):
    nc = bass.Bass("TRN2", target_bir_lowering=False)
    ik = "ExternalOutput" if dbg else "Internal"

    xbt = nc.dram_tensor("xbt", [4096, 2048], F32, kind="ExternalInput").ap()
    wq = nc.dram_tensor("wq", [4096, 4096], F32, kind="ExternalInput").ap()
    wk = nc.dram_tensor("wk", [4096, 4096], F32, kind="ExternalInput").ap()
    wv = nc.dram_tensor("wv", [4096, 4096], F32, kind="ExternalInput").ap()
    wo = nc.dram_tensor("wo", [4096, 4096], F32, kind="ExternalInput").ap()
    wgT = nc.dram_tensor("wgT", [4096, 11008], F32, kind="ExternalInput").ap()
    wuT = nc.dram_tensor("wuT", [4096, 11008], F32, kind="ExternalInput").ap()
    wdT = nc.dram_tensor("wdT", [11008, 4096], F32, kind="ExternalInput").ap()
    agt_d = nc.dram_tensor("agt", [128, 86], F32, kind="ExternalInput").ap()
    aut_d = nc.dram_tensor("aut", [128, 86], F32, kind="ExternalInput").ap()
    adt_d = nc.dram_tensor("adt", [128, 32], F32, kind="ExternalInput").ap()
    ropec_d = nc.dram_tensor("ropec", [128, 2048], F32, kind="ExternalInput").ap()
    ropes_d = nc.dram_tensor("ropes", [128, 2048], F32, kind="ExternalInput").ap()
    masks_d = nc.dram_tensor("masks", [28, 128, 512], F32, kind="ExternalInput").ap()
    pswap_d = nc.dram_tensor("pswap", [128, 128], F32, kind="ExternalInput").ap()
    ident_d = nc.dram_tensor("ident", [128, 128], F32, kind="ExternalInput").ap()
    consts_d = nc.dram_tensor("consts", [128, 16], F32, kind="ExternalInput").ap()
    y_t = nc.dram_tensor("y_t", [4096, 1024], F32, kind="ExternalOutput").ap()

    q_d = nc.dram_tensor("q_d", [32, 128, 1024], BF16, kind=ik).ap()
    k_d = nc.dram_tensor("k_d", [32, 128, 2048], BF16, kind=ik).ap()
    vT_d = nc.dram_tensor("vT_d", [32, 128, 2048], BF16, kind=ik).ap()
    v4_d = nc.dram_tensor("v4_d", [32, 128, 16, 128], BF16, kind=ik).ap()
    ar_d = nc.dram_tensor("ar_d", [32, 128, 1024], F32, kind=ik).ap()
    x1_d = nc.dram_tensor("x1_d", [32, 128, 1024], F32, kind=ik).ap()
    h_d = nc.dram_tensor("h_d", [2, 86, 128, 512], BF16, kind=ik).ap()
    fr_d = nc.dram_tensor("fr_d", [2, 32, 128, 512], F32, kind=ik).ap()
    if dbg:
        dbg_ln1 = nc.dram_tensor("dbg_ln1", [32, 128, 2048], BF16, kind="ExternalOutput").ap()
        dbg_ot = nc.dram_tensor("dbg_ot", [32, 128, 1024], BF16, kind="ExternalOutput").ap()
        dbg_ln2 = nc.dram_tensor("dbg_ln2", [32, 128, 1024], BF16, kind="ExternalOutput").ap()
        dbg_rsb = nc.dram_tensor("dbg_rsb", [128, 4096], BF16, kind="ExternalOutput").ap()

    per, ab, ar = _off()
    space = nc.alloc_sbuf_tensor("space", [128, nc.sbuf_bytes_remaining - 64], U8)
    A = nc.lookup_mloc(space).addr

    def at(name, shape, dtype, off):
        return nc.alloc_sbuf_tensor_at(name, shape, dtype, offset=A + off)

    ropeC = at("ropeC", [128, 2048], BF16, per["ropeC"])
    ropeS = at("ropeS", [128, 2048], BF16, per["ropeS"])
    consts = at("consts", [128, 16], F32, per["consts"])
    ones_bf = at("ones_bf", [128, 1], BF16, per["ones_bf"])
    ones_f = at("ones_f", [1, 128], F32, per["ones_f"])
    pswap = at("pswap", [128, 128], BF16, per["pswap"])
    ident = at("ident", [128, 128], BF16, per["ident"])
    agt = at("agt_s", [128, 86], F32, per["agt"])
    aut = at("aut_s", [128, 86], F32, per["aut"])
    adt = at("adt_s", [128, 32], F32, per["adt"])
    S1 = at("S1", [128, 2048], F32, per["S1"])
    S2 = at("S2", [128, 2048], F32, per["S2"])
    att1 = at("att1", [128, 512], F32, per["att1"])

    sq = [at("sq0", [128, 2048], BF16, ab + ar["sq0"]),
          at("sq1", [128, 2048], BF16, ab + ar["sq1"])]
    r_sb = at("r_sb", [128, 2048], BF16, ab + ar["r_sb"])
    mr_sb = at("mr_sb", [128, 2048], BF16, ab + ar["mr_sb"])
    LN1F = at("LN1F", [128, 32, 2048], BF16, ab + ar["LN1F"])
    wsl = [at("wsl0", [128, 32, 128], BF16, ab + ar["wsl0"]),
           at("wsl1", [128, 32, 128], BF16, ab + ar["wsl1"])]
    ksb = [at("ksb0", [128, 2048], BF16, ab + ar["ksb0"]),
           at("ksb1", [128, 2048], BF16, ab + ar["ksb1"])]
    t1b = at("t1b", [128, 2048], BF16, ab + ar["t1b"])
    t2b = at("t2b", [128, 2048], BF16, ab + ar["t2b"])
    ko = [at("ko0", [128, 2048], BF16, ab + ar["ko0"]),
          at("ko1", [128, 2048], BF16, ab + ar["ko1"])]
    vt = [at("vt0", [128, 2048], BF16, ab + ar["vt0"]),
          at("vt1", [128, 2048], BF16, ab + ar["vt1"])]
    v4s = [at("v40", [128, 2048], BF16, ab + ar["v40"]),
           at("v41", [128, 2048], BF16, ab + ar["v41"])]
    qsb = at("qsb", [128, 32, 512], BF16, ab + ar["qsb"])
    mkb = at("mkb", [128, 16, 512], BF16, ab + ar["mkb"])
    kb = [at("kb0", [128, 2048], BF16, ab + ar["kb0"]),
          at("kb1", [128, 2048], BF16, ab + ar["kb1"])]
    vb = [at("vb0", [128, 16, 128], BF16, ab + ar["vb0"]),
          at("vb1", [128, 16, 128], BF16, ab + ar["vb1"])]
    pt = [at("pt0", [128, 512], BF16, ab + ar["pt0"]),
          at("pt1", [128, 512], BF16, ab + ar["pt1"])]
    pm = [at("pm0", [128, 512], BF16, ab + ar["pm0"]),
          at("pm1", [128, 512], BF16, ab + ar["pm1"])]
    ob = [at("ob0", [128, 512], F32, ab + ar["ob0"]),
          at("ob1", [128, 512], F32, ab + ar["ob1"])]
    OT = at("OT", [128, 32, 1024], BF16, ab + ar["OT"])
    ars = [at("ars0", [128, 1024], F32, ab + ar["ars0"]),
           at("ars1", [128, 1024], F32, ab + ar["ars1"])]
    arb = [at("arb0", [128, 1024], BF16, ab + ar["arb0"]),
           at("arb1", [128, 1024], BF16, ab + ar["arb1"])]
    sqa = [at("sqa0", [128, 1024], BF16, ab + ar["sqa0"]),
           at("sqa1", [128, 1024], BF16, ab + ar["sqa1"])]
    xs = [at("xs0", [128, 1024], F32, ab + ar["xs0"]),
          at("xs1", [128, 1024], F32, ab + ar["xs1"])]
    as_ = [at("as0", [128, 1024], F32, ab + ar["as0"]),
           at("as1", [128, 1024], F32, ab + ar["as1"])]
    t6a = [at("t6a0", [128, 1024], F32, ab + ar["t6a0"]),
           at("t6a1", [128, 1024], F32, ab + ar["t6a1"])]
    t6b = [at("t6b0", [128, 1024], F32, ab + ar["t6b0"]),
           at("t6b1", [128, 1024], F32, ab + ar["t6b1"])]
    x1s = [at("x1s0", [128, 1024], F32, ab + ar["x1s0"]),
           at("x1s1", [128, 1024], F32, ab + ar["x1s1"])]
    r2sb = at("r2sb", [128, 1024], BF16, ab + ar["r2sb"])
    mr2sb = at("mr2sb", [128, 1024], BF16, ab + ar["mr2sb"])
    sq6 = [at("sq60", [128, 1024], BF16, ab + ar["sq60"]),
           at("sq61", [128, 1024], BF16, ab + ar["sq61"])]
    XH1 = at("XH1", [128, 32, 1024], BF16, ab + ar["XH1"])
    t7d = at("t7d", [128, 1024], BF16, ab + ar["t7d"])
    t7p = at("t7p", [128, 1024], BF16, ab + ar["t7p"])
    LN2 = at("LN2", [128, 32, 1024], BF16, ab + ar["LN2"])
    wu = [at("wu0", [128, 32, 128], BF16, ab + ar["wu0"]),
          at("wu1", [128, 32, 128], BF16, ab + ar["wu1"])]
    silu = [at("silu0", [128, 1024], BF16, ab + ar["silu0"]),
            at("silu1", [128, 1024], BF16, ab + ar["silu1"])]
    up = [at("up0", [128, 1024], BF16, ab + ar["up0"]),
          at("up1", [128, 1024], BF16, ab + ar["up1"])]
    h8 = [at("h80", [128, 1024], BF16, ab + ar["h80"]),
          at("h81", [128, 1024], BF16, ab + ar["h81"])]
    hq = [at("hq0", [128, 22, 1024], BF16, ab + ar["hq0"]),
          at("hq1", [128, 22, 1024], BF16, ab + ar["hq1"])]
    wd = [at("wd0", [128, 8, 512], BF16, ab + ar["wd0"]),
          at("wd1", [128, 8, 512], BF16, ab + ar["wd1"])]
    frs = [at("frs0", [128, 512], F32, ab + ar["frs0"]),
           at("frs1", [128, 512], F32, ab + ar["frs1"])]
    fb = [at("fb0", [128, 512], BF16, ab + ar["fb0"]),
          at("fb1", [128, 512], BF16, ab + ar["fb1"])]
    sf = [at("sf0", [128, 512], BF16, ab + ar["sf0"]),
          at("sf1", [128, 512], BF16, ab + ar["sf1"])]
    x1c = [at("x1c0", [128, 512], F32, ab + ar["x1c0"]),
           at("x1c1", [128, 512], F32, ab + ar["x1c1"])]
    frc = [at("frc0", [128, 512], F32, ab + ar["frc0"]),
           at("frc1", [128, 512], F32, ab + ar["frc1"])]
    tA = [at("tA0", [128, 512], F32, ab + ar["tA0"]),
          at("tA1", [128, 512], F32, ab + ar["tA1"])]
    tB = [at("tB0", [128, 512], F32, ab + ar["tB0"]),
          at("tB1", [128, 512], F32, ab + ar["tB1"])]
    osb = [at("os0", [128, 512], F32, ab + ar["os0"]),
           at("os1", [128, 512], F32, ab + ar["os1"])]

    ps_h = nc.alloc_psum_tensor("psall", [128, 4096], F32)

    def bank(b, p0=0, pn=128, n=512):
        return ps_h[p0:p0 + pn, 512 * b:512 * b + n]

    sems = {}

    def walk(E, e):
        cnt = {k: 0 for k in SEMS}

        def W(sem, val):
            if val and val > 0:
                e.wait_ge(sems[sem], val)

        def DMA(eng, sem, outf, inf, waits=()):
            cnt[sem] += 16
            if eng == E:
                for (s, v) in waits:
                    W(s, v)
                e.dma_start(out=outf(), in_=inf()).then_inc(sems[sem], 16)
            return cnt[sem]

        def OP(eng, fn, sem=None, waits=()):
            if sem:
                cnt[sem] += 1
            if eng == E:
                for (s, v) in waits:
                    W(s, v)
                ins = fn(e)
                if sem:
                    ins.then_inc(sems[sem], 1)
            return cnt[sem] if sem else None

        def BAR():
            cnt["ph"] += 5
            e.nop().then_inc(sems["ph"], 1)
            e.wait_ge(sems["ph"], cnt["ph"])

        def SSFULL():
            return [("sa", cnt["sa"]), ("sb", cnt["sb"])]

        DMA("gpsimd", "s0", lambda: ropeC[:, :], lambda: ropec_d)
        DMA("gpsimd", "s0", lambda: ropeS[:, :], lambda: ropes_d)
        DMA("gpsimd", "s0", lambda: consts[:, :], lambda: consts_d)
        DMA("gpsimd", "s0", lambda: pswap[:, :], lambda: pswap_d)
        DMA("gpsimd", "s0", lambda: ident[:, :], lambda: ident_d)
        DMA("gpsimd", "s0", lambda: agt[:, :], lambda: agt_d)
        DMA("gpsimd", "s0", lambda: aut[:, :], lambda: aut_d)
        DMA("gpsimd", "s0", lambda: adt[:, :], lambda: adt_d)
        OP("vector", lambda e: e.memset(ones_bf[:, :], 1.0), "dve")
        OP("vector", lambda e: e.memset(ones_f[:, :], 1.0), "dve")
        if E == "gpsimd":
            e.wait_ge(sems["s0"], cnt["s0"])
        BAR()

        XSEM = ("sx", "sx2", "lq", "lm")
        xg = {}
        act_sq = []
        pe_chunk = []
        pe0 = cnt["pe"]
        for g in range(8):
            a, b = 4 * g, 4 * g + 4
            wl = []
            if g >= 4:
                wl.append(("pe", pe0 + 4 * (g - 4) + 4))
            xg[g] = DMA("gpsimd", XSEM[g % 4],
                        lambda a=a, b=b: LN1F[:, a:b, :],
                        lambda a=a, b=b: xbt[128 * a:128 * b, :].rearrange(
                            "(c p) n -> p c n", p=128), waits=wl)
        for c in range(32):
            s_ = sq[c % 2]
            g = c // 4
            gw = (XSEM[g % 4], xg[g])
            wl = [gw, ("dve", 2)] if c % 4 == 0 else []
            if c >= 2:
                wl.append(("pe", pe_chunk[c - 2]))
            act_sq.append(OP("scalar", lambda e, c=c, s_=s_: e.activation(
                s_[:, :], LN1F[:, c, :], AF.Square), "act", waits=wl))
            for sl in range(4):
                OP("tensor", lambda e, c=c, sl=sl: e.matmul(
                    bank(sl, 0, 1), ones_bf[:, :], LN1F[:, c, 512 * sl:512 * (sl + 1)],
                    start=(c == 0), stop=(c == 31), skip_group_check=True),
                    waits=[gw, ("dve", 2)] if (sl == 0 and c % 4 == 0) else ())
            for sl in range(4):
                v = OP("tensor", lambda e, c=c, sl=sl, s_=s_: e.matmul(
                    bank(4 + sl, 0, 1), ones_bf[:, :], s_[:, 512 * sl:512 * (sl + 1)],
                    start=(c == 0), stop=(c == 31), skip_group_check=True),
                    sem="pe" if sl == 3 else None,
                    waits=[("act", act_sq[c])] if sl == 0 else ())
            pe_chunk.append(v)

        def ln_strips(T, cs, cq, pe_wait):
            OP("vector", lambda e: e.tensor_scalar(
                ps_h[0:1, cs:cs + T], ps_h[0:1, cs:cs + T], RN, None, ALU.mult),
                "dve", waits=[("pe", pe_wait)])
            OP("vector", lambda e: e.tensor_scalar(
                ps_h[0:1, cq:cq + T], ps_h[0:1, cq:cq + T], RN, None, ALU.mult), "dve")
            OP("vector", lambda e: e.tensor_copy(
                S2[0:1, 0:T], ps_h[0:1, cs:cs + T]), "dve")
            OP("vector", lambda e: e.tensor_tensor(
                S1[0:1, 0:T], S2[0:1, 0:T], S2[0:1, 0:T], ALU.mult), "dve")
            d4 = OP("vector", lambda e: e.tensor_tensor(
                ps_h[0:1, cq:cq + T], ps_h[0:1, cq:cq + T], S1[0:1, 0:T], ALU.subtract), "dve")
            OP("scalar", lambda e: e.activation(
                ps_h[0:1, cq:cq + T], ps_h[0:1, cq:cq + T], AF.Ln,
                bias=consts[0:1, 1:2]), "act", waits=[("dve", d4)])
            a2 = OP("scalar", lambda e: e.activation(
                S1[0:1, 0:T], ps_h[0:1, cq:cq + T], AF.Exp, scale=-0.5), "act")
            return OP("vector", lambda e: e.tensor_tensor(
                S2[0:1, 0:T], S2[0:1, 0:T], S1[0:1, 0:T], ALU.mult),
                "dve", waits=[("act", a2)])

        d_mr = ln_strips(2048, 0, 2048, pe_chunk[31])
        for sl in range(4):
            OP("tensor", lambda e, sl=sl: e.matmul(
                bank(sl), ones_f[:, :], S1[0:1, 512 * sl:512 * (sl + 1)],
                start=True, stop=True, skip_group_check=True),
                waits=[("dve", d_mr)] if sl == 0 else ())
        pe_bc = None
        for sl in range(4):
            pe_bc = OP("tensor", lambda e, sl=sl: e.matmul(
                bank(4 + sl), ones_f[:, :], S2[0:1, 512 * sl:512 * (sl + 1)],
                start=True, stop=True, skip_group_check=True),
                sem="pe" if sl == 3 else None)
        OP("scalar", lambda e: e.copy(r_sb[:, :], ps_h[:, 0:2048]),
           "act", waits=[("pe", pe_bc)])
        a_mr = OP("scalar", lambda e: e.copy(mr_sb[:, :], ps_h[:, 2048:4096]), "act")
        if dbg:
            DMA("sync", "dz", lambda: dbg_rsb[:, 0:2048], lambda: r_sb[:, :],
                waits=[("act", a_mr)])
            DMA("sync", "dz", lambda: dbg_rsb[:, 2048:4096], lambda: mr_sb[:, :])
        dz_rsb = cnt["dz"]

        d_sub = {}
        for c in range(32):
            tb = ksb[c % 2]
            wl = [("act", a_mr)] if c == 0 else []
            OP("vector", lambda e, c=c, tb=tb: e.tensor_tensor(
                tb[:, :], LN1F[:, c, :], r_sb[:, :], ALU.mult), waits=wl)
            d_sub[c] = OP("vector", lambda e, c=c, tb=tb: e.tensor_tensor(
                LN1F[:, c, :], tb[:, :], mr_sb[:, :], ALU.subtract), "dve")
        if dbg:
            for c in range(32):
                DMA("sync", "dz", lambda c=c: dbg_ln1[c], lambda c=c: LN1F[:, c, :],
                    waits=[("dve", cnt["dve"])] if c == 0 else ())
        dz_ln1 = cnt["dz"]

        def proj(w_src, nslab, out_d, rope, carry=None, carry_ksb=None,
                 carry_ko=None, carry_w=None, first_w_extra=(), mk_waits=None):
            TT = nslab * 512
            lw_s, pe_acc, pe_swap, act_ev, dve_ro, lx_ko = {}, {}, {}, {}, {}, {}

            def emit_swap(m):
                pst = m % 2
                base = 4 * pst + (2 if nslab == 2 else 0)
                wl = [("act", act_ev[m])]
                if m >= 2:
                    wl.append(("dve", dve_ro[m - 2]))
                v = None
                for j in range(nslab):
                    v = OP("tensor", lambda e, pst=pst, j=j, base=base: e.matmul(
                        bank(base + j), pswap[:, :],
                        ksb[pst][:, 512 * j:512 * (j + 1)],
                        start=True, stop=True, skip_group_check=True),
                        sem="pe" if j == nslab - 1 else None,
                        waits=wl if j == 0 else ())
                pe_swap[m] = v

            def emit_rope(m):
                st = m % 2
                base = 4 * st + (2 if nslab == 2 else 0)
                wl = [("act", act_ev[m]), ("pe", pe_swap[m])]
                if m < 2 and carry_ko is not None:
                    wl.append(carry_ko)
                if m >= 2:
                    wl.append((SS[m % 2], lx_ko[m - 2]))
                OP("vector", lambda e, st=st: e.tensor_tensor(
                    t1b[:, 0:TT], ksb[st][:, 0:TT], ropeC[:, 0:TT], ALU.mult),
                    waits=wl)
                OP("vector", lambda e, base=base: e.tensor_tensor(
                    t2b[:, 0:TT], ps_h[:, 512 * base:512 * base + TT],
                    ropeS[:, 0:TT], ALU.mult))
                dve_ro[m] = OP("vector", lambda e, st=st: e.tensor_add(
                    ko[st][:, 0:TT], t1b[:, 0:TT], t2b[:, 0:TT]), "dve")
                lx_ko[m] = DMA("sync", SS[m % 2], lambda m=m: out_d[m],
                               lambda st=st: ko[st][:, 0:TT],
                               waits=[("dve", dve_ro[m])])

            for m in range(32):
                st = m % 2
                wl = list(first_w_extra) if m == 0 else []
                if m < 2 and carry_w is not None:
                    wl.append(carry_w)
                if m >= 2:
                    wl = [("pe", pe_acc[m - 2])]
                lw_s[m] = DMA(
                    "gpsimd", WS[m % 2],
                    lambda st=st: wsl[st][:, :, :],
                    lambda m=m: w_src.rearrange("(c p) n -> p c n", p=128)[:, :, 128 * m:128 * (m + 1)],
                    waits=wl)
                for k in range(32):
                    if rope and k == 8 and m >= 1:
                        emit_swap(m - 1)
                    wl = []
                    if k == 0:
                        wl = [(WS[m % 2], lw_s[m])]
                        if m < 2 and carry is not None:
                            wl.append(carry)
                        if m >= 2:
                            if rope:
                                wl.append(("dve", dve_ro[m - 2]))
                            else:
                                wl.append(("act", act_ev[m - 2]))
                    if m == 0 and mk_waits is not None:
                        wl = wl + mk_waits(k)
                    v = None
                    for j in range(nslab):
                        v = OP("tensor", lambda e, m=m, k=k, j=j: e.matmul(
                            bank(4 * (m % 2) + j), wsl[m % 2][:, k, :],
                            LN1F[:, k, 512 * j:512 * (j + 1)],
                            start=(k == 0), stop=(k == 31), skip_group_check=True),
                            sem="pe" if (k == 31 and j == nslab - 1) else None,
                            waits=wl if (j == 0 and wl) else ())
                pe_acc[m] = v
                wl = [("pe", pe_acc[m])]
                if m < 2 and carry_ksb is not None:
                    wl.append(carry_ksb)
                if m >= 2:
                    if rope:
                        wl.append(("pe", pe_swap[m - 2]))
                        wl.append(("dve", dve_ro[m - 2]))
                    else:
                        wl.append((SS[m % 2], lx_ko[m - 2]))
                act_ev[m] = OP("scalar", lambda e, m=m: e.copy(
                    ksb[m % 2][:, 0:TT], ps_h[:, 2048 * (m % 2):2048 * (m % 2) + TT]),
                    "act", waits=wl)
                if rope:
                    if m >= 1:
                        emit_rope(m - 1)
                else:
                    lx_ko[m] = DMA("sync", SS[m % 2], lambda m=m: out_d[m],
                                   lambda m=m: ksb[m % 2][:, 0:TT],
                                   waits=[("act", act_ev[m])])
            if rope:
                emit_swap(31)
                emit_rope(31)
            if rope:
                g_ps = ("dve", dve_ro[31])
                g_ksb = ("dve", dve_ro[31])
                g_ko = (SS[31 % 2], lx_ko[31])
            else:
                g_ps = ("act", act_ev[31])
                g_ksb = (SS[31 % 2], lx_ko[31])
                g_ko = None
            return lx_ko, (g_ps, g_ksb, g_ko, ("pe", pe_acc[31]))

        first_w = [("pe", pe_chunk[31])]
        if dbg:
            first_w.append(("dz", dz_rsb))
        lx_k, (k_ps, k_ksb, k_ko, k_w) = proj(wk, 4, k_d, True,
                                              first_w_extra=first_w,
                                              mk_waits=lambda k: [("dve", d_sub[k])])
        lx_v, (v_ps, v_ksb, v_ko, v_w) = proj(wv, 4, vT_d, False,
                                              carry=k_ps, carry_ksb=k_ksb,
                                              carry_w=k_w)
        lx_q, _ = proj(wq, 2, q_d, True,
                       carry=v_ps, carry_ksb=v_ksb, carry_ko=k_ko,
                       carry_w=v_w)
        BAR()

        pe_tr, act_v4, lx_v4 = {}, {}, {}
        for c in range(32):
            st = c % 2
            wl = SSFULL() if c == 0 else []
            if c >= 2:
                wl.append(("pe", pe_tr[c - 2]))
            lx_in = DMA("sync", LS[c % 2], lambda st=st: vt[st][:, :],
                        lambda c=c: vT_d[c], waits=wl)
            wl = [(LS[c % 2], lx_in)]
            if c >= 2:
                wl.append(("act", act_v4[c - 2]))
            v = None
            for kt in range(16):
                v = OP("tensor", lambda e, st=st, kt=kt: e.matmul(
                    ps_h[:, 2048 * st + 128 * kt:2048 * st + 128 * (kt + 1)],
                    vt[st][:, 128 * kt:128 * (kt + 1)], ident[:, :],
                    start=True, stop=True, skip_group_check=True),
                    sem="pe" if kt == 15 else None, waits=wl if kt == 0 else ())
            pe_tr[c] = v
            wl = [("pe", pe_tr[c])]
            if c >= 2:
                wl.append((SS[c % 2], lx_v4[c - 2]))
            act_v4[c] = OP("scalar", lambda e, st=st: e.copy(
                v4s[st][:, :], ps_h[:, 2048 * st:2048 * (st + 1)]), "act", waits=wl)
            lx_v4[c] = DMA("sync", SS[c % 2],
                           lambda c=c: v4_d[c].rearrange("p a b -> p (a b)"),
                           lambda st=st: v4s[st][:, :],
                           waits=[("act", act_v4[c])])
        BAR()

        s_last = {}
        pv_last = {}
        recip_t = {}
        oev_t = {}
        bcast_t = {}
        norm_t = {}
        mask_last_slot = [None, None]
        lpv_last_slot = [None, None]
        exp_last_bank = [None, None]
        rb_of = {}

        def finish_head(gh):
            rb = rb_of[gh]
            h = gh % 32
            wl = [("dve", recip_t[gh])]
            if gh >= 1:
                wl.append(("dve", norm_t.get(gh - 1)))
            if dbg and gh == 0:
                wl.append(("dz", dz_ln1))
            bcast_t[gh] = OP("tensor", lambda e, rb=rb: e.matmul(
                bank(rb), ones_f[:, :], att1[0:1, 0:512],
                start=True, stop=True, skip_group_check=True), "pe", waits=wl)
            norm_t[gh] = OP("vector", lambda e, rb=rb, h=h, gh=gh: e.tensor_tensor(
                OT[:, h, 512 * rb:512 * (rb + 1)], ob[gh % 2][:, :],
                bank(rb), ALU.mult), "dve",
                waits=[("act", oev_t[gh]), ("pe", bcast_t[gh])])

        lx_kb = [None, None]
        lx_vb = [None, None]
        for rb in range(2):
            kts = KTS[rb]
            n = len(kts)
            wl = SSFULL() if rb == 0 else []
            if rb == 1:
                wl.append(("pe", s_last[31]))
            lx_q4 = DMA("sync", "lq", lambda: qsb[:, :, :],
                        lambda rb=rb: q_d.rearrange("h p t -> p h t")[:, :, 512 * rb:512 * (rb + 1)],
                        waits=wl)
            i0 = 0 if rb == 0 else 12
            wl = SSFULL() if rb == 0 else []
            if rb == 1:
                wl.append(("dve", mask_last_slot[0]))
                wl.append(("dve", mask_last_slot[1]))
            lx_m = DMA("gpsimd", "lm", lambda n=n: mkb[:, 0:n, :],
                       lambda i0=i0, n=n: masks_d[i0:i0 + n].rearrange("n p t -> p n t"),
                       waits=wl)
            for h in range(32):
                gh = rb * 32 + h
                rb_of[gh] = rb
                wl = []
                if gh >= 2:
                    wl.append(("pe", s_last[gh - 2]))
                lx_kb[gh % 2] = DMA("sync", LS[gh % 2], lambda gh=gh: kb[gh % 2][:, :],
                                    lambda h=h: k_d[h], waits=wl)
                wl = []
                if gh >= 2:
                    wl.append(("pe", pv_last[gh - 2]))
                lx_vb[gh % 2] = DMA("sync", LS2[gh % 2],
                                    lambda gh=gh: vb[gh % 2][:, :, :],
                                    lambda h=h: v4_d[h],
                                    waits=wl)
                s_t = {}

                def S_mm(i):
                    kt = kts[i]
                    wl = [(LS[gh % 2], lx_kb[gh % 2]), ("lq", lx_q4)]
                    if exp_last_bank[i % 2] is not None:
                        wl.append(("act", exp_last_bank[i % 2]))
                    s_t[i] = OP("tensor", lambda e, kt=kt, i=i, h=h, gh=gh: e.matmul(
                        bank(2 + (i % 2)), kb[gh % 2][:, 128 * kt:128 * (kt + 1)],
                        qsb[:, h, :], start=True, stop=True, skip_group_check=True),
                        "pe", waits=wl)

                S_mm(0)
                if n > 1:
                    S_mm(1)
                if gh >= 1:
                    finish_head(gh - 1)
                for i in range(n):
                    wl = [("pe", s_t[i])]
                    if mask_last_slot[i % 2] is not None:
                        wl.append(("dve", mask_last_slot[i % 2]))
                    a_e = OP("scalar", lambda e, i=i: e.activation(
                        pt[i % 2][:, :], bank(2 + (i % 2)), AF.Exp,
                        bias=consts[:, 0:1], scale=SM_SCALE), "act", waits=wl)
                    exp_last_bank[i % 2] = a_e
                    wl = [("act", a_e), ("lm", lx_m)]
                    if lpv_last_slot[i % 2] is not None:
                        wl.append(("pe", lpv_last_slot[i % 2]))
                    d_m = OP("vector", lambda e, i=i: e.tensor_tensor(
                        pm[i % 2][:, :], pt[i % 2][:, :],
                        mkb[:, i, :], ALU.mult), "dve", waits=wl)
                    mask_last_slot[i % 2] = d_m
                    if i + 2 < n:
                        S_mm(i + 2)
                    wl = [("dve", d_m)]
                    if i == 0:
                        wl.append((LS2[gh % 2], lx_vb[gh % 2]))
                        if gh >= 1:
                            wl.append(("dve", recip_t[gh - 1]))
                            wl.append(("act", oev_t[gh - 1]))
                    OP("tensor", lambda e, rb=rb, i=i, n=n: e.matmul(
                        bank(6 + rb, 0, 1), ones_bf[:, :], pm[i % 2][:, :],
                        start=(i == 0), stop=(i == n - 1), skip_group_check=True),
                        waits=wl)
                    kt = kts[i]
                    v = OP("tensor", lambda e, rb=rb, i=i, n=n, kt=kt, gh=gh: e.matmul(
                        bank(4 + rb), vb[gh % 2][:, kt, :], pm[i % 2][:, :],
                        start=(i == 0), stop=(i == n - 1), skip_group_check=True), "pe")
                    lpv_last_slot[i % 2] = v
                pv_last[gh] = v
                s_last[gh] = s_t[n - 1]
                wl = [("pe", pv_last[gh])]
                if gh >= 1:
                    wl.append(("pe", bcast_t.get(gh - 1)))
                recip_t[gh] = OP("vector", lambda e, rb=rb: e.reciprocal(
                    att1[0:1, 0:512], bank(6 + rb, 0, 1)), "dve", waits=wl)
                wl = [("pe", pv_last[gh])]
                if gh >= 2:
                    wl.append(("dve", norm_t.get(gh - 2)))
                oev_t[gh] = OP("scalar", lambda e, rb=rb, gh=gh: e.copy(
                    ob[gh % 2][:, :], bank(4 + rb)), "act", waits=wl)
        finish_head(63)
        if dbg:
            DMA("sync", "dz", lambda: dbg_ot.rearrange("h p t -> p h t"), lambda: OT[:, :, :],
                waits=[("dve", norm_t[63])])
        dz_ot = cnt["dz"]
        BAR()

        lw_s, pe_acc, act_ev, dve_cs, act_sq5, pe_st, lx_ar = {}, {}, {}, {}, {}, {}, {}
        for m in range(32):
            st = m % 2
            wl = []
            if m >= 2:
                wl.append(("pe", pe_acc[m - 2]))
            lw_s[m] = DMA(
                "gpsimd", WS[m % 2], lambda st=st: wsl[st][:, :, :],
                lambda m=m: wo.rearrange("(c p) n -> p c n", p=128)[:, :, 128 * m:128 * (m + 1)],
                waits=wl)
            v = None
            for k in range(32):
                wl = []
                if k == 0:
                    wl = [(WS[m % 2], lw_s[m])]
                    if m >= 2:
                        wl.append(("act", act_ev[m - 2]))
                for j in range(2):
                    v = OP("tensor", lambda e, m=m, k=k, j=j: e.matmul(
                        bank(2 * (m % 2) + j), wsl[m % 2][:, k, :],
                        OT[:, k, 512 * j:512 * (j + 1)],
                        start=(k == 0), stop=(k == 31), skip_group_check=True),
                        sem="pe" if (k == 31 and j == 1) else None,
                        waits=wl if (j == 0 and k == 0) else ())
            pe_acc[m] = v
            wl = [("pe", pe_acc[m])]
            if m >= 2:
                wl.append((SS[m % 2], lx_ar[m - 2]))
                wl.append(("dve", dve_cs[m - 2]))
            act_ev[m] = OP("scalar", lambda e, m=m: e.copy(
                ars[m % 2][:, :], ps_h[:, 1024 * (m % 2):1024 * (m % 2) + 1024]),
                "act", waits=wl)
            lx_ar[m] = DMA("sync", SS[m % 2], lambda m=m: ar_d[m],
                           lambda m=m: ars[m % 2][:, :],
                           waits=[("act", act_ev[m])])
            wl = [("act", act_ev[m])]
            if m >= 2:
                wl.append(("pe", pe_st[m - 2]))
            dve_cs[m] = OP("vector", lambda e, m=m: e.tensor_copy(
                arb[m % 2][:, :], ars[m % 2][:, :]), "dve", waits=wl)
            act_sq5[m] = OP("scalar", lambda e, m=m: e.activation(
                sqa[m % 2][:, :], arb[m % 2][:, :], AF.Square), "act",
                waits=[("dve", dve_cs[m])])
            for sl2 in range(2):
                OP("tensor", lambda e, m=m, sl2=sl2: e.matmul(
                    bank(4 + sl2, 0, 1), ones_bf[:, :],
                    arb[m % 2][:, 512 * sl2:512 * (sl2 + 1)],
                    start=(m == 0), stop=(m == 31), skip_group_check=True),
                    waits=[("dve", dve_cs[m])] if sl2 == 0 else ())
            for sl2 in range(2):
                v = OP("tensor", lambda e, m=m, sl2=sl2: e.matmul(
                    bank(6 + sl2, 0, 1), ones_bf[:, :],
                    sqa[m % 2][:, 512 * sl2:512 * (sl2 + 1)],
                    start=(m == 0), stop=(m == 31), skip_group_check=True),
                    sem="pe" if sl2 == 1 else None,
                    waits=[("act", act_sq5[m])] if sl2 == 0 else ())
            pe_st[m] = v
        d_mr = ln_strips(1024, 2048, 3072, pe_st[31])
        for sl2 in range(2):
            OP("tensor", lambda e, sl2=sl2: e.matmul(
                bank(sl2), ones_f[:, :], S1[0:1, 512 * sl2:512 * (sl2 + 1)],
                start=True, stop=True, skip_group_check=True),
                waits=[("dve", d_mr), ("act", act_ev[31])] if sl2 == 0 else ())
        pe_bc1 = None
        for sl2 in range(2):
            pe_bc1 = OP("tensor", lambda e, sl2=sl2: e.matmul(
                bank(2 + sl2), ones_f[:, :], S2[0:1, 512 * sl2:512 * (sl2 + 1)],
                start=True, stop=True, skip_group_check=True),
                sem="pe" if sl2 == 1 else None)
        BAR()

        lx_x, lx_a, dve_t2, pool_x1, act_c, act_s6, pe_st6, lx_x1 = \
            {}, {}, {}, {}, {}, {}, {}, {}
        ss_ar = SSFULL()
        for c in range(32):
            st = c % 2
            wl = []
            if c >= 2:
                wl.append(("pool", pool_x1[c - 2]))
            lx_x[c] = DMA("sync", LS[c % 2], lambda st=st: xs[st][:, :],
                          lambda c=c: xbt[128 * c:128 * (c + 1), 0:1024], waits=wl)
            wl = list(ss_ar) if c < 2 else []
            if c >= 2:
                wl.append(("dve", dve_t2[c - 2]))
            lx_a[c] = DMA("sync", LS2[c % 2], lambda st=st: as_[st][:, :],
                          lambda c=c: ar_d[c], waits=wl)
            wl = [(LS2[c % 2], lx_a[c]), ("pe", pe_bc1)]
            if c == 0:
                wl += ss_ar
            if c >= 2:
                wl.append(("dve", dve_t2[c - 2]))
            OP("vector", lambda e, st=st: e.tensor_tensor(
                t6a[st][:, :], as_[st][:, :], ps_h[:, 0:1024], ALU.mult), waits=wl)
            wl = []
            if c >= 2:
                wl.append(("pool", pool_x1[c - 2]))
            dve_t2[c] = OP("vector", lambda e, st=st: e.tensor_tensor(
                t6b[st][:, :], t6a[st][:, :], ps_h[:, 1024:2048], ALU.subtract),
                "dve", waits=wl)
            wl = [("dve", dve_t2[c]), (LS[c % 2], lx_x[c])]
            if c >= 2:
                wl.append((SS[c % 2], lx_x1[c - 2]))
                wl.append(("act", act_c[c - 2]))
            pool_x1[c] = OP("gpsimd", lambda e, st=st: e.tensor_add(
                x1s[st][:, :], t6b[st][:, :], xs[st][:, :]), "pool", waits=wl)
            lx_x1[c] = DMA("sync", SS[c % 2], lambda c=c: x1_d[c],
                           lambda st=st: x1s[st][:, :],
                           waits=[("pool", pool_x1[c])])
            wl = [("pool", pool_x1[c])]
            if dbg and c == 0:
                wl.append(("dz", dz_ot))
            act_c[c] = OP("scalar", lambda e, c=c, st=st: e.copy(
                XH1[:, c, :], x1s[st][:, :]), "act", waits=wl)
            wl = [("act", act_c[c])]
            if c >= 2:
                wl.append(("pe", pe_st6[c - 2]))
            act_s6[c] = OP("scalar", lambda e, c=c, st=st: e.activation(
                sq6[st][:, :], XH1[:, c, :], AF.Square), "act", waits=wl)
            for sl2 in range(2):
                OP("tensor", lambda e, c=c, sl2=sl2: e.matmul(
                    bank(4 + sl2, 0, 1), ones_bf[:, :],
                    XH1[:, c, 512 * sl2:512 * (sl2 + 1)],
                    start=(c == 0), stop=(c == 31), skip_group_check=True),
                    waits=[("act", act_c[c])] if sl2 == 0 else ())
            v = None
            for sl2 in range(2):
                v = OP("tensor", lambda e, c=c, st=st, sl2=sl2: e.matmul(
                    bank(6 + sl2, 0, 1), ones_bf[:, :],
                    sq6[st][:, 512 * sl2:512 * (sl2 + 1)],
                    start=(c == 0), stop=(c == 31), skip_group_check=True),
                    sem="pe" if sl2 == 1 else None,
                    waits=[("act", act_s6[c])] if sl2 == 0 else ())
            pe_st6[c] = v
        d_mr = ln_strips(1024, 2048, 3072, pe_st6[31])
        for sl2 in range(2):
            OP("tensor", lambda e, sl2=sl2: e.matmul(
                bank(4 + sl2), ones_f[:, :], S1[0:1, 512 * sl2:512 * (sl2 + 1)],
                start=True, stop=True, skip_group_check=True),
                waits=[("dve", d_mr)] if sl2 == 0 else ())
        pe_bc2 = None
        for sl2 in range(2):
            pe_bc2 = OP("tensor", lambda e, sl2=sl2: e.matmul(
                bank(6 + sl2), ones_f[:, :], S2[0:1, 512 * sl2:512 * (sl2 + 1)],
                start=True, stop=True, skip_group_check=True),
                sem="pe" if sl2 == 1 else None)
        OP("scalar", lambda e: e.copy(r2sb[:, :], ps_h[:, 2048:3072]),
           "act", waits=[("pe", pe_bc2)])
        a_mr2 = OP("scalar", lambda e: e.copy(mr2sb[:, :], ps_h[:, 3072:4096]), "act")
        BAR()

        d_mul, p_sub = {}, {}
        for c in range(32):
            tb = [t7d, t7p][c % 2]
            wl = [("act", a_mr2)] if c == 0 else []
            if c >= 2:
                wl.append(("pool", p_sub[c - 2]))
            d_mul[c] = OP("vector", lambda e, c=c, tb=tb: e.tensor_tensor(
                tb[:, :], XH1[:, c, :], r2sb[:, :], ALU.mult), "dve", waits=wl)
            p_sub[c] = OP("gpsimd", lambda e, c=c, tb=tb: e.tensor_tensor(
                LN2[:, c, :], tb[:, :], mr2sb[:, :], ALU.subtract), "pool",
                waits=[("dve", d_mul[c])])
        BAR()

        if dbg:
            for c in range(32):
                DMA("sync", "dz", lambda c=c: dbg_ln2[c], lambda c=c: LN2[:, c, :])
        dz_ln2 = cnt["dz"]
        lw_g, lw_u, pe_g, pe_u, act_si, dve_up, dve_h, lx_h = \
            {}, {}, {}, {}, {}, {}, {}, {}
        ss_x1 = SSFULL()
        for mt in range(IC):
            st = mt % 2
            wl = []
            if mt >= 2:
                wl.append(("pe", pe_g[mt - 2]))
            lw_g[mt] = DMA(
                "gpsimd", WS[mt % 2], lambda st=st: wsl[st][:, :, :],
                lambda mt=mt: wgT.rearrange("(c p) n -> p c n", p=128)[:, :, 128 * mt:128 * (mt + 1)],
                waits=wl)
            wl = []
            if mt >= 2:
                wl.append(("pe", pe_u[mt - 2]))
            lw_u[mt] = DMA(
                "gpsimd", US[mt % 2], lambda st=st: wu[st][:, :, :],
                lambda mt=mt: wuT.rearrange("(c p) n -> p c n", p=128)[:, :, 128 * mt:128 * (mt + 1)],
                waits=wl)
            v = None
            for k in range(32):
                wl = []
                if k == 0:
                    wl = [(WS[mt % 2], lw_g[mt])]
                    if mt >= 2:
                        wl.append(("act", act_si[mt - 2]))
                for j in range(2):
                    v = OP("tensor", lambda e, mt=mt, k=k, j=j: e.matmul(
                        bank(4 * (mt % 2) + j), wsl[mt % 2][:, k, :],
                        LN2[:, k, 512 * j:512 * (j + 1)],
                        start=(k == 0), stop=(k == 31), skip_group_check=True),
                        sem="pe" if (k == 31 and j == 1) else None,
                        waits=wl if (j == 0 and k == 0) else ())
            pe_g[mt] = v
            for k in range(32):
                wl = []
                if k == 0:
                    wl = [(US[mt % 2], lw_u[mt])]
                    if mt >= 2:
                        wl.append(("dve", dve_up[mt - 2]))
                for j in range(2):
                    v = OP("tensor", lambda e, mt=mt, k=k, j=j: e.matmul(
                        bank(4 * (mt % 2) + 2 + j), wu[mt % 2][:, k, :],
                        LN2[:, k, 512 * j:512 * (j + 1)],
                        start=(k == 0), stop=(k == 31), skip_group_check=True),
                        sem="pe" if (k == 31 and j == 1) else None,
                        waits=wl if (j == 0 and k == 0) else ())
            pe_u[mt] = v
            wl = [("pe", pe_g[mt])]
            if mt == 0:
                wl += ss_x1
            if mt >= 2:
                wl.append(("dve", dve_h[mt - 2]))
            act_si[mt] = OP("scalar", lambda e, mt=mt: e.activation(
                silu[mt % 2][:, :], ps_h[:, 2048 * (mt % 2):2048 * (mt % 2) + 1024],
                AF.Silu, scale=agt[:, mt:mt + 1]), "act", waits=wl)
            wl = [("pe", pe_u[mt])]
            if mt == 0:
                wl += ss_x1
            if mt >= 2:
                wl.append(("dve", dve_h[mt - 2]))
            dve_up[mt] = OP("vector", lambda e, mt=mt: e.tensor_scalar(
                up[mt % 2][:, :], ps_h[:, 2048 * (mt % 2) + 1024:2048 * (mt % 2) + 2048],
                aut[:, mt:mt + 1], None, ALU.mult), "dve", waits=wl)
            wl = [("act", act_si[mt]), ("dve", dve_up[mt])]
            if mt >= 2:
                wl.append((SS[mt % 2], lx_h[mt - 2]))
            dve_h[mt] = OP("vector", lambda e, mt=mt: e.tensor_tensor(
                h8[mt % 2][:, :], silu[mt % 2][:, :], up[mt % 2][:, :], ALU.mult),
                "dve", waits=wl)
            DMA("sync", SS[mt % 2], lambda mt=mt: h_d[0, mt],
                lambda mt=mt: h8[mt % 2][:, 0:512],
                waits=[("dve", dve_h[mt])])
            lx_h[mt] = DMA("sync", SS[mt % 2], lambda mt=mt: h_d[1, mt],
                           lambda mt=mt: h8[mt % 2][:, 512:1024])
        BAR()

        dve_fr_last = None
        lx_fr = {}
        ss_h = SSFULL()
        first_hq = True
        for g in range(8):
            base = cnt["pe"]
            lx_hq = {}
            lw_wd = {}
            for qi, (q0, nk) in enumerate(QUARTS):
                wl = []
                if first_hq:
                    wl += ss_h
                    if dbg:
                        wl.append(("dz", dz_ln2))
                if qi >= 2:
                    p0, pn = QUARTS[qi - 2]
                    wl.append(("pe", base + p0 + pn))
                elif g >= 1:
                    wl.append(("pe", base))
                first_hq = False
                v = None
                for rb in range(2):
                    v = DMA("sync", LS[qi % 2],
                            lambda qi=qi, nk=nk, rb=rb: hq[qi % 2][:, 0:nk, 512 * rb:512 * (rb + 1)],
                            lambda q0=q0, nk=nk, rb=rb: h_d[rb, q0:q0 + nk].rearrange("k p t -> p k t"),
                            waits=wl if rb == 0 else ())
                lx_hq[qi] = v
            for oi, (o0, no) in enumerate(OCTS):
                wl = []
                if oi >= 2:
                    p0, pn = OCTS[oi - 2]
                    wl.append(("pe", base + p0 + pn))
                elif g >= 1:
                    wl.append(("pe", base))
                lw_wd[oi] = DMA(
                    "gpsimd", WS[oi % 2], lambda oi=oi, no=no: wd[oi % 2][:, 0:no, :],
                    lambda o0=o0, no=no, g=g: wdT.rearrange(
                        "(c p) n -> p c n", p=128)[:, o0:o0 + no, 512 * g:512 * (g + 1)],
                    waits=wl)
            qi = 0
            v = None
            for k in range(IC):
                while qi + 1 < 4 and k >= QUARTS[qi][0] + QUARTS[qi][1]:
                    qi += 1
                oi = k // 8
                q0 = QUARTS[qi][0]
                wl = []
                if k == q0:
                    wl.append((LS[qi % 2], lx_hq[qi]))
                if k == OCTS[oi][0]:
                    wl.append((WS[oi % 2], lw_wd[oi]))
                if k == 0 and g >= 1:
                    wl.append(("dve", dve_fr_last))
                first = True
                for m in range(4):
                    for rb in range(2):
                        last = (m == 3 and rb == 1)
                        v = OP("tensor", lambda e, k=k, m=m, rb=rb, oi=oi, q0=q0, qi=qi: e.matmul(
                            bank(2 * m + rb), wd[oi % 2][:, k - 8 * oi, 128 * m:128 * (m + 1)],
                            hq[qi % 2][:, k - q0, 512 * rb:512 * (rb + 1)],
                            start=(k == 0), stop=(k == IC - 1), skip_group_check=True),
                            sem="pe" if last else None,
                            waits=wl if first else ())
                        first = False
            pe_done = v
            assert cnt["pe"] == base + IC
            ei = 0
            for m in range(4):
                for rb in range(2):
                    mg = g * 4 + m
                    st = ei % 2
                    wl = [("pe", pe_done)]
                    prev = lx_fr.get((g, ei - 2))
                    if prev:
                        wl.append((SS[ei % 2], prev))
                    dve_fr_last = OP("vector", lambda e, m=m, rb=rb, mg=mg, st=st: e.tensor_scalar(
                        frs[st][:, :], bank(2 * m + rb), adt[:, mg:mg + 1], None, ALU.mult),
                        "dve", waits=wl)
                    lx_fr[(g, ei)] = DMA("sync", SS[ei % 2],
                                         lambda rb=rb, mg=mg: fr_d[rb, mg],
                                         lambda st=st: frs[st][:, :],
                                         waits=[("dve", dve_fr_last)])
                    ei += 1
        BAR()

        ss_fr = SSFULL()
        d_apply_prev = None
        d_mr_prev = None
        first_gp = True
        first_sy = True
        for rb in range(2):
            lw_f, act_sf, pe_sf = {}, {}, {}
            for c in range(32):
                st = c % 2
                wl = []
                if first_gp:
                    wl += ss_fr
                    first_gp = False
                if c >= 2:
                    wl.append(("pe", pe_sf[c - 2]))
                lw_f[c] = DMA("gpsimd", US[c % 2], lambda st=st: fb[st][:, :],
                              lambda rb=rb, c=c: fr_d[rb, c], waits=wl)
                wl = [(US[c % 2], lw_f[c])]
                if c >= 2:
                    wl.append(("pe", pe_sf[c - 2]))
                act_sf[c] = OP("scalar", lambda e, st=st: e.activation(
                    sf[st][:, :], fb[st][:, :], AF.Square), "act", waits=wl)
                wl = [(US[c % 2], lw_f[c])]
                if rb == 1 and c == 0:
                    wl.append(("dve", d_mr_prev))
                OP("tensor", lambda e, c=c, st=st: e.matmul(
                    bank(4, 0, 1), ones_bf[:, :], fb[st][:, :],
                    start=(c == 0), stop=(c == 31), skip_group_check=True),
                    waits=wl)
                pe_sf[c] = OP("tensor", lambda e, c=c, st=st: e.matmul(
                    bank(5, 0, 1), ones_bf[:, :], sf[st][:, :],
                    start=(c == 0), stop=(c == 31), skip_group_check=True),
                    sem="pe", waits=[("act", act_sf[c])])
            d_mr_prev = ln_strips(512, 2048, 2560, pe_sf[31])
            wl = [("dve", d_mr_prev)]
            if rb == 1:
                wl.append(("dve", d_apply_prev))
            OP("tensor", lambda e: e.matmul(
                bank(6), ones_f[:, :], S1[0:1, 0:512],
                start=True, stop=True, skip_group_check=True), waits=wl)
            pe_bc3 = OP("tensor", lambda e: e.matmul(
                bank(7), ones_f[:, :], S2[0:1, 0:512],
                start=True, stop=True, skip_group_check=True), "pe")
            lx_xc, lx_fc, dve_tb, pool_o, lx_o = {}, {}, {}, {}, {}
            for c in range(32):
                st = c % 2
                wl = []
                if first_sy:
                    wl += ss_fr
                    first_sy = False
                if c >= 2:
                    wl.append(("pool", pool_o[c - 2]))
                lx_xc[c] = DMA("sync", LS[c % 2], lambda st=st: x1c[st][:, :],
                               lambda c=c, rb=rb: x1_d[c][:, 512 * rb:512 * (rb + 1)],
                               waits=wl)
                wl = []
                if c >= 2:
                    wl.append(("dve", dve_tb[c - 2]))
                lx_fc[c] = DMA("sync", LS2[c % 2], lambda st=st: frc[st][:, :],
                               lambda c=c, rb=rb: fr_d[rb, c], waits=wl)
                wl = [(LS2[c % 2], lx_fc[c]), ("pe", pe_bc3)]
                if c >= 2:
                    wl.append(("dve", dve_tb[c - 2]))
                OP("vector", lambda e, st=st: e.tensor_tensor(
                    tA[st][:, :], frc[st][:, :], bank(6), ALU.mult), waits=wl)
                wl = []
                if c >= 2:
                    wl.append(("pool", pool_o[c - 2]))
                dve_tb[c] = OP("vector", lambda e, st=st: e.tensor_tensor(
                    tB[st][:, :], tA[st][:, :], bank(7), ALU.subtract), "dve", waits=wl)
                wl = [("dve", dve_tb[c]), (LS[c % 2], lx_xc[c])]
                if c >= 2:
                    wl.append(("sy", lx_o[c - 2]))
                pool_o[c] = OP("gpsimd", lambda e, st=st: e.tensor_add(
                    osb[st][:, :], tB[st][:, :], x1c[st][:, :]), "pool", waits=wl)
                lx_o[c] = DMA("sync", "sy",
                              lambda c=c, rb=rb: y_t[128 * c:128 * (c + 1), 512 * rb:512 * (rb + 1)],
                              lambda st=st: osb[st][:, :],
                              waits=[("pool", pool_o[c])])
            d_apply_prev = dve_tb[31]
        if E == "sync":
            e.wait_ge(sems["sy"], cnt["sy"])
            if dbg:
                e.wait_ge(sems["dz"], cnt["dz"])
        return cnt

    for s in SEMS:
        sems[s] = nc.semaphore(s).__enter__()
    with (
        nc.Block() as blk,
    ):
        @blk.sync
        def _(e):
            walk("sync", e)

        @blk.gpsimd
        def _(e):
            walk("gpsimd", e)

        @blk.tensor
        def _(e):
            walk("tensor", e)

        @blk.scalar
        def _(e):
            walk("scalar", e)

        @blk.vector
        def _(e):
            walk("vector", e)

    return nc


def stage_inputs(inputs):
    x = np.ascontiguousarray(np.asarray(inputs["L0_attn_x"], dtype=np.float32))
    ws = {n: np.ascontiguousarray(np.asarray(inputs[k], dtype=np.float32))
          for n, k in (("wq", "L0_attn_wq"), ("wk", "L0_attn_wk"),
                       ("wv", "L0_attn_wv"), ("wo", "L0_attn_wo"))}
    wgT = np.ascontiguousarray(np.asarray(inputs["L0_ffn_Wg"], dtype=np.float32).T)
    wuT = np.ascontiguousarray(np.asarray(inputs["L0_ffn_Wu"], dtype=np.float32).T)
    wdT = np.ascontiguousarray(np.asarray(inputs["L0_ffn_Wd"], dtype=np.float32).T)
    agt = np.ascontiguousarray(np.asarray(inputs["L0_ffn_ag"], dtype=np.float32).reshape(86, 128).T)
    aut = np.ascontiguousarray(np.asarray(inputs["L0_ffn_au"], dtype=np.float32).reshape(86, 128).T)
    adt = np.ascontiguousarray(np.asarray(inputs["L0_ffn_ad"], dtype=np.float32).reshape(32, 128).T)
    pswap = np.zeros((128, 128), np.float32)
    for i in range(64):
        pswap[2 * i, 2 * i + 1] = 1.0
        pswap[2 * i + 1, 2 * i] = 1.0
    ident = np.eye(128, dtype=np.float32)
    consts = np.zeros((128, 16), np.float32)
    consts[:, 0] = -C_EXP
    consts[:, 1] = EPS

    inv = 1.0 / 10000.0 ** (np.arange(0, 128, 2)[:64].astype(np.float32) / 128.0)
    cls = {}
    for h in (0, 1):
        pos = np.concatenate([np.arange(h * 1024, (h + 1) * 1024),
                              np.arange((1 - h) * 1024, (2 - h) * 1024)]).astype(np.float32)
        f = np.outer(inv, pos)
        ropec = np.repeat(np.cos(f), 2, axis=0).astype(np.float32)
        ropes = np.repeat(np.sin(f), 2, axis=0).astype(np.float32)
        ropes[0::2] *= -1.0
        gq = h * 1024 + np.arange(1024)
        masks = np.zeros((28, 128, 512), np.float32)
        idx = 0
        for rb in range(2):
            rows = gq[512 * rb:512 * (rb + 1)]
            for kt in KTS[rb]:
                keys = pos[128 * kt:128 * (kt + 1)]
                masks[idx] = (keys[:, None] <= rows[None, :]).astype(np.float32)
                idx += 1
        cls[h] = (ropec, ropes, masks)

    in_maps = []
    for c in range(8):
        b, h = c // 2, c % 2
        own = x[b, h * 1024:(h + 1) * 1024]
        par = x[b, (1 - h) * 1024:(2 - h) * 1024]
        xbt = np.ascontiguousarray(np.concatenate([own, par], 0).T)
        ropec, ropes, masks = cls[h]
        in_maps.append({
            "xbt": xbt, "wq": ws["wq"], "wk": ws["wk"], "wv": ws["wv"],
            "wo": ws["wo"], "wgT": wgT, "wuT": wuT, "wdT": wdT,
            "agt": agt, "aut": aut, "adt": adt,
            "ropec": ropec, "ropes": ropes, "masks": masks,
            "pswap": pswap, "ident": ident, "consts": consts,
        })
    return in_maps


def kernel(**inputs):
    nc = build(dbg=False)
    in_maps = stage_inputs(inputs)
    res = run_bass_kernel_spmd(nc, in_maps, core_ids=list(range(8)))
    out = np.empty((4, 2048, 4096), np.float32)
    for c in range(8):
        b, h = c // 2, c % 2
        out[b, h * 1024:(h + 1) * 1024, :] = res.results[c]["y_t"].T
    return out


# version 08
# speedup vs baseline: 1.1033x; 1.0470x over previous
import numpy as np
import ml_dtypes
import concourse.bass as bass
import concourse.mybir as mybir
from concourse.bass_utils import run_bass_kernel_spmd

F32 = mybir.dt.float32
BF16 = mybir.dt.bfloat16
U8 = mybir.dt.uint8
AF = mybir.ActivationFunctionType
ALU = mybir.AluOpType

DC = 32
IC = 86
C_EXP = 20.0
SM_SCALE = float(1.0 / np.sqrt(128.0))
EPS = 1e-5
RN = 1.0 / 4096.0
KTS = [[0, 1, 2, 3, 8, 9, 10, 11, 12, 13, 14, 15], list(range(16))]
QUARTS = [(0, 22), (22, 22), (44, 21), (65, 21)]
OCTS = [(8 * i, min(8, IC - 8 * i)) for i in range((IC + 7) // 8)]

SEMS = ("s0", "sx", "sx2", "w0", "w1", "u0", "u1", "la", "lb", "lc", "ld",
        "sa", "sb", "lq", "lm", "sy", "dz",
        "pe", "act", "dve", "pool", "ph")
WS = ("w0", "w1")
US = ("u0", "u1")
LS = ("la", "lb")
LS2 = ("lc", "ld")
SS = ("sa", "sb")


def _off():
    per = {}
    o = 0
    def put(name, size):
        nonlocal o
        per[name] = o
        o += (size + 31) // 32 * 32
    put("ropeC", 4096)
    put("ropeS", 4096)
    put("consts", 64)
    put("ones_bf", 32)
    put("ones_f", 512)
    put("pswap", 256)
    put("ident", 256)
    put("agt", 344)
    put("aut", 344)
    put("adt", 128)
    put("S1", 8192)
    put("S2", 8192)
    put("att1", 2048)
    ab = (o + 31) // 32 * 32
    ar = {
        "sq0": 0, "sq1": 4096,
        "r_sb": 172032, "mr_sb": 176128,
        "LN1F": 40960,
        "wsl0": 0, "wsl1": 8192,
        "ksb0": 16384, "ksb1": 20480,
        "t1b": 24576, "t2b": 28672,
        "ko0": 32768, "ko1": 36864,
        "qsb": 0,
        "mkb": 32768,
        "kb0": 49152, "kb1": 53248,
        "vb0": 57344, "vb1": 61440,
        "pt0": 65536, "pt1": 66560,
        "pm0": 67584, "pm1": 68608,
        "ob0": 69632, "ob1": 71680,
        "OT": 106496,
        "ars0": 16384, "ars1": 20480,
        "arB": 32768,
        "sqa0": 28672, "sqa1": 30720,
        "xs0": 98304, "xs1": 102400,
        "t6a0": 16384, "t6a1": 20480,
        "t6b0": 24576, "t6b1": 28672,
        "x1s0": 0, "x1s1": 4096,
        "r2sb": 176128, "mr2sb": 178176,
        "sq60": 172032, "sq61": 174080,
        "XH1": 106496,
        "t7d": 180224, "t7p": 180224,
        "LN2": 40960,
        "wu0": 16384, "wu1": 24576,
        "silu0": 32768, "silu1": 34816,
        "up0": 36864, "up1": 38912,
        "h80": 106496, "h81": 108544,
        "hq0": 0, "hq1": 45056,
        "wd0": 90112, "wd1": 98304,
        "frs0": 106496, "frs1": 108544,
        "fb0": 0, "fb1": 1024,
        "sf0": 2048, "sf1": 3072,
        "x1c0": 4096, "x1c1": 6144,
        "frc0": 8192, "frc1": 10240,
        "tA0": 12288, "tA1": 14336,
        "tB0": 16384, "tB1": 18432,
        "os0": 20480, "os1": 22528,
    }
    return per, ab, ar


def build(dbg=False):
    nc = bass.Bass("TRN2", target_bir_lowering=False)
    ik = "ExternalOutput" if dbg else "Internal"

    xbt = nc.dram_tensor("xbt", [4096, 2048], F32, kind="ExternalInput").ap()
    wq = nc.dram_tensor("wq", [4096, 4096], F32, kind="ExternalInput").ap()
    wk = nc.dram_tensor("wk", [4096, 4096], F32, kind="ExternalInput").ap()
    wv = nc.dram_tensor("wv", [4096, 4096], F32, kind="ExternalInput").ap()
    wo = nc.dram_tensor("wo", [4096, 4096], F32, kind="ExternalInput").ap()
    wgT = nc.dram_tensor("wgT", [4096, 11008], F32, kind="ExternalInput").ap()
    wuT = nc.dram_tensor("wuT", [4096, 11008], F32, kind="ExternalInput").ap()
    wdT = nc.dram_tensor("wdT", [11008, 4096], F32, kind="ExternalInput").ap()
    agt_d = nc.dram_tensor("agt", [128, 86], F32, kind="ExternalInput").ap()
    aut_d = nc.dram_tensor("aut", [128, 86], F32, kind="ExternalInput").ap()
    adt_d = nc.dram_tensor("adt", [128, 32], F32, kind="ExternalInput").ap()
    ropec_d = nc.dram_tensor("ropec", [128, 2048], F32, kind="ExternalInput").ap()
    ropes_d = nc.dram_tensor("ropes", [128, 2048], F32, kind="ExternalInput").ap()
    masks_d = nc.dram_tensor("masks", [28, 128, 512], F32, kind="ExternalInput").ap()
    pswap_d = nc.dram_tensor("pswap", [128, 128], F32, kind="ExternalInput").ap()
    ident_d = nc.dram_tensor("ident", [128, 128], F32, kind="ExternalInput").ap()
    consts_d = nc.dram_tensor("consts", [128, 16], F32, kind="ExternalInput").ap()
    y_t = nc.dram_tensor("y_t", [4096, 1024], F32, kind="ExternalOutput").ap()

    q_d = nc.dram_tensor("q_d", [32, 128, 1024], BF16, kind=ik).ap()
    k_d = nc.dram_tensor("k_d", [32, 128, 2048], BF16, kind=ik).ap()
    v4_d = nc.dram_tensor("v4_d", [32, 128, 16, 128], BF16, kind=ik).ap()
    x1_d = nc.dram_tensor("x1_d", [32, 128, 1024], F32, kind=ik).ap()
    h_d = nc.dram_tensor("h_d", [2, 86, 128, 512], BF16, kind=ik).ap()
    fr_d = nc.dram_tensor("fr_d", [2, 32, 128, 512], F32, kind=ik).ap()
    if dbg:
        dbg_ln1 = nc.dram_tensor("dbg_ln1", [32, 128, 2048], BF16, kind="ExternalOutput").ap()
        dbg_ot = nc.dram_tensor("dbg_ot", [32, 128, 1024], BF16, kind="ExternalOutput").ap()
        dbg_ln2 = nc.dram_tensor("dbg_ln2", [32, 128, 1024], BF16, kind="ExternalOutput").ap()
        dbg_rsb = nc.dram_tensor("dbg_rsb", [128, 4096], BF16, kind="ExternalOutput").ap()

    per, ab, ar = _off()
    space = nc.alloc_sbuf_tensor("space", [128, nc.sbuf_bytes_remaining - 64], U8)
    A = nc.lookup_mloc(space).addr

    def at(name, shape, dtype, off):
        return nc.alloc_sbuf_tensor_at(name, shape, dtype, offset=A + off)

    ropeC = at("ropeC", [128, 2048], BF16, per["ropeC"])
    ropeS = at("ropeS", [128, 2048], BF16, per["ropeS"])
    consts = at("consts", [128, 16], F32, per["consts"])
    ones_bf = at("ones_bf", [128, 1], BF16, per["ones_bf"])
    ones_f = at("ones_f", [1, 128], F32, per["ones_f"])
    pswap = at("pswap", [128, 128], BF16, per["pswap"])
    ident = at("ident", [128, 128], BF16, per["ident"])
    agt = at("agt_s", [128, 86], F32, per["agt"])
    aut = at("aut_s", [128, 86], F32, per["aut"])
    adt = at("adt_s", [128, 32], F32, per["adt"])
    S1 = at("S1", [128, 2048], F32, per["S1"])
    S2 = at("S2", [128, 2048], F32, per["S2"])
    att1 = at("att1", [128, 512], F32, per["att1"])

    sq = [at("sq0", [128, 2048], BF16, ab + ar["sq0"]),
          at("sq1", [128, 2048], BF16, ab + ar["sq1"])]
    r_sb = at("r_sb", [128, 2048], BF16, ab + ar["r_sb"])
    mr_sb = at("mr_sb", [128, 2048], BF16, ab + ar["mr_sb"])
    LN1F = at("LN1F", [128, 32, 2048], BF16, ab + ar["LN1F"])
    wsl = [at("wsl0", [128, 32, 128], BF16, ab + ar["wsl0"]),
           at("wsl1", [128, 32, 128], BF16, ab + ar["wsl1"])]
    ksb = [at("ksb0", [128, 2048], BF16, ab + ar["ksb0"]),
           at("ksb1", [128, 2048], BF16, ab + ar["ksb1"])]
    t1b = at("t1b", [128, 2048], BF16, ab + ar["t1b"])
    t2b = at("t2b", [128, 2048], BF16, ab + ar["t2b"])
    ko = [at("ko0", [128, 2048], BF16, ab + ar["ko0"]),
          at("ko1", [128, 2048], BF16, ab + ar["ko1"])]
    qsb = at("qsb", [128, 32, 512], BF16, ab + ar["qsb"])
    mkb = at("mkb", [128, 16, 512], BF16, ab + ar["mkb"])
    kb = [at("kb0", [128, 2048], BF16, ab + ar["kb0"]),
          at("kb1", [128, 2048], BF16, ab + ar["kb1"])]
    vb = [at("vb0", [128, 16, 128], BF16, ab + ar["vb0"]),
          at("vb1", [128, 16, 128], BF16, ab + ar["vb1"])]
    pt = [at("pt0", [128, 512], BF16, ab + ar["pt0"]),
          at("pt1", [128, 512], BF16, ab + ar["pt1"])]
    pm = [at("pm0", [128, 512], BF16, ab + ar["pm0"]),
          at("pm1", [128, 512], BF16, ab + ar["pm1"])]
    ob = [at("ob0", [128, 512], F32, ab + ar["ob0"]),
          at("ob1", [128, 512], F32, ab + ar["ob1"])]
    OT = at("OT", [128, 32, 1024], BF16, ab + ar["OT"])
    ars = [at("ars0", [128, 1024], F32, ab + ar["ars0"]),
           at("ars1", [128, 1024], F32, ab + ar["ars1"])]
    arB = at("arB", [128, 32, 1024], BF16, ab + ar["arB"])
    sqa = [at("sqa0", [128, 1024], BF16, ab + ar["sqa0"]),
           at("sqa1", [128, 1024], BF16, ab + ar["sqa1"])]
    xs = [at("xs0", [128, 1024], F32, ab + ar["xs0"]),
          at("xs1", [128, 1024], F32, ab + ar["xs1"])]
    t6a = [at("t6a0", [128, 1024], F32, ab + ar["t6a0"]),
           at("t6a1", [128, 1024], F32, ab + ar["t6a1"])]
    t6b = [at("t6b0", [128, 1024], F32, ab + ar["t6b0"]),
           at("t6b1", [128, 1024], F32, ab + ar["t6b1"])]
    x1s = [at("x1s0", [128, 1024], F32, ab + ar["x1s0"]),
           at("x1s1", [128, 1024], F32, ab + ar["x1s1"])]
    r2sb = at("r2sb", [128, 1024], BF16, ab + ar["r2sb"])
    mr2sb = at("mr2sb", [128, 1024], BF16, ab + ar["mr2sb"])
    sq6 = [at("sq60", [128, 1024], BF16, ab + ar["sq60"]),
           at("sq61", [128, 1024], BF16, ab + ar["sq61"])]
    XH1 = at("XH1", [128, 32, 1024], BF16, ab + ar["XH1"])
    t7d = at("t7d", [128, 1024], BF16, ab + ar["t7d"])
    t7p = at("t7p", [128, 1024], BF16, ab + ar["t7p"])
    LN2 = at("LN2", [128, 32, 1024], BF16, ab + ar["LN2"])
    wu = [at("wu0", [128, 32, 128], BF16, ab + ar["wu0"]),
          at("wu1", [128, 32, 128], BF16, ab + ar["wu1"])]
    silu = [at("silu0", [128, 1024], BF16, ab + ar["silu0"]),
            at("silu1", [128, 1024], BF16, ab + ar["silu1"])]
    up = [at("up0", [128, 1024], BF16, ab + ar["up0"]),
          at("up1", [128, 1024], BF16, ab + ar["up1"])]
    h8 = [at("h80", [128, 1024], BF16, ab + ar["h80"]),
          at("h81", [128, 1024], BF16, ab + ar["h81"])]
    hq = [at("hq0", [128, 22, 1024], BF16, ab + ar["hq0"]),
          at("hq1", [128, 22, 1024], BF16, ab + ar["hq1"])]
    wd = [at("wd0", [128, 8, 512], BF16, ab + ar["wd0"]),
          at("wd1", [128, 8, 512], BF16, ab + ar["wd1"])]
    frs = [at("frs0", [128, 512], F32, ab + ar["frs0"]),
           at("frs1", [128, 512], F32, ab + ar["frs1"])]
    fb = [at("fb0", [128, 512], BF16, ab + ar["fb0"]),
          at("fb1", [128, 512], BF16, ab + ar["fb1"])]
    sf = [at("sf0", [128, 512], BF16, ab + ar["sf0"]),
          at("sf1", [128, 512], BF16, ab + ar["sf1"])]
    x1c = [at("x1c0", [128, 512], F32, ab + ar["x1c0"]),
           at("x1c1", [128, 512], F32, ab + ar["x1c1"])]
    frc = [at("frc0", [128, 512], F32, ab + ar["frc0"]),
           at("frc1", [128, 512], F32, ab + ar["frc1"])]
    tA = [at("tA0", [128, 512], F32, ab + ar["tA0"]),
          at("tA1", [128, 512], F32, ab + ar["tA1"])]
    tB = [at("tB0", [128, 512], F32, ab + ar["tB0"]),
          at("tB1", [128, 512], F32, ab + ar["tB1"])]
    osb = [at("os0", [128, 512], F32, ab + ar["os0"]),
           at("os1", [128, 512], F32, ab + ar["os1"])]

    ps_h = nc.alloc_psum_tensor("psall", [128, 4096], F32)

    def bank(b, p0=0, pn=128, n=512):
        return ps_h[p0:p0 + pn, 512 * b:512 * b + n]

    sems = {}

    def walk(E, e):
        cnt = {k: 0 for k in SEMS}

        def W(sem, val):
            if val and val > 0:
                e.wait_ge(sems[sem], val)

        def DMA(eng, sem, outf, inf, waits=()):
            cnt[sem] += 16
            if eng == E:
                for (s, v) in waits:
                    W(s, v)
                e.dma_start(out=outf(), in_=inf()).then_inc(sems[sem], 16)
            return cnt[sem]

        def OP(eng, fn, sem=None, waits=()):
            if sem:
                cnt[sem] += 1
            if eng == E:
                for (s, v) in waits:
                    W(s, v)
                ins = fn(e)
                if sem:
                    ins.then_inc(sems[sem], 1)
            return cnt[sem] if sem else None

        def BAR():
            cnt["ph"] += 5
            e.nop().then_inc(sems["ph"], 1)
            e.wait_ge(sems["ph"], cnt["ph"])

        def SSFULL():
            return [("sa", cnt["sa"]), ("sb", cnt["sb"])]

        DMA("gpsimd", "s0", lambda: ropeC[:, :], lambda: ropec_d)
        DMA("gpsimd", "s0", lambda: ropeS[:, :], lambda: ropes_d)
        DMA("gpsimd", "s0", lambda: consts[:, :], lambda: consts_d)
        DMA("gpsimd", "s0", lambda: pswap[:, :], lambda: pswap_d)
        DMA("gpsimd", "s0", lambda: ident[:, :], lambda: ident_d)
        DMA("gpsimd", "s0", lambda: agt[:, :], lambda: agt_d)
        DMA("gpsimd", "s0", lambda: aut[:, :], lambda: aut_d)
        DMA("gpsimd", "s0", lambda: adt[:, :], lambda: adt_d)
        OP("vector", lambda e: e.memset(ones_bf[:, :], 1.0), "dve")
        OP("vector", lambda e: e.memset(ones_f[:, :], 1.0), "dve")
        if E == "gpsimd":
            e.wait_ge(sems["s0"], cnt["s0"])
        BAR()

        XSEM = ("sx", "sx2", "lq", "lm")
        xg = {}
        act_sq = []
        pe_chunk = []
        pe0 = cnt["pe"]
        for g in range(8):
            a, b = 4 * g, 4 * g + 4
            wl = []
            if g >= 4:
                wl.append(("pe", pe0 + 4 * (g - 4) + 4))
            xg[g] = DMA("gpsimd", XSEM[g % 4],
                        lambda a=a, b=b: LN1F[:, a:b, :],
                        lambda a=a, b=b: xbt[128 * a:128 * b, :].rearrange(
                            "(c p) n -> p c n", p=128), waits=wl)
        for c in range(32):
            s_ = sq[c % 2]
            g = c // 4
            gw = (XSEM[g % 4], xg[g])
            wl = [gw, ("dve", 2)] if c % 4 == 0 else []
            if c >= 2:
                wl.append(("pe", pe_chunk[c - 2]))
            act_sq.append(OP("scalar", lambda e, c=c, s_=s_: e.activation(
                s_[:, :], LN1F[:, c, :], AF.Square), "act", waits=wl))
            for sl in range(4):
                OP("tensor", lambda e, c=c, sl=sl: e.matmul(
                    bank(sl, 0, 1), ones_bf[:, :], LN1F[:, c, 512 * sl:512 * (sl + 1)],
                    start=(c == 0), stop=(c == 31), skip_group_check=True),
                    waits=[gw, ("dve", 2)] if (sl == 0 and c % 4 == 0) else ())
            for sl in range(4):
                v = OP("tensor", lambda e, c=c, sl=sl, s_=s_: e.matmul(
                    bank(4 + sl, 0, 1), ones_bf[:, :], s_[:, 512 * sl:512 * (sl + 1)],
                    start=(c == 0), stop=(c == 31), skip_group_check=True),
                    sem="pe" if sl == 3 else None,
                    waits=[("act", act_sq[c])] if sl == 0 else ())
            pe_chunk.append(v)

        def ln_strips(T, cs, cq, pe_wait):
            OP("vector", lambda e: e.tensor_scalar(
                ps_h[0:1, cs:cs + T], ps_h[0:1, cs:cs + T], RN, None, ALU.mult),
                "dve", waits=[("pe", pe_wait)])
            OP("vector", lambda e: e.tensor_scalar(
                ps_h[0:1, cq:cq + T], ps_h[0:1, cq:cq + T], RN, None, ALU.mult), "dve")
            OP("vector", lambda e: e.tensor_copy(
                S2[0:1, 0:T], ps_h[0:1, cs:cs + T]), "dve")
            OP("vector", lambda e: e.tensor_tensor(
                S1[0:1, 0:T], S2[0:1, 0:T], S2[0:1, 0:T], ALU.mult), "dve")
            d4 = OP("vector", lambda e: e.tensor_tensor(
                ps_h[0:1, cq:cq + T], ps_h[0:1, cq:cq + T], S1[0:1, 0:T], ALU.subtract), "dve")
            OP("scalar", lambda e: e.activation(
                ps_h[0:1, cq:cq + T], ps_h[0:1, cq:cq + T], AF.Ln,
                bias=consts[0:1, 1:2]), "act", waits=[("dve", d4)])
            a2 = OP("scalar", lambda e: e.activation(
                S1[0:1, 0:T], ps_h[0:1, cq:cq + T], AF.Exp, scale=-0.5), "act")
            return OP("vector", lambda e: e.tensor_tensor(
                S2[0:1, 0:T], S2[0:1, 0:T], S1[0:1, 0:T], ALU.mult),
                "dve", waits=[("act", a2)])

        d_mr = ln_strips(2048, 0, 2048, pe_chunk[31])
        for sl in range(4):
            OP("tensor", lambda e, sl=sl: e.matmul(
                bank(sl), ones_f[:, :], S1[0:1, 512 * sl:512 * (sl + 1)],
                start=True, stop=True, skip_group_check=True),
                waits=[("dve", d_mr)] if sl == 0 else ())
        pe_bc = None
        for sl in range(4):
            pe_bc = OP("tensor", lambda e, sl=sl: e.matmul(
                bank(4 + sl), ones_f[:, :], S2[0:1, 512 * sl:512 * (sl + 1)],
                start=True, stop=True, skip_group_check=True),
                sem="pe" if sl == 3 else None)
        OP("scalar", lambda e: e.copy(r_sb[:, :], ps_h[:, 0:2048]),
           "act", waits=[("pe", pe_bc)])
        a_mr = OP("scalar", lambda e: e.copy(mr_sb[:, :], ps_h[:, 2048:4096]), "act")
        if dbg:
            DMA("sync", "dz", lambda: dbg_rsb[:, 0:2048], lambda: r_sb[:, :],
                waits=[("act", a_mr)])
            DMA("sync", "dz", lambda: dbg_rsb[:, 2048:4096], lambda: mr_sb[:, :])
        dz_rsb = cnt["dz"]

        d_sub = {}
        for c in range(32):
            tb = ksb[c % 2]
            wl = [("act", a_mr)] if c == 0 else []
            OP("vector", lambda e, c=c, tb=tb: e.tensor_tensor(
                tb[:, :], LN1F[:, c, :], r_sb[:, :], ALU.mult), waits=wl)
            d_sub[c] = OP("vector", lambda e, c=c, tb=tb: e.tensor_tensor(
                LN1F[:, c, :], tb[:, :], mr_sb[:, :], ALU.subtract), "dve")
        if dbg:
            for c in range(32):
                DMA("sync", "dz", lambda c=c: dbg_ln1[c], lambda c=c: LN1F[:, c, :],
                    waits=[("dve", cnt["dve"])] if c == 0 else ())
        dz_ln1 = cnt["dz"]

        def proj(w_src, nslab, out_d, rope, carry=None, carry_ksb=None,
                 carry_ko=None, carry_w=None, first_w_extra=(), mk_waits=None):
            TT = nslab * 512
            lw_s, pe_acc, pe_swap, act_ev, dve_ro, lx_ko = {}, {}, {}, {}, {}, {}
            tr_pe, av4 = {}, {}

            def emit_tr(mm):
                st_ = mm % 2
                v = None
                for kt in range(16):
                    v = OP("tensor", lambda e, st_=st_, kt=kt: e.matmul(
                        ps_h[:, 2048 * st_ + 128 * kt:2048 * st_ + 128 * (kt + 1)],
                        ksb[st_][:, 128 * kt:128 * (kt + 1)], ident[:, :],
                        start=True, stop=True, skip_group_check=True),
                        sem="pe" if kt == 15 else None,
                        waits=[("act", act_ev[mm])] if kt == 0 else ())
                tr_pe[mm] = v
                av4[mm] = OP("scalar", lambda e, st_=st_: e.copy(
                    ksb[st_][:, :], ps_h[:, 2048 * st_:2048 * (st_ + 1)]),
                    "act", waits=[("pe", tr_pe[mm])])
                lx_ko[mm] = DMA("sync", SS[mm % 2],
                                lambda mm=mm: out_d[mm].rearrange("p a b -> p (a b)"),
                                lambda st_=st_: ksb[st_][:, :],
                                waits=[("act", av4[mm])])

            def emit_swap(m):
                pst = m % 2
                base = 4 * pst + (2 if nslab == 2 else 0)
                wl = [("act", act_ev[m])]
                if m >= 2:
                    wl.append(("dve", dve_ro[m - 2]))
                v = None
                for j in range(nslab):
                    v = OP("tensor", lambda e, pst=pst, j=j, base=base: e.matmul(
                        bank(base + j), pswap[:, :],
                        ksb[pst][:, 512 * j:512 * (j + 1)],
                        start=True, stop=True, skip_group_check=True),
                        sem="pe" if j == nslab - 1 else None,
                        waits=wl if j == 0 else ())
                pe_swap[m] = v

            def emit_rope(m):
                st = m % 2
                base = 4 * st + (2 if nslab == 2 else 0)
                wl = [("act", act_ev[m]), ("pe", pe_swap[m])]
                if m < 2 and carry_ko is not None:
                    wl.append(carry_ko)
                if m >= 2:
                    wl.append((SS[m % 2], lx_ko[m - 2]))
                OP("vector", lambda e, st=st: e.tensor_tensor(
                    t1b[:, 0:TT], ksb[st][:, 0:TT], ropeC[:, 0:TT], ALU.mult),
                    waits=wl)
                OP("vector", lambda e, base=base: e.tensor_tensor(
                    t2b[:, 0:TT], ps_h[:, 512 * base:512 * base + TT],
                    ropeS[:, 0:TT], ALU.mult))
                dve_ro[m] = OP("vector", lambda e, st=st: e.tensor_add(
                    ko[st][:, 0:TT], t1b[:, 0:TT], t2b[:, 0:TT]), "dve")
                lx_ko[m] = DMA("sync", SS[m % 2], lambda m=m: out_d[m],
                               lambda st=st: ko[st][:, 0:TT],
                               waits=[("dve", dve_ro[m])])

            for m in range(32):
                st = m % 2
                wl = list(first_w_extra) if m == 0 else []
                if m < 2 and carry_w is not None:
                    wl.append(carry_w)
                if m >= 2:
                    wl = [("pe", pe_acc[m - 2])]
                lw_s[m] = DMA(
                    "gpsimd", WS[m % 2],
                    lambda st=st: wsl[st][:, :, :],
                    lambda m=m: w_src.rearrange("(c p) n -> p c n", p=128)[:, :, 128 * m:128 * (m + 1)],
                    waits=wl)
                for k in range(32):
                    if k == 8 and m >= 1:
                        (emit_swap if rope else emit_tr)(m - 1)
                    wl = []
                    if k == 0:
                        wl = [(WS[m % 2], lw_s[m])]
                        if m < 2 and carry is not None:
                            wl.append(carry)
                        if m >= 2:
                            if rope:
                                wl.append(("dve", dve_ro[m - 2]))
                            else:
                                wl.append(("act", av4[m - 2]))
                    if m == 0 and mk_waits is not None:
                        wl = wl + mk_waits(k)
                    v = None
                    for j in range(nslab):
                        v = OP("tensor", lambda e, m=m, k=k, j=j: e.matmul(
                            bank(4 * (m % 2) + j), wsl[m % 2][:, k, :],
                            LN1F[:, k, 512 * j:512 * (j + 1)],
                            start=(k == 0), stop=(k == 31), skip_group_check=True),
                            sem="pe" if (k == 31 and j == nslab - 1) else None,
                            waits=wl if (j == 0 and wl) else ())
                pe_acc[m] = v
                wl = [("pe", pe_acc[m])]
                if m < 2 and carry_ksb is not None:
                    wl.append(carry_ksb)
                if m >= 2:
                    if rope:
                        wl.append(("pe", pe_swap[m - 2]))
                        wl.append(("dve", dve_ro[m - 2]))
                    else:
                        wl.append((SS[m % 2], lx_ko[m - 2]))
                act_ev[m] = OP("scalar", lambda e, m=m: e.copy(
                    ksb[m % 2][:, 0:TT], ps_h[:, 2048 * (m % 2):2048 * (m % 2) + TT]),
                    "act", waits=wl)
                if rope and m >= 1:
                    emit_rope(m - 1)
            if rope:
                emit_swap(31)
                emit_rope(31)
                g_ps = ("dve", dve_ro[31])
                g_ksb = ("dve", dve_ro[31])
                g_ko = (SS[31 % 2], lx_ko[31])
            else:
                emit_tr(31)
                g_ps = ("act", av4[31])
                g_ksb = (SS[31 % 2], lx_ko[31])
                g_ko = None
            return lx_ko, (g_ps, g_ksb, g_ko, ("pe", pe_acc[31]))

        first_w = [("pe", pe_chunk[31])]
        if dbg:
            first_w.append(("dz", dz_rsb))
        lx_k, (k_ps, k_ksb, k_ko, k_w) = proj(wk, 4, k_d, True,
                                              first_w_extra=first_w,
                                              mk_waits=lambda k: [("dve", d_sub[k])])
        lx_v, (v_ps, v_ksb, v_ko, v_w) = proj(wv, 4, v4_d, False,
                                              carry=k_ps, carry_ksb=k_ksb,
                                              carry_w=k_w)
        lx_q, _ = proj(wq, 2, q_d, True,
                       carry=v_ps, carry_ksb=v_ksb, carry_ko=k_ko,
                       carry_w=v_w)
        BAR()

        s_last = {}
        pv_last = {}
        recip_t = {}
        oev_t = {}
        bcast_t = {}
        norm_t = {}
        mask_last_slot = [None, None]
        lpv_last_slot = [None, None]
        exp_last_bank = [None, None]
        rb_of = {}

        def finish_head(gh):
            rb = rb_of[gh]
            h = gh % 32
            wl = [("dve", recip_t[gh])]
            if gh >= 1:
                wl.append(("dve", norm_t.get(gh - 1)))
            if dbg and gh == 0:
                wl.append(("dz", dz_ln1))
            bcast_t[gh] = OP("tensor", lambda e, rb=rb: e.matmul(
                bank(rb), ones_f[:, :], att1[0:1, 0:512],
                start=True, stop=True, skip_group_check=True), "pe", waits=wl)
            norm_t[gh] = OP("vector", lambda e, rb=rb, h=h, gh=gh: e.tensor_tensor(
                OT[:, h, 512 * rb:512 * (rb + 1)], ob[gh % 2][:, :],
                bank(rb), ALU.mult), "dve",
                waits=[("act", oev_t[gh]), ("pe", bcast_t[gh])])

        lx_kb = [None, None]
        lx_vb = [None, None]
        for rb in range(2):
            kts = KTS[rb]
            n = len(kts)
            wl = SSFULL() if rb == 0 else []
            if rb == 1:
                wl.append(("pe", s_last[31]))
            lx_q4 = DMA("sync", "lq", lambda: qsb[:, :, :],
                        lambda rb=rb: q_d.rearrange("h p t -> p h t")[:, :, 512 * rb:512 * (rb + 1)],
                        waits=wl)
            i0 = 0 if rb == 0 else 12
            wl = SSFULL() if rb == 0 else []
            if rb == 1:
                wl.append(("dve", mask_last_slot[0]))
                wl.append(("dve", mask_last_slot[1]))
            lx_m = DMA("gpsimd", "lm", lambda n=n: mkb[:, 0:n, :],
                       lambda i0=i0, n=n: masks_d[i0:i0 + n].rearrange("n p t -> p n t"),
                       waits=wl)
            for h in range(32):
                gh = rb * 32 + h
                rb_of[gh] = rb
                wl = []
                if gh >= 2:
                    wl.append(("pe", s_last[gh - 2]))
                lx_kb[gh % 2] = DMA("sync", LS[gh % 2], lambda gh=gh: kb[gh % 2][:, :],
                                    lambda h=h: k_d[h], waits=wl)
                wl = []
                if gh >= 2:
                    wl.append(("pe", pv_last[gh - 2]))
                lx_vb[gh % 2] = DMA("sync", LS2[gh % 2],
                                    lambda gh=gh: vb[gh % 2][:, :, :],
                                    lambda h=h: v4_d[h],
                                    waits=wl)
                s_t = {}

                def S_mm(i):
                    kt = kts[i]
                    wl = [(LS[gh % 2], lx_kb[gh % 2]), ("lq", lx_q4)]
                    if exp_last_bank[i % 2] is not None:
                        wl.append(("act", exp_last_bank[i % 2]))
                    s_t[i] = OP("tensor", lambda e, kt=kt, i=i, h=h, gh=gh: e.matmul(
                        bank(2 + (i % 2)), kb[gh % 2][:, 128 * kt:128 * (kt + 1)],
                        qsb[:, h, :], start=True, stop=True, skip_group_check=True),
                        "pe", waits=wl)

                S_mm(0)
                if n > 1:
                    S_mm(1)
                if gh >= 1:
                    finish_head(gh - 1)
                for i in range(n):
                    wl = [("pe", s_t[i])]
                    if mask_last_slot[i % 2] is not None:
                        wl.append(("dve", mask_last_slot[i % 2]))
                    a_e = OP("scalar", lambda e, i=i: e.activation(
                        pt[i % 2][:, :], bank(2 + (i % 2)), AF.Exp,
                        bias=consts[:, 0:1], scale=SM_SCALE), "act", waits=wl)
                    exp_last_bank[i % 2] = a_e
                    wl = [("act", a_e), ("lm", lx_m)]
                    if lpv_last_slot[i % 2] is not None:
                        wl.append(("pe", lpv_last_slot[i % 2]))
                    d_m = OP("vector", lambda e, i=i: e.tensor_tensor(
                        pm[i % 2][:, :], pt[i % 2][:, :],
                        mkb[:, i, :], ALU.mult), "dve", waits=wl)
                    mask_last_slot[i % 2] = d_m
                    if i + 2 < n:
                        S_mm(i + 2)
                    wl = [("dve", d_m)]
                    if i == 0:
                        wl.append((LS2[gh % 2], lx_vb[gh % 2]))
                        if gh >= 1:
                            wl.append(("dve", recip_t[gh - 1]))
                            wl.append(("act", oev_t[gh - 1]))
                    OP("tensor", lambda e, rb=rb, i=i, n=n: e.matmul(
                        bank(6 + rb, 0, 1), ones_bf[:, :], pm[i % 2][:, :],
                        start=(i == 0), stop=(i == n - 1), skip_group_check=True),
                        waits=wl)
                    kt = kts[i]
                    v = OP("tensor", lambda e, rb=rb, i=i, n=n, kt=kt, gh=gh: e.matmul(
                        bank(4 + rb), vb[gh % 2][:, kt, :], pm[i % 2][:, :],
                        start=(i == 0), stop=(i == n - 1), skip_group_check=True), "pe")
                    lpv_last_slot[i % 2] = v
                pv_last[gh] = v
                s_last[gh] = s_t[n - 1]
                wl = [("pe", pv_last[gh])]
                if gh >= 1:
                    wl.append(("pe", bcast_t.get(gh - 1)))
                recip_t[gh] = OP("vector", lambda e, rb=rb: e.reciprocal(
                    att1[0:1, 0:512], bank(6 + rb, 0, 1)), "dve", waits=wl)
                wl = [("pe", pv_last[gh])]
                if gh >= 2:
                    wl.append(("dve", norm_t.get(gh - 2)))
                oev_t[gh] = OP("scalar", lambda e, rb=rb, gh=gh: e.copy(
                    ob[gh % 2][:, :], bank(4 + rb)), "act", waits=wl)
        finish_head(63)
        if dbg:
            DMA("sync", "dz", lambda: dbg_ot.rearrange("h p t -> p h t"), lambda: OT[:, :, :],
                waits=[("dve", norm_t[63])])
        dz_ot = cnt["dz"]
        BAR()

        lw_s, pe_acc, act_ev, dve_cs, act_sq5, pe_st, lx_ar = {}, {}, {}, {}, {}, {}, {}
        for m in range(32):
            st = m % 2
            wl = []
            if m >= 2:
                wl.append(("pe", pe_acc[m - 2]))
            lw_s[m] = DMA(
                "gpsimd", WS[m % 2], lambda st=st: wsl[st][:, :, :],
                lambda m=m: wo.rearrange("(c p) n -> p c n", p=128)[:, :, 128 * m:128 * (m + 1)],
                waits=wl)
            v = None
            for k in range(32):
                wl = []
                if k == 0:
                    wl = [(WS[m % 2], lw_s[m])]
                    if m >= 2:
                        wl.append(("act", act_ev[m - 2]))
                for j in range(2):
                    v = OP("tensor", lambda e, m=m, k=k, j=j: e.matmul(
                        bank(2 * (m % 2) + j), wsl[m % 2][:, k, :],
                        OT[:, k, 512 * j:512 * (j + 1)],
                        start=(k == 0), stop=(k == 31), skip_group_check=True),
                        sem="pe" if (k == 31 and j == 1) else None,
                        waits=wl if (j == 0 and k == 0) else ())
            pe_acc[m] = v
            wl = [("pe", pe_acc[m])]
            if m >= 2:
                wl.append(("dve", dve_cs[m - 2]))
            act_ev[m] = OP("scalar", lambda e, m=m: e.copy(
                ars[m % 2][:, :], ps_h[:, 1024 * (m % 2):1024 * (m % 2) + 1024]),
                "act", waits=wl)
            wl = [("act", act_ev[m])]
            if m >= 2:
                wl.append(("pe", pe_st[m - 2]))
            dve_cs[m] = OP("vector", lambda e, m=m: e.tensor_copy(
                arB[:, m, :], ars[m % 2][:, :]), "dve", waits=wl)
            act_sq5[m] = OP("scalar", lambda e, m=m: e.activation(
                sqa[m % 2][:, :], arB[:, m, :], AF.Square), "act",
                waits=[("dve", dve_cs[m])])
            for sl2 in range(2):
                OP("tensor", lambda e, m=m, sl2=sl2: e.matmul(
                    bank(4 + sl2, 0, 1), ones_bf[:, :],
                    arB[:, m, 512 * sl2:512 * (sl2 + 1)],
                    start=(m == 0), stop=(m == 31), skip_group_check=True),
                    waits=[("dve", dve_cs[m])] if sl2 == 0 else ())
            for sl2 in range(2):
                v = OP("tensor", lambda e, m=m, sl2=sl2: e.matmul(
                    bank(6 + sl2, 0, 1), ones_bf[:, :],
                    sqa[m % 2][:, 512 * sl2:512 * (sl2 + 1)],
                    start=(m == 0), stop=(m == 31), skip_group_check=True),
                    sem="pe" if sl2 == 1 else None,
                    waits=[("act", act_sq5[m])] if sl2 == 0 else ())
            pe_st[m] = v
        d_mr = ln_strips(1024, 2048, 3072, pe_st[31])
        for sl2 in range(2):
            OP("tensor", lambda e, sl2=sl2: e.matmul(
                bank(sl2), ones_f[:, :], S1[0:1, 512 * sl2:512 * (sl2 + 1)],
                start=True, stop=True, skip_group_check=True),
                waits=[("dve", d_mr), ("act", act_ev[31])] if sl2 == 0 else ())
        pe_bc1 = None
        for sl2 in range(2):
            pe_bc1 = OP("tensor", lambda e, sl2=sl2: e.matmul(
                bank(2 + sl2), ones_f[:, :], S2[0:1, 512 * sl2:512 * (sl2 + 1)],
                start=True, stop=True, skip_group_check=True),
                sem="pe" if sl2 == 1 else None)
        p5_pe = cnt["pe"]
        BAR()

        lx_x, dve_t2, pool_x1, act_c, act_s6, pe_st6, lx_x1 = \
            {}, {}, {}, {}, {}, {}, {}
        for c in range(32):
            st = c % 2
            wl = []
            if c >= 2:
                wl.append(("dve", pool_x1[c - 2]))
            lx_x[c] = DMA("sync", LS[c % 2], lambda st=st: xs[st][:, :],
                          lambda c=c: xbt[128 * c:128 * (c + 1), 0:1024], waits=wl)
            wl = [("pe", pe_bc1)] if c == 0 else []
            if c >= 2:
                wl.append(("dve", dve_t2[c - 2]))
            OP("vector", lambda e, c=c, st=st: e.tensor_tensor(
                t6a[st][:, :], arB[:, c, :], ps_h[:, 0:1024], ALU.mult), waits=wl)
            dve_t2[c] = OP("vector", lambda e, st=st: e.tensor_tensor(
                t6b[st][:, :], t6a[st][:, :], ps_h[:, 1024:2048], ALU.subtract),
                "dve")
            wl = [(LS[c % 2], lx_x[c])]
            if c >= 2:
                wl.append((SS[c % 2], lx_x1[c - 2]))
                wl.append(("act", act_c[c - 2]))
            pool_x1[c] = OP("vector", lambda e, st=st: e.tensor_add(
                x1s[st][:, :], t6b[st][:, :], xs[st][:, :]), "dve", waits=wl)
            lx_x1[c] = DMA("sync", SS[c % 2], lambda c=c: x1_d[c],
                           lambda st=st: x1s[st][:, :],
                           waits=[("dve", pool_x1[c])])
            wl = [("dve", pool_x1[c])]
            if dbg and c == 0:
                wl.append(("dz", dz_ot))
            act_c[c] = OP("scalar", lambda e, c=c, st=st: e.copy(
                XH1[:, c, :], x1s[st][:, :]), "act", waits=wl)
            wl = [("act", act_c[c])]
            if c >= 2:
                wl.append(("pe", pe_st6[c - 2]))
            act_s6[c] = OP("scalar", lambda e, c=c, st=st: e.activation(
                sq6[st][:, :], XH1[:, c, :], AF.Square), "act", waits=wl)
            for sl2 in range(2):
                OP("tensor", lambda e, c=c, sl2=sl2: e.matmul(
                    bank(4 + sl2, 0, 1), ones_bf[:, :],
                    XH1[:, c, 512 * sl2:512 * (sl2 + 1)],
                    start=(c == 0), stop=(c == 31), skip_group_check=True),
                    waits=[("act", act_c[c])] if sl2 == 0 else ())
            v = None
            for sl2 in range(2):
                v = OP("tensor", lambda e, c=c, st=st, sl2=sl2: e.matmul(
                    bank(6 + sl2, 0, 1), ones_bf[:, :],
                    sq6[st][:, 512 * sl2:512 * (sl2 + 1)],
                    start=(c == 0), stop=(c == 31), skip_group_check=True),
                    sem="pe" if sl2 == 1 else None,
                    waits=[("act", act_s6[c])] if sl2 == 0 else ())
            pe_st6[c] = v
        d_mr = ln_strips(1024, 2048, 3072, pe_st6[31])
        for sl2 in range(2):
            OP("tensor", lambda e, sl2=sl2: e.matmul(
                bank(4 + sl2), ones_f[:, :], S1[0:1, 512 * sl2:512 * (sl2 + 1)],
                start=True, stop=True, skip_group_check=True),
                waits=[("dve", d_mr)] if sl2 == 0 else ())
        pe_bc2 = None
        for sl2 in range(2):
            pe_bc2 = OP("tensor", lambda e, sl2=sl2: e.matmul(
                bank(6 + sl2), ones_f[:, :], S2[0:1, 512 * sl2:512 * (sl2 + 1)],
                start=True, stop=True, skip_group_check=True),
                sem="pe" if sl2 == 1 else None)
        OP("scalar", lambda e: e.copy(r2sb[:, :], ps_h[:, 2048:3072]),
           "act", waits=[("pe", pe_bc2)])
        a_mr2 = OP("scalar", lambda e: e.copy(mr2sb[:, :], ps_h[:, 3072:4096]), "act")
        p6_dve = pool_x1[31]

        d_mul, d_sub7 = {}, {}
        for c in range(32):
            wl = [("act", a_mr2)] if c == 0 else []
            d_mul[c] = OP("vector", lambda e, c=c: e.tensor_tensor(
                t7d[:, :], XH1[:, c, :], r2sb[:, :], ALU.mult), waits=wl)
            d_sub7[c] = OP("vector", lambda e, c=c: e.tensor_tensor(
                LN2[:, c, :], t7d[:, :], mr2sb[:, :], ALU.subtract), "dve")
        BAR()

        if dbg:
            for c in range(32):
                DMA("sync", "dz", lambda c=c: dbg_ln2[c], lambda c=c: LN2[:, c, :],
                    waits=[("dve", d_sub7[31])] if c == 0 else ())
        dz_ln2 = cnt["dz"]
        lw_g, lw_u, pe_g, pe_u, act_si, dve_up, dve_h, lx_h = \
            {}, {}, {}, {}, {}, {}, {}, {}
        ss_x1 = SSFULL()
        act_c31 = act_c[31]
        for mt in range(IC):
            st = mt % 2
            wl = []
            if mt < 2:
                wl.append(("act", act_c31))
                wl += ss_x1
            if mt >= 2:
                wl.append(("pe", pe_g[mt - 2]))
            lw_g[mt] = DMA(
                "gpsimd", WS[mt % 2], lambda st=st: wsl[st][:, :, :],
                lambda mt=mt: wgT.rearrange("(c p) n -> p c n", p=128)[:, :, 128 * mt:128 * (mt + 1)],
                waits=wl)
            wl = []
            if mt < 2:
                wl.append(("dve", p6_dve))
            if mt >= 2:
                wl.append(("pe", pe_u[mt - 2]))
            lw_u[mt] = DMA(
                "gpsimd", US[mt % 2], lambda st=st: wu[st][:, :, :],
                lambda mt=mt: wuT.rearrange("(c p) n -> p c n", p=128)[:, :, 128 * mt:128 * (mt + 1)],
                waits=wl)
            v = None
            for k in range(32):
                wl = []
                if k == 0:
                    wl = [(WS[mt % 2], lw_g[mt])]
                    if mt >= 2:
                        wl.append(("act", act_si[mt - 2]))
                if mt == 0:
                    wl = wl + [("dve", d_sub7[k])]
                if mt == 1 and k == 0:
                    wl.append(("act", a_mr2))
                for j in range(2):
                    v = OP("tensor", lambda e, mt=mt, k=k, j=j: e.matmul(
                        bank(4 * (mt % 2) + j), wsl[mt % 2][:, k, :],
                        LN2[:, k, 512 * j:512 * (j + 1)],
                        start=(k == 0), stop=(k == 31), skip_group_check=True),
                        sem="pe" if (k == 31 and j == 1) else None,
                        waits=wl if (j == 0 and k == 0) else ())
            pe_g[mt] = v
            for k in range(32):
                wl = []
                if k == 0:
                    wl = [(US[mt % 2], lw_u[mt])]
                    if mt >= 2:
                        wl.append(("dve", dve_up[mt - 2]))
                for j in range(2):
                    v = OP("tensor", lambda e, mt=mt, k=k, j=j: e.matmul(
                        bank(4 * (mt % 2) + 2 + j), wu[mt % 2][:, k, :],
                        LN2[:, k, 512 * j:512 * (j + 1)],
                        start=(k == 0), stop=(k == 31), skip_group_check=True),
                        sem="pe" if (k == 31 and j == 1) else None,
                        waits=wl if (j == 0 and k == 0) else ())
            pe_u[mt] = v
            wl = [("pe", pe_g[mt])]
            if mt == 0:
                wl += ss_x1
            if mt >= 2:
                wl.append(("dve", dve_h[mt - 2]))
            act_si[mt] = OP("scalar", lambda e, mt=mt: e.activation(
                silu[mt % 2][:, :], ps_h[:, 2048 * (mt % 2):2048 * (mt % 2) + 1024],
                AF.Silu, scale=agt[:, mt:mt + 1]), "act", waits=wl)
            wl = [("pe", pe_u[mt])]
            if mt == 0:
                wl += ss_x1
            if mt >= 2:
                wl.append(("dve", dve_h[mt - 2]))
            dve_up[mt] = OP("vector", lambda e, mt=mt: e.tensor_scalar(
                up[mt % 2][:, :], ps_h[:, 2048 * (mt % 2) + 1024:2048 * (mt % 2) + 2048],
                aut[:, mt:mt + 1], None, ALU.mult), "dve", waits=wl)
            wl = [("act", act_si[mt]), ("dve", dve_up[mt])]
            if mt >= 2:
                wl.append((SS[mt % 2], lx_h[mt - 2]))
            dve_h[mt] = OP("vector", lambda e, mt=mt: e.tensor_tensor(
                h8[mt % 2][:, :], silu[mt % 2][:, :], up[mt % 2][:, :], ALU.mult),
                "dve", waits=wl)
            DMA("sync", SS[mt % 2], lambda mt=mt: h_d[0, mt],
                lambda mt=mt: h8[mt % 2][:, 0:512],
                waits=[("dve", dve_h[mt])])
            lx_h[mt] = DMA("sync", SS[mt % 2], lambda mt=mt: h_d[1, mt],
                           lambda mt=mt: h8[mt % 2][:, 512:1024])
        BAR()

        dve_fr_last = None
        lx_fr = {}
        ss_h = SSFULL()
        first_hq = True
        for g in range(8):
            base = cnt["pe"]
            lx_hq = {}
            lw_wd = {}
            for qi, (q0, nk) in enumerate(QUARTS):
                wl = []
                if first_hq:
                    wl += ss_h
                    if dbg:
                        wl.append(("dz", dz_ln2))
                if qi >= 2:
                    p0, pn = QUARTS[qi - 2]
                    wl.append(("pe", base + p0 + pn))
                elif g >= 1:
                    wl.append(("pe", base))
                first_hq = False
                v = None
                for rb in range(2):
                    v = DMA("sync", LS[qi % 2],
                            lambda qi=qi, nk=nk, rb=rb: hq[qi % 2][:, 0:nk, 512 * rb:512 * (rb + 1)],
                            lambda q0=q0, nk=nk, rb=rb: h_d[rb, q0:q0 + nk].rearrange("k p t -> p k t"),
                            waits=wl if rb == 0 else ())
                lx_hq[qi] = v
            for oi, (o0, no) in enumerate(OCTS):
                wl = []
                if oi >= 2:
                    p0, pn = OCTS[oi - 2]
                    wl.append(("pe", base + p0 + pn))
                elif g >= 1:
                    wl.append(("pe", base))
                lw_wd[oi] = DMA(
                    "gpsimd", WS[oi % 2], lambda oi=oi, no=no: wd[oi % 2][:, 0:no, :],
                    lambda o0=o0, no=no, g=g: wdT.rearrange(
                        "(c p) n -> p c n", p=128)[:, o0:o0 + no, 512 * g:512 * (g + 1)],
                    waits=wl)
            qi = 0
            v = None
            for k in range(IC):
                while qi + 1 < 4 and k >= QUARTS[qi][0] + QUARTS[qi][1]:
                    qi += 1
                oi = k // 8
                q0 = QUARTS[qi][0]
                wl = []
                if k == q0:
                    wl.append((LS[qi % 2], lx_hq[qi]))
                if k == OCTS[oi][0]:
                    wl.append((WS[oi % 2], lw_wd[oi]))
                if k == 0 and g >= 1:
                    wl.append(("dve", dve_fr_last))
                first = True
                for m in range(4):
                    for rb in range(2):
                        last = (m == 3 and rb == 1)
                        v = OP("tensor", lambda e, k=k, m=m, rb=rb, oi=oi, q0=q0, qi=qi: e.matmul(
                            bank(2 * m + rb), wd[oi % 2][:, k - 8 * oi, 128 * m:128 * (m + 1)],
                            hq[qi % 2][:, k - q0, 512 * rb:512 * (rb + 1)],
                            start=(k == 0), stop=(k == IC - 1), skip_group_check=True),
                            sem="pe" if last else None,
                            waits=wl if first else ())
                        first = False
            pe_done = v
            assert cnt["pe"] == base + IC
            ei = 0
            for m in range(4):
                for rb in range(2):
                    mg = g * 4 + m
                    st = ei % 2
                    wl = [("pe", pe_done)]
                    prev = lx_fr.get((g, ei - 2))
                    if prev:
                        wl.append((SS[ei % 2], prev))
                    dve_fr_last = OP("vector", lambda e, m=m, rb=rb, mg=mg, st=st: e.tensor_scalar(
                        frs[st][:, :], bank(2 * m + rb), adt[:, mg:mg + 1], None, ALU.mult),
                        "dve", waits=wl)
                    lx_fr[(g, ei)] = DMA("sync", SS[ei % 2],
                                         lambda rb=rb, mg=mg: fr_d[rb, mg],
                                         lambda st=st: frs[st][:, :],
                                         waits=[("dve", dve_fr_last)])
                    ei += 1
        BAR()

        ss_fr = SSFULL()
        d_apply_prev = None
        d_mr_prev = None
        first_gp = True
        first_sy = True
        for rb in range(2):
            lw_f, act_sf, pe_sf = {}, {}, {}
            for c in range(32):
                st = c % 2
                wl = []
                if first_gp:
                    wl += ss_fr
                    first_gp = False
                if c >= 2:
                    wl.append(("pe", pe_sf[c - 2]))
                lw_f[c] = DMA("gpsimd", US[c % 2], lambda st=st: fb[st][:, :],
                              lambda rb=rb, c=c: fr_d[rb, c], waits=wl)
                wl = [(US[c % 2], lw_f[c])]
                if c >= 2:
                    wl.append(("pe", pe_sf[c - 2]))
                act_sf[c] = OP("scalar", lambda e, st=st: e.activation(
                    sf[st][:, :], fb[st][:, :], AF.Square), "act", waits=wl)
                wl = [(US[c % 2], lw_f[c])]
                if rb == 1 and c == 0:
                    wl.append(("dve", d_mr_prev))
                OP("tensor", lambda e, c=c, st=st: e.matmul(
                    bank(4, 0, 1), ones_bf[:, :], fb[st][:, :],
                    start=(c == 0), stop=(c == 31), skip_group_check=True),
                    waits=wl)
                pe_sf[c] = OP("tensor", lambda e, c=c, st=st: e.matmul(
                    bank(5, 0, 1), ones_bf[:, :], sf[st][:, :],
                    start=(c == 0), stop=(c == 31), skip_group_check=True),
                    sem="pe", waits=[("act", act_sf[c])])
            d_mr_prev = ln_strips(512, 2048, 2560, pe_sf[31])
            wl = [("dve", d_mr_prev)]
            if rb == 1:
                wl.append(("dve", d_apply_prev))
            OP("tensor", lambda e: e.matmul(
                bank(6), ones_f[:, :], S1[0:1, 0:512],
                start=True, stop=True, skip_group_check=True), waits=wl)
            pe_bc3 = OP("tensor", lambda e: e.matmul(
                bank(7), ones_f[:, :], S2[0:1, 0:512],
                start=True, stop=True, skip_group_check=True), "pe")
            lx_xc, lx_fc, dve_tb, pool_o, lx_o = {}, {}, {}, {}, {}
            for c in range(32):
                st = c % 2
                wl = []
                if first_sy:
                    wl += ss_fr
                    first_sy = False
                if c >= 2:
                    wl.append(("pool", pool_o[c - 2]))
                lx_xc[c] = DMA("sync", LS[c % 2], lambda st=st: x1c[st][:, :],
                               lambda c=c, rb=rb: x1_d[c][:, 512 * rb:512 * (rb + 1)],
                               waits=wl)
                wl = []
                if c >= 2:
                    wl.append(("dve", dve_tb[c - 2]))
                lx_fc[c] = DMA("sync", LS2[c % 2], lambda st=st: frc[st][:, :],
                               lambda c=c, rb=rb: fr_d[rb, c], waits=wl)
                wl = [(LS2[c % 2], lx_fc[c]), ("pe", pe_bc3)]
                if c >= 2:
                    wl.append(("dve", dve_tb[c - 2]))
                OP("vector", lambda e, st=st: e.tensor_tensor(
                    tA[st][:, :], frc[st][:, :], bank(6), ALU.mult), waits=wl)
                wl = []
                if c >= 2:
                    wl.append(("pool", pool_o[c - 2]))
                dve_tb[c] = OP("vector", lambda e, st=st: e.tensor_tensor(
                    tB[st][:, :], tA[st][:, :], bank(7), ALU.subtract), "dve", waits=wl)
                wl = [("dve", dve_tb[c]), (LS[c % 2], lx_xc[c])]
                if c >= 2:
                    wl.append(("sy", lx_o[c - 2]))
                pool_o[c] = OP("gpsimd", lambda e, st=st: e.tensor_add(
                    osb[st][:, :], tB[st][:, :], x1c[st][:, :]), "pool", waits=wl)
                lx_o[c] = DMA("sync", "sy",
                              lambda c=c, rb=rb: y_t[128 * c:128 * (c + 1), 512 * rb:512 * (rb + 1)],
                              lambda st=st: osb[st][:, :],
                              waits=[("pool", pool_o[c])])
            d_apply_prev = dve_tb[31]
        if E == "sync":
            e.wait_ge(sems["sy"], cnt["sy"])
            if dbg:
                e.wait_ge(sems["dz"], cnt["dz"])
        return cnt

    for s in SEMS:
        sems[s] = nc.semaphore(s).__enter__()
    with (
        nc.Block() as blk,
    ):
        @blk.sync
        def _(e):
            walk("sync", e)

        @blk.gpsimd
        def _(e):
            walk("gpsimd", e)

        @blk.tensor
        def _(e):
            walk("tensor", e)

        @blk.scalar
        def _(e):
            walk("scalar", e)

        @blk.vector
        def _(e):
            walk("vector", e)

    return nc


def stage_inputs(inputs):
    x = np.ascontiguousarray(np.asarray(inputs["L0_attn_x"], dtype=np.float32))
    ws = {n: np.ascontiguousarray(np.asarray(inputs[k], dtype=np.float32))
          for n, k in (("wq", "L0_attn_wq"), ("wk", "L0_attn_wk"),
                       ("wv", "L0_attn_wv"), ("wo", "L0_attn_wo"))}
    wgT = np.ascontiguousarray(np.asarray(inputs["L0_ffn_Wg"], dtype=np.float32).T)
    wuT = np.ascontiguousarray(np.asarray(inputs["L0_ffn_Wu"], dtype=np.float32).T)
    wdT = np.ascontiguousarray(np.asarray(inputs["L0_ffn_Wd"], dtype=np.float32).T)
    agt = np.ascontiguousarray(np.asarray(inputs["L0_ffn_ag"], dtype=np.float32).reshape(86, 128).T)
    aut = np.ascontiguousarray(np.asarray(inputs["L0_ffn_au"], dtype=np.float32).reshape(86, 128).T)
    adt = np.ascontiguousarray(np.asarray(inputs["L0_ffn_ad"], dtype=np.float32).reshape(32, 128).T)
    pswap = np.zeros((128, 128), np.float32)
    for i in range(64):
        pswap[2 * i, 2 * i + 1] = 1.0
        pswap[2 * i + 1, 2 * i] = 1.0
    ident = np.eye(128, dtype=np.float32)
    consts = np.zeros((128, 16), np.float32)
    consts[:, 0] = -C_EXP
    consts[:, 1] = EPS

    inv = 1.0 / 10000.0 ** (np.arange(0, 128, 2)[:64].astype(np.float32) / 128.0)
    cls = {}
    for h in (0, 1):
        pos = np.concatenate([np.arange(h * 1024, (h + 1) * 1024),
                              np.arange((1 - h) * 1024, (2 - h) * 1024)]).astype(np.float32)
        f = np.outer(inv, pos)
        ropec = np.repeat(np.cos(f), 2, axis=0).astype(np.float32)
        ropes = np.repeat(np.sin(f), 2, axis=0).astype(np.float32)
        ropes[0::2] *= -1.0
        gq = h * 1024 + np.arange(1024)
        masks = np.zeros((28, 128, 512), np.float32)
        idx = 0
        for rb in range(2):
            rows = gq[512 * rb:512 * (rb + 1)]
            for kt in KTS[rb]:
                keys = pos[128 * kt:128 * (kt + 1)]
                masks[idx] = (keys[:, None] <= rows[None, :]).astype(np.float32)
                idx += 1
        cls[h] = (ropec, ropes, masks)

    in_maps = []
    for c in range(8):
        b, h = c // 2, c % 2
        own = x[b, h * 1024:(h + 1) * 1024]
        par = x[b, (1 - h) * 1024:(2 - h) * 1024]
        xbt = np.ascontiguousarray(np.concatenate([own, par], 0).T)
        ropec, ropes, masks = cls[h]
        in_maps.append({
            "xbt": xbt, "wq": ws["wq"], "wk": ws["wk"], "wv": ws["wv"],
            "wo": ws["wo"], "wgT": wgT, "wuT": wuT, "wdT": wdT,
            "agt": agt, "aut": aut, "adt": adt,
            "ropec": ropec, "ropes": ropes, "masks": masks,
            "pswap": pswap, "ident": ident, "consts": consts,
        })
    return in_maps


def kernel(**inputs):
    nc = build(dbg=False)
    in_maps = stage_inputs(inputs)
    res = run_bass_kernel_spmd(nc, in_maps, core_ids=list(range(8)))
    out = np.empty((4, 2048, 4096), np.float32)
    for c in range(8):
        b, h = c // 2, c % 2
        out[b, h * 1024:(h + 1) * 1024, :] = res.results[c]["y_t"].T
    return out


# version 11 (final)
# speedup vs baseline: 1.1319x; 1.0260x over previous
import numpy as np
import ml_dtypes
import concourse.bass as bass
import concourse.mybir as mybir
from concourse.bass_utils import run_bass_kernel_spmd

F32 = mybir.dt.float32
BF16 = mybir.dt.bfloat16
U8 = mybir.dt.uint8
AF = mybir.ActivationFunctionType
ALU = mybir.AluOpType

DC = 32
IC = 86
C_EXP = 20.0
SM_SCALE = float(1.0 / np.sqrt(128.0))
EPS = 1e-5
RN = 1.0 / 4096.0
KTS = [[0, 1, 2, 3, 8, 9, 10, 11, 12, 13, 14, 15], list(range(16))]
QUARTS = [(0, 22), (22, 22), (44, 21), (65, 21)]
OCTS = [(8 * i, min(8, IC - 8 * i)) for i in range((IC + 7) // 8)]

SEMS = ("s0", "sx", "sx2", "w0", "w1", "u0", "u1", "la", "lb", "lc", "ld",
        "sa", "sb", "lq", "lm", "sy", "dz",
        "pe", "act", "dve", "pool", "ph")
WS = ("w0", "w1")
US = ("u0", "u1")
LS = ("la", "lb")
LS2 = ("lc", "ld")
SS = ("sa", "sb")


def _off():
    per = {}
    o = 0
    def put(name, size):
        nonlocal o
        per[name] = o
        o += (size + 31) // 32 * 32
    put("ropeC", 4096)
    put("ropeS", 4096)
    put("consts", 64)
    put("ones_bf", 32)
    put("ones_f", 512)
    put("pswap", 256)
    put("ident", 256)
    put("agt", 344)
    put("aut", 344)
    put("adt", 128)
    put("S1", 8192)
    put("S2", 8192)
    put("att1", 2048)
    ab = (o + 31) // 32 * 32
    ar = {
        "sq0": 0, "sq1": 4096,
        "r_sb": 172032, "mr_sb": 176128,
        "LN1F": 40960,
        "wsl0": 0, "wsl1": 8192,
        "ksb0": 16384, "ksb1": 20480,
        "t1b": 24576, "t2b": 28672,
        "ko0": 32768, "ko1": 36864,
        "qsb": 0,
        "mkb": 32768,
        "kb0": 49152, "kb1": 53248,
        "vb0": 57344, "vb1": 61440,
        "pt0": 65536, "pt1": 66560,
        "pt2": 73728, "pt3": 74752,
        "pm0": 67584, "pm1": 68608,
        "pm2": 75776, "pm3": 76800,
        "ob0": 69632, "ob1": 71680,
        "OT": 106496,
        "ars0": 16384, "ars1": 20480,
        "arB": 32768,
        "sqa0": 28672, "sqa1": 30720,
        "xs0": 98304, "xs1": 102400,
        "t6a0": 16384, "t6a1": 20480,
        "t6b0": 24576, "t6b1": 28672,
        "x1s0": 0, "x1s1": 4096,
        "r2sb": 176128, "mr2sb": 178176,
        "sq60": 172032, "sq61": 174080,
        "XH1": 106496,
        "t7d": 180224, "t7p": 180224,
        "LN2": 40960,
        "wu0": 16384, "wu1": 24576,
        "silu0": 32768, "silu1": 34816,
        "up0": 36864, "up1": 38912,
        "h80": 106496, "h81": 108544,
        "hq0": 0, "hq1": 45056,
        "wd0": 90112, "wd1": 98304,
        "frs0": 106496, "frs1": 108544,
        "fb0": 0, "fb1": 1024,
        "sf0": 2048, "sf1": 3072,
        "x1c0": 4096, "x1c1": 6144,
        "frc0": 8192, "frc1": 10240,
        "tA0": 12288, "tA1": 14336,
        "tB0": 16384, "tB1": 18432,
        "os0": 20480, "os1": 22528,
    }
    return per, ab, ar


def build(dbg=False):
    nc = bass.Bass("TRN2", target_bir_lowering=False)
    ik = "ExternalOutput" if dbg else "Internal"

    xbt = nc.dram_tensor("xbt", [4096, 2048], F32, kind="ExternalInput").ap()
    wq = nc.dram_tensor("wq", [4096, 4096], F32, kind="ExternalInput").ap()
    wk = nc.dram_tensor("wk", [4096, 4096], F32, kind="ExternalInput").ap()
    wv = nc.dram_tensor("wv", [4096, 4096], F32, kind="ExternalInput").ap()
    wo = nc.dram_tensor("wo", [4096, 4096], F32, kind="ExternalInput").ap()
    wgT = nc.dram_tensor("wgT", [4096, 11008], F32, kind="ExternalInput").ap()
    wuT = nc.dram_tensor("wuT", [4096, 11008], F32, kind="ExternalInput").ap()
    wdT = nc.dram_tensor("wdT", [11008, 4096], F32, kind="ExternalInput").ap()
    agt_d = nc.dram_tensor("agt", [128, 86], F32, kind="ExternalInput").ap()
    aut_d = nc.dram_tensor("aut", [128, 86], F32, kind="ExternalInput").ap()
    adt_d = nc.dram_tensor("adt", [128, 32], F32, kind="ExternalInput").ap()
    ropec_d = nc.dram_tensor("ropec", [128, 2048], F32, kind="ExternalInput").ap()
    ropes_d = nc.dram_tensor("ropes", [128, 2048], F32, kind="ExternalInput").ap()
    masks_d = nc.dram_tensor("masks", [28, 128, 512], F32, kind="ExternalInput").ap()
    pswap_d = nc.dram_tensor("pswap", [128, 128], F32, kind="ExternalInput").ap()
    ident_d = nc.dram_tensor("ident", [128, 128], F32, kind="ExternalInput").ap()
    consts_d = nc.dram_tensor("consts", [128, 16], F32, kind="ExternalInput").ap()
    y_t = nc.dram_tensor("y_t", [4096, 1024], F32, kind="ExternalOutput").ap()

    q_d = nc.dram_tensor("q_d", [32, 128, 1024], BF16, kind=ik).ap()
    k_d = nc.dram_tensor("k_d", [32, 128, 2048], BF16, kind=ik).ap()
    v4_d = nc.dram_tensor("v4_d", [32, 128, 16, 128], BF16, kind=ik).ap()
    x1_d = nc.dram_tensor("x1_d", [32, 128, 1024], F32, kind=ik).ap()
    h_d = nc.dram_tensor("h_d", [2, 86, 128, 512], BF16, kind=ik).ap()
    fr_d = nc.dram_tensor("fr_d", [2, 32, 128, 512], F32, kind=ik).ap()
    if dbg:
        dbg_ln1 = nc.dram_tensor("dbg_ln1", [32, 128, 2048], BF16, kind="ExternalOutput").ap()
        dbg_ot = nc.dram_tensor("dbg_ot", [32, 128, 1024], BF16, kind="ExternalOutput").ap()
        dbg_ln2 = nc.dram_tensor("dbg_ln2", [32, 128, 1024], BF16, kind="ExternalOutput").ap()
        dbg_rsb = nc.dram_tensor("dbg_rsb", [128, 4096], BF16, kind="ExternalOutput").ap()

    per, ab, ar = _off()
    space = nc.alloc_sbuf_tensor("space", [128, nc.sbuf_bytes_remaining - 64], U8)
    A = nc.lookup_mloc(space).addr

    def at(name, shape, dtype, off):
        return nc.alloc_sbuf_tensor_at(name, shape, dtype, offset=A + off)

    ropeC = at("ropeC", [128, 2048], BF16, per["ropeC"])
    ropeS = at("ropeS", [128, 2048], BF16, per["ropeS"])
    consts = at("consts", [128, 16], F32, per["consts"])
    ones_bf = at("ones_bf", [128, 1], BF16, per["ones_bf"])
    ones_f = at("ones_f", [1, 128], F32, per["ones_f"])
    pswap = at("pswap", [128, 128], BF16, per["pswap"])
    ident = at("ident", [128, 128], BF16, per["ident"])
    agt = at("agt_s", [128, 86], F32, per["agt"])
    aut = at("aut_s", [128, 86], F32, per["aut"])
    adt = at("adt_s", [128, 32], F32, per["adt"])
    S1 = at("S1", [128, 2048], F32, per["S1"])
    S2 = at("S2", [128, 2048], F32, per["S2"])
    att1 = at("att1", [128, 512], F32, per["att1"])

    sq = [at("sq0", [128, 2048], BF16, ab + ar["sq0"]),
          at("sq1", [128, 2048], BF16, ab + ar["sq1"])]
    r_sb = at("r_sb", [128, 2048], BF16, ab + ar["r_sb"])
    mr_sb = at("mr_sb", [128, 2048], BF16, ab + ar["mr_sb"])
    LN1F = at("LN1F", [128, 32, 2048], BF16, ab + ar["LN1F"])
    wsl = [at("wsl0", [128, 32, 128], BF16, ab + ar["wsl0"]),
           at("wsl1", [128, 32, 128], BF16, ab + ar["wsl1"])]
    ksb = [at("ksb0", [128, 2048], BF16, ab + ar["ksb0"]),
           at("ksb1", [128, 2048], BF16, ab + ar["ksb1"])]
    t1b = at("t1b", [128, 2048], BF16, ab + ar["t1b"])
    t2b = at("t2b", [128, 2048], BF16, ab + ar["t2b"])
    ko = [at("ko0", [128, 2048], BF16, ab + ar["ko0"]),
          at("ko1", [128, 2048], BF16, ab + ar["ko1"])]
    qsb = at("qsb", [128, 32, 512], BF16, ab + ar["qsb"])
    mkb = at("mkb", [128, 16, 512], BF16, ab + ar["mkb"])
    kb = [at("kb0", [128, 2048], BF16, ab + ar["kb0"]),
          at("kb1", [128, 2048], BF16, ab + ar["kb1"])]
    vb = [at("vb0", [128, 16, 128], BF16, ab + ar["vb0"]),
          at("vb1", [128, 16, 128], BF16, ab + ar["vb1"])]
    pt = [at(f"pt{i}", [128, 512], BF16, ab + ar[f"pt{i}"]) for i in range(4)]
    pm = [at(f"pm{i}", [128, 512], BF16, ab + ar[f"pm{i}"]) for i in range(4)]
    ob = [at("ob0", [128, 512], F32, ab + ar["ob0"]),
          at("ob1", [128, 512], F32, ab + ar["ob1"])]
    OT = at("OT", [128, 32, 1024], BF16, ab + ar["OT"])
    ars = [at("ars0", [128, 1024], F32, ab + ar["ars0"]),
           at("ars1", [128, 1024], F32, ab + ar["ars1"])]
    arB = at("arB", [128, 32, 1024], BF16, ab + ar["arB"])
    sqa = [at("sqa0", [128, 1024], BF16, ab + ar["sqa0"]),
           at("sqa1", [128, 1024], BF16, ab + ar["sqa1"])]
    xs = [at("xs0", [128, 1024], F32, ab + ar["xs0"]),
          at("xs1", [128, 1024], F32, ab + ar["xs1"])]
    t6a = [at("t6a0", [128, 1024], F32, ab + ar["t6a0"]),
           at("t6a1", [128, 1024], F32, ab + ar["t6a1"])]
    t6b = [at("t6b0", [128, 1024], F32, ab + ar["t6b0"]),
           at("t6b1", [128, 1024], F32, ab + ar["t6b1"])]
    x1s = [at("x1s0", [128, 1024], F32, ab + ar["x1s0"]),
           at("x1s1", [128, 1024], F32, ab + ar["x1s1"])]
    r2sb = at("r2sb", [128, 1024], BF16, ab + ar["r2sb"])
    mr2sb = at("mr2sb", [128, 1024], BF16, ab + ar["mr2sb"])
    sq6 = [at("sq60", [128, 1024], BF16, ab + ar["sq60"]),
           at("sq61", [128, 1024], BF16, ab + ar["sq61"])]
    XH1 = at("XH1", [128, 32, 1024], BF16, ab + ar["XH1"])
    t7d = at("t7d", [128, 1024], BF16, ab + ar["t7d"])
    t7p = at("t7p", [128, 1024], BF16, ab + ar["t7p"])
    LN2 = at("LN2", [128, 32, 1024], BF16, ab + ar["LN2"])
    wu = [at("wu0", [128, 32, 128], BF16, ab + ar["wu0"]),
          at("wu1", [128, 32, 128], BF16, ab + ar["wu1"])]
    silu = [at("silu0", [128, 1024], BF16, ab + ar["silu0"]),
            at("silu1", [128, 1024], BF16, ab + ar["silu1"])]
    up = [at("up0", [128, 1024], BF16, ab + ar["up0"]),
          at("up1", [128, 1024], BF16, ab + ar["up1"])]
    h8 = [at("h80", [128, 1024], BF16, ab + ar["h80"]),
          at("h81", [128, 1024], BF16, ab + ar["h81"])]
    hq = [at("hq0", [128, 22, 1024], BF16, ab + ar["hq0"]),
          at("hq1", [128, 22, 1024], BF16, ab + ar["hq1"])]
    wd = [at("wd0", [128, 8, 512], BF16, ab + ar["wd0"]),
          at("wd1", [128, 8, 512], BF16, ab + ar["wd1"])]
    frs = [at("frs0", [128, 512], F32, ab + ar["frs0"]),
           at("frs1", [128, 512], F32, ab + ar["frs1"])]
    fb = [at("fb0", [128, 512], BF16, ab + ar["fb0"]),
          at("fb1", [128, 512], BF16, ab + ar["fb1"])]
    sf = [at("sf0", [128, 512], BF16, ab + ar["sf0"]),
          at("sf1", [128, 512], BF16, ab + ar["sf1"])]
    x1c = [at("x1c0", [128, 512], F32, ab + ar["x1c0"]),
           at("x1c1", [128, 512], F32, ab + ar["x1c1"])]
    frc = [at("frc0", [128, 512], F32, ab + ar["frc0"]),
           at("frc1", [128, 512], F32, ab + ar["frc1"])]
    tA = [at("tA0", [128, 512], F32, ab + ar["tA0"]),
          at("tA1", [128, 512], F32, ab + ar["tA1"])]
    tB = [at("tB0", [128, 512], F32, ab + ar["tB0"]),
          at("tB1", [128, 512], F32, ab + ar["tB1"])]
    osb = [at("os0", [128, 512], F32, ab + ar["os0"]),
           at("os1", [128, 512], F32, ab + ar["os1"])]

    ps_h = nc.alloc_psum_tensor("psall", [128, 4096], F32)

    def bank(b, p0=0, pn=128, n=512):
        return ps_h[p0:p0 + pn, 512 * b:512 * b + n]

    sems = {}

    def walk(E, e):
        cnt = {k: 0 for k in SEMS}

        def W(sem, val):
            if val and val > 0:
                e.wait_ge(sems[sem], val)

        def DMA(eng, sem, outf, inf, waits=()):
            cnt[sem] += 16
            if eng == E:
                for (s, v) in waits:
                    W(s, v)
                e.dma_start(out=outf(), in_=inf()).then_inc(sems[sem], 16)
            return cnt[sem]

        def OP(eng, fn, sem=None, waits=()):
            if sem:
                cnt[sem] += 1
            if eng == E:
                for (s, v) in waits:
                    W(s, v)
                ins = fn(e)
                if sem:
                    ins.then_inc(sems[sem], 1)
            return cnt[sem] if sem else None

        def BAR():
            cnt["ph"] += 5
            e.nop().then_inc(sems["ph"], 1)
            e.wait_ge(sems["ph"], cnt["ph"])

        def SSFULL():
            return [("sa", cnt["sa"]), ("sb", cnt["sb"])]

        DMA("gpsimd", "s0", lambda: ropeC[:, :], lambda: ropec_d)
        DMA("gpsimd", "s0", lambda: ropeS[:, :], lambda: ropes_d)
        DMA("gpsimd", "s0", lambda: consts[:, :], lambda: consts_d)
        DMA("gpsimd", "s0", lambda: pswap[:, :], lambda: pswap_d)
        DMA("gpsimd", "s0", lambda: ident[:, :], lambda: ident_d)
        DMA("gpsimd", "s0", lambda: agt[:, :], lambda: agt_d)
        DMA("gpsimd", "s0", lambda: aut[:, :], lambda: aut_d)
        DMA("gpsimd", "s0", lambda: adt[:, :], lambda: adt_d)
        OP("vector", lambda e: e.memset(ones_bf[:, :], 1.0), "dve")
        OP("vector", lambda e: e.memset(ones_f[:, :], 1.0), "dve")
        if E == "gpsimd":
            e.wait_ge(sems["s0"], cnt["s0"])
        BAR()

        XSEM = ("sx", "sx2", "lq", "lm")
        xg = {}
        act_sq = []
        pe_chunk = []
        pe0 = cnt["pe"]
        for g in range(8):
            a, b = 4 * g, 4 * g + 4
            wl = []
            if g >= 4:
                wl.append(("pe", pe0 + 4 * (g - 4) + 4))
            xg[g] = DMA("gpsimd", XSEM[g % 4],
                        lambda a=a, b=b: LN1F[:, a:b, :],
                        lambda a=a, b=b: xbt[128 * a:128 * b, :].rearrange(
                            "(c p) n -> p c n", p=128), waits=wl)
        for c in range(32):
            s_ = sq[c % 2]
            g = c // 4
            gw = (XSEM[g % 4], xg[g])
            wl = [gw, ("dve", 2)] if c % 4 == 0 else []
            if c >= 2:
                wl.append(("pe", pe_chunk[c - 2]))
            act_sq.append(OP("scalar", lambda e, c=c, s_=s_: e.activation(
                s_[:, :], LN1F[:, c, :], AF.Square), "act", waits=wl))
            for sl in range(4):
                OP("tensor", lambda e, c=c, sl=sl: e.matmul(
                    bank(sl, 0, 1), ones_bf[:, :], LN1F[:, c, 512 * sl:512 * (sl + 1)],
                    start=(c == 0), stop=(c == 31), skip_group_check=True),
                    waits=[gw, ("dve", 2)] if (sl == 0 and c % 4 == 0) else ())
            for sl in range(4):
                v = OP("tensor", lambda e, c=c, sl=sl, s_=s_: e.matmul(
                    bank(4 + sl, 0, 1), ones_bf[:, :], s_[:, 512 * sl:512 * (sl + 1)],
                    start=(c == 0), stop=(c == 31), skip_group_check=True),
                    sem="pe" if sl == 3 else None,
                    waits=[("act", act_sq[c])] if sl == 0 else ())
            pe_chunk.append(v)

        def ln_strips(T, cs, cq, pe_wait):
            OP("vector", lambda e: e.tensor_scalar(
                ps_h[0:1, cs:cs + T], ps_h[0:1, cs:cs + T], RN, None, ALU.mult),
                "dve", waits=[("pe", pe_wait)])
            OP("vector", lambda e: e.tensor_scalar(
                ps_h[0:1, cq:cq + T], ps_h[0:1, cq:cq + T], RN, None, ALU.mult), "dve")
            OP("vector", lambda e: e.tensor_copy(
                S2[0:1, 0:T], ps_h[0:1, cs:cs + T]), "dve")
            OP("vector", lambda e: e.tensor_tensor(
                S1[0:1, 0:T], S2[0:1, 0:T], S2[0:1, 0:T], ALU.mult), "dve")
            d4 = OP("vector", lambda e: e.tensor_tensor(
                ps_h[0:1, cq:cq + T], ps_h[0:1, cq:cq + T], S1[0:1, 0:T], ALU.subtract), "dve")
            OP("scalar", lambda e: e.activation(
                ps_h[0:1, cq:cq + T], ps_h[0:1, cq:cq + T], AF.Ln,
                bias=consts[0:1, 1:2]), "act", waits=[("dve", d4)])
            a2 = OP("scalar", lambda e: e.activation(
                S1[0:1, 0:T], ps_h[0:1, cq:cq + T], AF.Exp, scale=-0.5), "act")
            return OP("vector", lambda e: e.tensor_tensor(
                S2[0:1, 0:T], S2[0:1, 0:T], S1[0:1, 0:T], ALU.mult),
                "dve", waits=[("act", a2)])

        d_mr = ln_strips(2048, 0, 2048, pe_chunk[31])
        for sl in range(4):
            OP("tensor", lambda e, sl=sl: e.matmul(
                bank(sl), ones_f[:, :], S1[0:1, 512 * sl:512 * (sl + 1)],
                start=True, stop=True, skip_group_check=True),
                waits=[("dve", d_mr)] if sl == 0 else ())
        pe_bc = None
        for sl in range(4):
            pe_bc = OP("tensor", lambda e, sl=sl: e.matmul(
                bank(4 + sl), ones_f[:, :], S2[0:1, 512 * sl:512 * (sl + 1)],
                start=True, stop=True, skip_group_check=True),
                sem="pe" if sl == 3 else None)
        OP("scalar", lambda e: e.copy(r_sb[:, :], ps_h[:, 0:2048]),
           "act", waits=[("pe", pe_bc)])
        a_mr = OP("scalar", lambda e: e.copy(mr_sb[:, :], ps_h[:, 2048:4096]), "act")
        if dbg:
            DMA("sync", "dz", lambda: dbg_rsb[:, 0:2048], lambda: r_sb[:, :],
                waits=[("act", a_mr)])
            DMA("sync", "dz", lambda: dbg_rsb[:, 2048:4096], lambda: mr_sb[:, :])
        dz_rsb = cnt["dz"]

        d_sub = {}
        for c in range(32):
            tb = ksb[c % 2]
            wl = [("act", a_mr)] if c == 0 else []
            OP("vector", lambda e, c=c, tb=tb: e.tensor_tensor(
                tb[:, :], LN1F[:, c, :], r_sb[:, :], ALU.mult), waits=wl)
            d_sub[c] = OP("vector", lambda e, c=c, tb=tb: e.tensor_tensor(
                LN1F[:, c, :], tb[:, :], mr_sb[:, :], ALU.subtract), "dve")
        if dbg:
            for c in range(32):
                DMA("sync", "dz", lambda c=c: dbg_ln1[c], lambda c=c: LN1F[:, c, :],
                    waits=[("dve", cnt["dve"])] if c == 0 else ())
        dz_ln1 = cnt["dz"]

        def proj(w_src, nslab, out_d, rope, carry=None, carry_ksb=None,
                 carry_ko=None, carry_w=None, first_w_extra=(), mk_waits=None):
            TT = nslab * 512
            lw_s, pe_acc, pe_swap, act_ev, dve_ro, lx_ko = {}, {}, {}, {}, {}, {}
            tr_pe, av4 = {}, {}

            def emit_tr(mm):
                st_ = mm % 2
                v = None
                for kt in range(16):
                    v = OP("tensor", lambda e, st_=st_, kt=kt: e.matmul(
                        ps_h[:, 2048 * st_ + 128 * kt:2048 * st_ + 128 * (kt + 1)],
                        ksb[st_][:, 128 * kt:128 * (kt + 1)], ident[:, :],
                        start=True, stop=True, skip_group_check=True),
                        sem="pe" if kt == 15 else None,
                        waits=[("act", act_ev[mm])] if kt == 0 else ())
                tr_pe[mm] = v
                av4[mm] = OP("scalar", lambda e, st_=st_: e.copy(
                    ksb[st_][:, :], ps_h[:, 2048 * st_:2048 * (st_ + 1)]),
                    "act", waits=[("pe", tr_pe[mm])])
                lx_ko[mm] = DMA("sync", SS[mm % 2],
                                lambda mm=mm: out_d[mm].rearrange("p a b -> p (a b)"),
                                lambda st_=st_: ksb[st_][:, :],
                                waits=[("act", av4[mm])])

            def emit_swap(m):
                pst = m % 2
                base = 4 * pst + (2 if nslab == 2 else 0)
                wl = [("act", act_ev[m])]
                if m >= 2:
                    wl.append(("dve", dve_ro[m - 2]))
                v = None
                for j in range(nslab):
                    v = OP("tensor", lambda e, pst=pst, j=j, base=base: e.matmul(
                        bank(base + j), pswap[:, :],
                        ksb[pst][:, 512 * j:512 * (j + 1)],
                        start=True, stop=True, skip_group_check=True),
                        sem="pe" if j == nslab - 1 else None,
                        waits=wl if j == 0 else ())
                pe_swap[m] = v

            def emit_rope(m):
                st = m % 2
                base = 4 * st + (2 if nslab == 2 else 0)
                wl = [("act", act_ev[m]), ("pe", pe_swap[m])]
                if m < 2 and carry_ko is not None:
                    wl.append(carry_ko)
                if m >= 2:
                    wl.append((SS[m % 2], lx_ko[m - 2]))
                OP("vector", lambda e, st=st: e.tensor_tensor(
                    t1b[:, 0:TT], ksb[st][:, 0:TT], ropeC[:, 0:TT], ALU.mult),
                    waits=wl)
                OP("vector", lambda e, base=base: e.tensor_tensor(
                    t2b[:, 0:TT], ps_h[:, 512 * base:512 * base + TT],
                    ropeS[:, 0:TT], ALU.mult))
                dve_ro[m] = OP("vector", lambda e, st=st: e.tensor_add(
                    ko[st][:, 0:TT], t1b[:, 0:TT], t2b[:, 0:TT]), "dve")
                lx_ko[m] = DMA("sync", SS[m % 2], lambda m=m: out_d[m],
                               lambda st=st: ko[st][:, 0:TT],
                               waits=[("dve", dve_ro[m])])

            for m in range(32):
                st = m % 2
                wl = list(first_w_extra) if m == 0 else []
                if m < 2 and carry_w is not None:
                    wl.append(carry_w)
                if m >= 2:
                    wl = [("pe", pe_acc[m - 2])]
                lw_s[m] = DMA(
                    "gpsimd", WS[m % 2],
                    lambda st=st: wsl[st][:, :, :],
                    lambda m=m: w_src.rearrange("(c p) n -> p c n", p=128)[:, :, 128 * m:128 * (m + 1)],
                    waits=wl)
                for k in range(32):
                    if k == 8 and m >= 1:
                        (emit_swap if rope else emit_tr)(m - 1)
                    wl = []
                    if k == 0:
                        wl = [(WS[m % 2], lw_s[m])]
                        if m < 2 and carry is not None:
                            wl.append(carry)
                        if m >= 2:
                            if rope:
                                wl.append(("dve", dve_ro[m - 2]))
                            else:
                                wl.append(("act", av4[m - 2]))
                    if m == 0 and mk_waits is not None:
                        wl = wl + mk_waits(k)
                    v = None
                    for j in range(nslab):
                        v = OP("tensor", lambda e, m=m, k=k, j=j: e.matmul(
                            bank(4 * (m % 2) + j), wsl[m % 2][:, k, :],
                            LN1F[:, k, 512 * j:512 * (j + 1)],
                            start=(k == 0), stop=(k == 31), skip_group_check=True),
                            sem="pe" if (k == 31 and j == nslab - 1) else None,
                            waits=wl if (j == 0 and wl) else ())
                pe_acc[m] = v
                wl = [("pe", pe_acc[m])]
                if m < 2 and carry_ksb is not None:
                    wl.append(carry_ksb)
                if m >= 2:
                    if rope:
                        wl.append(("pe", pe_swap[m - 2]))
                        wl.append(("dve", dve_ro[m - 2]))
                    else:
                        wl.append((SS[m % 2], lx_ko[m - 2]))
                act_ev[m] = OP("scalar", lambda e, m=m: e.copy(
                    ksb[m % 2][:, 0:TT], ps_h[:, 2048 * (m % 2):2048 * (m % 2) + TT]),
                    "act", waits=wl)
                if rope and m >= 1:
                    emit_rope(m - 1)
            if rope:
                emit_swap(31)
                emit_rope(31)
                g_ps = ("dve", dve_ro[31])
                g_ksb = ("dve", dve_ro[31])
                g_ko = (SS[31 % 2], lx_ko[31])
            else:
                emit_tr(31)
                g_ps = ("act", av4[31])
                g_ksb = (SS[31 % 2], lx_ko[31])
                g_ko = None
            return lx_ko, (g_ps, g_ksb, g_ko, ("pe", pe_acc[31]))

        first_w = [("pe", pe_chunk[31])]
        if dbg:
            first_w.append(("dz", dz_rsb))
        lx_k, (k_ps, k_ksb, k_ko, k_w) = proj(wk, 4, k_d, True,
                                              first_w_extra=first_w,
                                              mk_waits=lambda k: [("dve", d_sub[k])])
        lx_v, (v_ps, v_ksb, v_ko, v_w) = proj(wv, 4, v4_d, False,
                                              carry=k_ps, carry_ksb=k_ksb,
                                              carry_w=k_w)
        lx_q, _ = proj(wq, 2, q_d, True,
                       carry=v_ps, carry_ksb=v_ksb, carry_ko=k_ko,
                       carry_w=v_w)
        BAR()

        s_last = {}
        pv_last = {}
        recip_t = {}
        oev_t = {}
        bcast_t = {}
        norm_t = {}
        mask_last_slot = [None] * 4
        lpv_last_slot = [None] * 4
        exp_last_bank = [None] * 4
        SB = ([2, 3, 1, 5], [2, 3, 0, 4])
        rb_of = {}

        b1_war = [None]

        def finish_head(gh):
            rb = rb_of[gh]
            h = gh % 32
            wl = [("dve", recip_t[gh])]
            if gh >= 1:
                wl.append(("dve", norm_t.get(gh - 1)))
            if gh == 32 and b1_war[0] is not None:
                wl.append(("act", b1_war[0]))
            if dbg and gh == 0:
                wl.append(("dz", dz_ln1))
            bcast_t[gh] = OP("tensor", lambda e, rb=rb: e.matmul(
                bank(rb), ones_f[:, :], att1[0:1, 0:512],
                start=True, stop=True, skip_group_check=True), "pe", waits=wl)
            norm_t[gh] = OP("vector", lambda e, rb=rb, h=h, gh=gh: e.tensor_tensor(
                OT[:, h, 512 * rb:512 * (rb + 1)], ob[gh % 2][:, :],
                bank(rb), ALU.mult), "dve",
                waits=[("act", oev_t[gh]), ("pe", bcast_t[gh])])

        lx_kb = [None, None]
        lx_vb = [None, None]
        for rb in range(2):
            kts = KTS[rb]
            n = len(kts)
            wl = SSFULL() if rb == 0 else []
            if rb == 1:
                wl.append(("pe", s_last[31]))
            lx_q4 = DMA("sync", "lq", lambda: qsb[:, :, :],
                        lambda rb=rb: q_d.rearrange("h p t -> p h t")[:, :, 512 * rb:512 * (rb + 1)],
                        waits=wl)
            i0 = 0 if rb == 0 else 12
            wl = SSFULL() if rb == 0 else []
            if rb == 1:
                wl.append(("dve", mask_last_slot[0]))
                wl.append(("dve", mask_last_slot[1]))
            lx_m = DMA("gpsimd", "lm", lambda n=n: mkb[:, 0:n, :],
                       lambda i0=i0, n=n: masks_d[i0:i0 + n].rearrange("n p t -> p n t"),
                       waits=wl)
            for h in range(32):
                gh = rb * 32 + h
                rb_of[gh] = rb
                wl = []
                if gh >= 2:
                    wl.append(("pe", s_last[gh - 2]))
                lx_kb[gh % 2] = DMA("sync", LS[gh % 2], lambda gh=gh: kb[gh % 2][:, :],
                                    lambda h=h: k_d[h], waits=wl)
                wl = []
                if gh >= 2:
                    wl.append(("pe", pv_last[gh - 2]))
                lx_vb[gh % 2] = DMA("sync", LS2[gh % 2],
                                    lambda gh=gh: vb[gh % 2][:, :, :],
                                    lambda h=h: v4_d[h],
                                    waits=wl)
                s_t = {}

                def S_mm(i):
                    kt = kts[i]
                    sl = i % 4
                    wl = [(LS[gh % 2], lx_kb[gh % 2]), ("lq", lx_q4)]
                    if gh == 32 and i == 2:
                        wl.append(("dve", norm_t[31]))
                    elif gh == 32 and i == 3:
                        wl.append(("act", oev_t[31]))
                    elif exp_last_bank[sl] is not None:
                        wl.append(("act", exp_last_bank[sl]))
                    s_t[i] = OP("tensor", lambda e, kt=kt, sl=sl, h=h, gh=gh, rb=rb: e.matmul(
                        bank(SB[rb][sl]), kb[gh % 2][:, 128 * kt:128 * (kt + 1)],
                        qsb[:, h, :], start=True, stop=True, skip_group_check=True),
                        "pe", waits=wl)

                S_mm(0)
                if n > 1:
                    S_mm(1)
                if gh >= 1:
                    finish_head(gh - 1)
                for ii in range(2, min(4, n)):
                    S_mm(ii)
                for i in range(n):
                    sl = i % 4
                    wl = [("pe", s_t[i])]
                    if mask_last_slot[sl] is not None:
                        wl.append(("dve", mask_last_slot[sl]))
                    a_e = OP("scalar", lambda e, sl=sl, rb=rb: e.activation(
                        pt[sl][:, :], bank(SB[rb][sl]), AF.Exp,
                        bias=consts[:, 0:1], scale=SM_SCALE), "act", waits=wl)
                    exp_last_bank[sl] = a_e
                    wl = [("act", a_e), ("lm", lx_m)]
                    if lpv_last_slot[sl] is not None:
                        wl.append(("pe", lpv_last_slot[sl]))
                    d_m = OP("vector", lambda e, i=i, sl=sl: e.tensor_tensor(
                        pm[sl][:, :], pt[sl][:, :],
                        mkb[:, i, :], ALU.mult), "dve", waits=wl)
                    mask_last_slot[sl] = d_m
                    if i + 4 < n:
                        S_mm(i + 4)
                    wl = [("dve", d_m)]
                    if i == 0:
                        wl.append((LS2[gh % 2], lx_vb[gh % 2]))
                        if gh >= 1:
                            wl.append(("dve", recip_t[gh - 1]))
                            wl.append(("act", oev_t[gh - 1]))
                    OP("tensor", lambda e, rb=rb, i=i, n=n, sl=sl: e.matmul(
                        bank(6 + rb, 0, 1), ones_bf[:, :], pm[sl][:, :],
                        start=(i == 0), stop=(i == n - 1), skip_group_check=True),
                        waits=wl)
                    kt = kts[i]
                    v = OP("tensor", lambda e, rb=rb, i=i, n=n, kt=kt, gh=gh, sl=sl: e.matmul(
                        bank(4 + rb), vb[gh % 2][:, kt, :], pm[sl][:, :],
                        start=(i == 0), stop=(i == n - 1), skip_group_check=True), "pe")
                    lpv_last_slot[sl] = v
                pv_last[gh] = v
                s_last[gh] = s_t[n - 1]
                wl = [("pe", pv_last[gh])]
                if gh >= 1:
                    wl.append(("pe", bcast_t.get(gh - 1)))
                recip_t[gh] = OP("vector", lambda e, rb=rb: e.reciprocal(
                    att1[0:1, 0:512], bank(6 + rb, 0, 1)), "dve", waits=wl)
                wl = [("pe", pv_last[gh])]
                if gh >= 2:
                    wl.append(("dve", norm_t.get(gh - 2)))
                oev_t[gh] = OP("scalar", lambda e, rb=rb, gh=gh: e.copy(
                    ob[gh % 2][:, :], bank(4 + rb)), "act", waits=wl)
            if rb == 0:
                b1_war[0] = exp_last_bank[2]
        finish_head(63)
        if dbg:
            DMA("sync", "dz", lambda: dbg_ot.rearrange("h p t -> p h t"), lambda: OT[:, :, :],
                waits=[("dve", norm_t[63])])
        dz_ot = cnt["dz"]
        BAR()

        lw_s, pe_acc, act_ev, dve_cs, act_sq5, pe_st, lx_ar = {}, {}, {}, {}, {}, {}, {}
        for m in range(32):
            st = m % 2
            wl = []
            if m >= 2:
                wl.append(("pe", pe_acc[m - 2]))
            lw_s[m] = DMA(
                "gpsimd", WS[m % 2], lambda st=st: wsl[st][:, :, :],
                lambda m=m: wo.rearrange("(c p) n -> p c n", p=128)[:, :, 128 * m:128 * (m + 1)],
                waits=wl)
            v = None
            for k in range(32):
                wl = []
                if k == 0:
                    wl = [(WS[m % 2], lw_s[m])]
                    if m >= 2:
                        wl.append(("act", act_ev[m - 2]))
                for j in range(2):
                    v = OP("tensor", lambda e, m=m, k=k, j=j: e.matmul(
                        bank(2 * (m % 2) + j), wsl[m % 2][:, k, :],
                        OT[:, k, 512 * j:512 * (j + 1)],
                        start=(k == 0), stop=(k == 31), skip_group_check=True),
                        sem="pe" if (k == 31 and j == 1) else None,
                        waits=wl if (j == 0 and k == 0) else ())
            pe_acc[m] = v
            wl = [("pe", pe_acc[m])]
            if m >= 2:
                wl.append(("dve", dve_cs[m - 2]))
            act_ev[m] = OP("scalar", lambda e, m=m: e.copy(
                ars[m % 2][:, :], ps_h[:, 1024 * (m % 2):1024 * (m % 2) + 1024]),
                "act", waits=wl)
            wl = [("act", act_ev[m])]
            if m >= 2:
                wl.append(("pe", pe_st[m - 2]))
            dve_cs[m] = OP("vector", lambda e, m=m: e.tensor_copy(
                arB[:, m, :], ars[m % 2][:, :]), "dve", waits=wl)
            act_sq5[m] = OP("scalar", lambda e, m=m: e.activation(
                sqa[m % 2][:, :], arB[:, m, :], AF.Square), "act",
                waits=[("dve", dve_cs[m])])
            for sl2 in range(2):
                OP("tensor", lambda e, m=m, sl2=sl2: e.matmul(
                    bank(4 + sl2, 0, 1), ones_bf[:, :],
                    arB[:, m, 512 * sl2:512 * (sl2 + 1)],
                    start=(m == 0), stop=(m == 31), skip_group_check=True),
                    waits=[("dve", dve_cs[m])] if sl2 == 0 else ())
            for sl2 in range(2):
                v = OP("tensor", lambda e, m=m, sl2=sl2: e.matmul(
                    bank(6 + sl2, 0, 1), ones_bf[:, :],
                    sqa[m % 2][:, 512 * sl2:512 * (sl2 + 1)],
                    start=(m == 0), stop=(m == 31), skip_group_check=True),
                    sem="pe" if sl2 == 1 else None,
                    waits=[("act", act_sq5[m])] if sl2 == 0 else ())
            pe_st[m] = v
        d_mr = ln_strips(1024, 2048, 3072, pe_st[31])
        for sl2 in range(2):
            OP("tensor", lambda e, sl2=sl2: e.matmul(
                bank(sl2), ones_f[:, :], S1[0:1, 512 * sl2:512 * (sl2 + 1)],
                start=True, stop=True, skip_group_check=True),
                waits=[("dve", d_mr), ("act", act_ev[31])] if sl2 == 0 else ())
        pe_bc1 = None
        for sl2 in range(2):
            pe_bc1 = OP("tensor", lambda e, sl2=sl2: e.matmul(
                bank(2 + sl2), ones_f[:, :], S2[0:1, 512 * sl2:512 * (sl2 + 1)],
                start=True, stop=True, skip_group_check=True),
                sem="pe" if sl2 == 1 else None)
        p5_pe = cnt["pe"]
        BAR()

        lx_x, dve_t2, pool_x1, act_c, act_s6, pe_st6, lx_x1 = \
            {}, {}, {}, {}, {}, {}, {}
        for c in range(32):
            st = c % 2
            wl = []
            if c >= 2:
                wl.append(("dve", pool_x1[c - 2]))
            lx_x[c] = DMA("sync", LS[c % 2], lambda st=st: xs[st][:, :],
                          lambda c=c: xbt[128 * c:128 * (c + 1), 0:1024], waits=wl)
            wl = [("pe", pe_bc1)] if c == 0 else []
            if c >= 2:
                wl.append(("dve", dve_t2[c - 2]))
            OP("vector", lambda e, c=c, st=st: e.tensor_tensor(
                t6a[st][:, :], arB[:, c, :], ps_h[:, 0:1024], ALU.mult), waits=wl)
            dve_t2[c] = OP("vector", lambda e, st=st: e.tensor_tensor(
                t6b[st][:, :], t6a[st][:, :], ps_h[:, 1024:2048], ALU.subtract),
                "dve")
            wl = [(LS[c % 2], lx_x[c])]
            if c >= 2:
                wl.append((SS[c % 2], lx_x1[c - 2]))
                wl.append(("act", act_c[c - 2]))
            pool_x1[c] = OP("vector", lambda e, st=st: e.tensor_add(
                x1s[st][:, :], t6b[st][:, :], xs[st][:, :]), "dve", waits=wl)
            lx_x1[c] = DMA("sync", SS[c % 2], lambda c=c: x1_d[c],
                           lambda st=st: x1s[st][:, :],
                           waits=[("dve", pool_x1[c])])
            wl = [("dve", pool_x1[c])]
            if dbg and c == 0:
                wl.append(("dz", dz_ot))
            act_c[c] = OP("scalar", lambda e, c=c, st=st: e.copy(
                XH1[:, c, :], x1s[st][:, :]), "act", waits=wl)
            wl = [("act", act_c[c])]
            if c >= 2:
                wl.append(("pe", pe_st6[c - 2]))
            act_s6[c] = OP("scalar", lambda e, c=c, st=st: e.activation(
                sq6[st][:, :], XH1[:, c, :], AF.Square), "act", waits=wl)
            for sl2 in range(2):
                OP("tensor", lambda e, c=c, sl2=sl2: e.matmul(
                    bank(4 + sl2, 0, 1), ones_bf[:, :],
                    XH1[:, c, 512 * sl2:512 * (sl2 + 1)],
                    start=(c == 0), stop=(c == 31), skip_group_check=True),
                    waits=[("act", act_c[c])] if sl2 == 0 else ())
            v = None
            for sl2 in range(2):
                v = OP("tensor", lambda e, c=c, st=st, sl2=sl2: e.matmul(
                    bank(6 + sl2, 0, 1), ones_bf[:, :],
                    sq6[st][:, 512 * sl2:512 * (sl2 + 1)],
                    start=(c == 0), stop=(c == 31), skip_group_check=True),
                    sem="pe" if sl2 == 1 else None,
                    waits=[("act", act_s6[c])] if sl2 == 0 else ())
            pe_st6[c] = v
        d_mr = ln_strips(1024, 2048, 3072, pe_st6[31])
        for sl2 in range(2):
            OP("tensor", lambda e, sl2=sl2: e.matmul(
                bank(4 + sl2), ones_f[:, :], S1[0:1, 512 * sl2:512 * (sl2 + 1)],
                start=True, stop=True, skip_group_check=True),
                waits=[("dve", d_mr)] if sl2 == 0 else ())
        pe_bc2 = None
        for sl2 in range(2):
            pe_bc2 = OP("tensor", lambda e, sl2=sl2: e.matmul(
                bank(6 + sl2), ones_f[:, :], S2[0:1, 512 * sl2:512 * (sl2 + 1)],
                start=True, stop=True, skip_group_check=True),
                sem="pe" if sl2 == 1 else None)
        OP("scalar", lambda e: e.copy(r2sb[:, :], ps_h[:, 2048:3072]),
           "act", waits=[("pe", pe_bc2)])
        a_mr2 = OP("scalar", lambda e: e.copy(mr2sb[:, :], ps_h[:, 3072:4096]), "act")
        p6_dve = pool_x1[31]

        d_mul, d_sub7 = {}, {}
        for c in range(32):
            wl = [("act", a_mr2)] if c == 0 else []
            d_mul[c] = OP("vector", lambda e, c=c: e.tensor_tensor(
                t7d[:, :], XH1[:, c, :], r2sb[:, :], ALU.mult), waits=wl)
            d_sub7[c] = OP("vector", lambda e, c=c: e.tensor_tensor(
                LN2[:, c, :], t7d[:, :], mr2sb[:, :], ALU.subtract), "dve")
        BAR()

        if dbg:
            for c in range(32):
                DMA("sync", "dz", lambda c=c: dbg_ln2[c], lambda c=c: LN2[:, c, :],
                    waits=[("dve", d_sub7[31])] if c == 0 else ())
        dz_ln2 = cnt["dz"]
        lw_g, lw_u, pe_g, pe_u, act_si, dve_up, dve_h, lx_h = \
            {}, {}, {}, {}, {}, {}, {}, {}
        ss_x1 = SSFULL()
        act_c31 = act_c[31]
        for mt in range(IC):
            st = mt % 2
            wl = []
            if mt < 2:
                wl.append(("act", act_c31))
                wl += ss_x1
            if mt >= 2:
                wl.append(("pe", pe_g[mt - 2]))
            lw_g[mt] = DMA(
                "gpsimd", WS[mt % 2], lambda st=st: wsl[st][:, :, :],
                lambda mt=mt: wgT.rearrange("(c p) n -> p c n", p=128)[:, :, 128 * mt:128 * (mt + 1)],
                waits=wl)
            wl = []
            if mt < 2:
                wl.append(("dve", p6_dve))
            if mt >= 2:
                wl.append(("pe", pe_u[mt - 2]))
            lw_u[mt] = DMA(
                "gpsimd", US[mt % 2], lambda st=st: wu[st][:, :, :],
                lambda mt=mt: wuT.rearrange("(c p) n -> p c n", p=128)[:, :, 128 * mt:128 * (mt + 1)],
                waits=wl)
            v = None
            for k in range(32):
                wl = []
                if k == 0:
                    wl = [(WS[mt % 2], lw_g[mt])]
                    if mt >= 2:
                        wl.append(("act", act_si[mt - 2]))
                if mt == 0:
                    wl = wl + [("dve", d_sub7[k])]
                if mt == 1 and k == 0:
                    wl.append(("act", a_mr2))
                for j in range(2):
                    v = OP("tensor", lambda e, mt=mt, k=k, j=j: e.matmul(
                        bank(4 * (mt % 2) + j), wsl[mt % 2][:, k, :],
                        LN2[:, k, 512 * j:512 * (j + 1)],
                        start=(k == 0), stop=(k == 31), skip_group_check=True),
                        sem="pe" if (k == 31 and j == 1) else None,
                        waits=wl if (j == 0 and k == 0) else ())
            pe_g[mt] = v
            for k in range(32):
                wl = []
                if k == 0:
                    wl = [(US[mt % 2], lw_u[mt])]
                    if mt >= 2:
                        wl.append(("dve", dve_up[mt - 2]))
                for j in range(2):
                    v = OP("tensor", lambda e, mt=mt, k=k, j=j: e.matmul(
                        bank(4 * (mt % 2) + 2 + j), wu[mt % 2][:, k, :],
                        LN2[:, k, 512 * j:512 * (j + 1)],
                        start=(k == 0), stop=(k == 31), skip_group_check=True),
                        sem="pe" if (k == 31 and j == 1) else None,
                        waits=wl if (j == 0 and k == 0) else ())
            pe_u[mt] = v
            wl = [("pe", pe_g[mt])]
            if mt == 0:
                wl += ss_x1
            if mt >= 2:
                wl.append(("dve", dve_h[mt - 2]))
            act_si[mt] = OP("scalar", lambda e, mt=mt: e.activation(
                silu[mt % 2][:, :], ps_h[:, 2048 * (mt % 2):2048 * (mt % 2) + 1024],
                AF.Silu, scale=agt[:, mt:mt + 1]), "act", waits=wl)
            wl = [("pe", pe_u[mt])]
            if mt == 0:
                wl += ss_x1
            if mt >= 2:
                wl.append(("dve", dve_h[mt - 2]))
            dve_up[mt] = OP("vector", lambda e, mt=mt: e.tensor_scalar(
                up[mt % 2][:, :], ps_h[:, 2048 * (mt % 2) + 1024:2048 * (mt % 2) + 2048],
                aut[:, mt:mt + 1], None, ALU.mult), "dve", waits=wl)
            wl = [("act", act_si[mt]), ("dve", dve_up[mt])]
            if mt >= 2:
                wl.append((SS[mt % 2], lx_h[mt - 2]))
            dve_h[mt] = OP("vector", lambda e, mt=mt: e.tensor_tensor(
                h8[mt % 2][:, :], silu[mt % 2][:, :], up[mt % 2][:, :], ALU.mult),
                "dve", waits=wl)
            DMA("sync", SS[mt % 2], lambda mt=mt: h_d[0, mt],
                lambda mt=mt: h8[mt % 2][:, 0:512],
                waits=[("dve", dve_h[mt])])
            lx_h[mt] = DMA("sync", SS[mt % 2], lambda mt=mt: h_d[1, mt],
                           lambda mt=mt: h8[mt % 2][:, 512:1024])
        BAR()

        dve_fr_last = None
        lx_fr = {}
        ss_h = SSFULL()
        first_hq = True
        for g in range(8):
            base = cnt["pe"]
            lx_hq = {}
            lw_wd = {}
            for qi, (q0, nk) in enumerate(QUARTS):
                wl = []
                if first_hq:
                    wl += ss_h
                    if dbg:
                        wl.append(("dz", dz_ln2))
                if qi >= 2:
                    p0, pn = QUARTS[qi - 2]
                    wl.append(("pe", base + p0 + pn))
                elif g >= 1:
                    if qi == 0:
                        wl.append(("pe", base - IC + QUARTS[2][0] + QUARTS[2][1]))
                    else:
                        wl.append(("pe", base))
                first_hq = False
                v = None
                for rb in range(2):
                    v = DMA("sync", LS[qi % 2],
                            lambda qi=qi, nk=nk, rb=rb: hq[qi % 2][:, 0:nk, 512 * rb:512 * (rb + 1)],
                            lambda q0=q0, nk=nk, rb=rb: h_d[rb, q0:q0 + nk].rearrange("k p t -> p k t"),
                            waits=wl if rb == 0 else ())
                lx_hq[qi] = v
            for oi, (o0, no) in enumerate(OCTS):
                wl = []
                if oi >= 2:
                    p0, pn = OCTS[oi - 2]
                    wl.append(("pe", base + p0 + pn))
                elif g >= 1:
                    wl.append(("pe", base))
                lw_wd[oi] = DMA(
                    "gpsimd", WS[oi % 2], lambda oi=oi, no=no: wd[oi % 2][:, 0:no, :],
                    lambda o0=o0, no=no, g=g: wdT.rearrange(
                        "(c p) n -> p c n", p=128)[:, o0:o0 + no, 512 * g:512 * (g + 1)],
                    waits=wl)
            qi = 0
            v = None
            for k in range(IC):
                while qi + 1 < 4 and k >= QUARTS[qi][0] + QUARTS[qi][1]:
                    qi += 1
                oi = k // 8
                q0 = QUARTS[qi][0]
                wl = []
                if k == q0:
                    wl.append((LS[qi % 2], lx_hq[qi]))
                if k == OCTS[oi][0]:
                    wl.append((WS[oi % 2], lw_wd[oi]))
                if k == 0 and g >= 1:
                    wl.append(("dve", dve_fr_last))
                first = True
                for m in range(4):
                    for rb in range(2):
                        last = (m == 3 and rb == 1)
                        v = OP("tensor", lambda e, k=k, m=m, rb=rb, oi=oi, q0=q0, qi=qi: e.matmul(
                            bank(2 * m + rb), wd[oi % 2][:, k - 8 * oi, 128 * m:128 * (m + 1)],
                            hq[qi % 2][:, k - q0, 512 * rb:512 * (rb + 1)],
                            start=(k == 0), stop=(k == IC - 1), skip_group_check=True),
                            sem="pe" if last else None,
                            waits=wl if first else ())
                        first = False
            pe_done = v
            assert cnt["pe"] == base + IC
            ei = 0
            for m in range(4):
                for rb in range(2):
                    mg = g * 4 + m
                    st = ei % 2
                    wl = [("pe", pe_done)]
                    prev = lx_fr.get((g, ei - 2))
                    if prev:
                        wl.append((SS[ei % 2], prev))
                    dve_fr_last = OP("vector", lambda e, m=m, rb=rb, mg=mg, st=st: e.tensor_scalar(
                        frs[st][:, :], bank(2 * m + rb), adt[:, mg:mg + 1], None, ALU.mult),
                        "dve", waits=wl)
                    lx_fr[(g, ei)] = DMA("sync", SS[ei % 2],
                                         lambda rb=rb, mg=mg: fr_d[rb, mg],
                                         lambda st=st: frs[st][:, :],
                                         waits=[("dve", dve_fr_last)])
                    ei += 1
        BAR()

        ss_fr = SSFULL()
        d_apply_prev = None
        d_mr_prev = None
        first_gp = True
        first_sy = True
        for rb in range(2):
            lw_f, act_sf, pe_sf = {}, {}, {}
            for c in range(32):
                st = c % 2
                wl = []
                if first_gp:
                    wl += ss_fr
                    first_gp = False
                if c >= 2:
                    wl.append(("pe", pe_sf[c - 2]))
                lw_f[c] = DMA("gpsimd", US[c % 2], lambda st=st: fb[st][:, :],
                              lambda rb=rb, c=c: fr_d[rb, c], waits=wl)
                wl = [(US[c % 2], lw_f[c])]
                if c >= 2:
                    wl.append(("pe", pe_sf[c - 2]))
                act_sf[c] = OP("scalar", lambda e, st=st: e.activation(
                    sf[st][:, :], fb[st][:, :], AF.Square), "act", waits=wl)
                wl = [(US[c % 2], lw_f[c])]
                if rb == 1 and c == 0:
                    wl.append(("dve", d_mr_prev))
                OP("tensor", lambda e, c=c, st=st: e.matmul(
                    bank(4, 0, 1), ones_bf[:, :], fb[st][:, :],
                    start=(c == 0), stop=(c == 31), skip_group_check=True),
                    waits=wl)
                pe_sf[c] = OP("tensor", lambda e, c=c, st=st: e.matmul(
                    bank(5, 0, 1), ones_bf[:, :], sf[st][:, :],
                    start=(c == 0), stop=(c == 31), skip_group_check=True),
                    sem="pe", waits=[("act", act_sf[c])])
            d_mr_prev = ln_strips(512, 2048, 2560, pe_sf[31])
            wl = [("dve", d_mr_prev)]
            if rb == 1:
                wl.append(("dve", d_apply_prev))
            OP("tensor", lambda e: e.matmul(
                bank(6), ones_f[:, :], S1[0:1, 0:512],
                start=True, stop=True, skip_group_check=True), waits=wl)
            pe_bc3 = OP("tensor", lambda e: e.matmul(
                bank(7), ones_f[:, :], S2[0:1, 0:512],
                start=True, stop=True, skip_group_check=True), "pe")
            lx_xc, lx_fc, dve_tb, pool_o, lx_o = {}, {}, {}, {}, {}
            for c in range(32):
                st = c % 2
                wl = []
                if first_sy:
                    wl += ss_fr
                    first_sy = False
                if c >= 2:
                    wl.append(("pool", pool_o[c - 2]))
                lx_xc[c] = DMA("sync", LS[c % 2], lambda st=st: x1c[st][:, :],
                               lambda c=c, rb=rb: x1_d[c][:, 512 * rb:512 * (rb + 1)],
                               waits=wl)
                wl = []
                if c >= 2:
                    wl.append(("dve", dve_tb[c - 2]))
                lx_fc[c] = DMA("sync", LS2[c % 2], lambda st=st: frc[st][:, :],
                               lambda c=c, rb=rb: fr_d[rb, c], waits=wl)
                wl = [(LS2[c % 2], lx_fc[c]), ("pe", pe_bc3)]
                if c >= 2:
                    wl.append(("dve", dve_tb[c - 2]))
                OP("vector", lambda e, st=st: e.tensor_tensor(
                    tA[st][:, :], frc[st][:, :], bank(6), ALU.mult), waits=wl)
                wl = []
                if c >= 2:
                    wl.append(("pool", pool_o[c - 2]))
                dve_tb[c] = OP("vector", lambda e, st=st: e.tensor_tensor(
                    tB[st][:, :], tA[st][:, :], bank(7), ALU.subtract), "dve", waits=wl)
                wl = [("dve", dve_tb[c]), (LS[c % 2], lx_xc[c])]
                if c >= 2:
                    wl.append(("sy", lx_o[c - 2]))
                pool_o[c] = OP("gpsimd", lambda e, st=st: e.tensor_add(
                    osb[st][:, :], tB[st][:, :], x1c[st][:, :]), "pool", waits=wl)
                lx_o[c] = DMA("sync", "sy",
                              lambda c=c, rb=rb: y_t[128 * c:128 * (c + 1), 512 * rb:512 * (rb + 1)],
                              lambda st=st: osb[st][:, :],
                              waits=[("pool", pool_o[c])])
            d_apply_prev = dve_tb[31]
        if E == "sync":
            e.wait_ge(sems["sy"], cnt["sy"])
            if dbg:
                e.wait_ge(sems["dz"], cnt["dz"])
        return cnt

    for s in SEMS:
        sems[s] = nc.semaphore(s).__enter__()
    with (
        nc.Block() as blk,
    ):
        @blk.sync
        def _(e):
            walk("sync", e)

        @blk.gpsimd
        def _(e):
            walk("gpsimd", e)

        @blk.tensor
        def _(e):
            walk("tensor", e)

        @blk.scalar
        def _(e):
            walk("scalar", e)

        @blk.vector
        def _(e):
            walk("vector", e)

    return nc


def stage_inputs(inputs):
    x = np.ascontiguousarray(np.asarray(inputs["L0_attn_x"], dtype=np.float32))
    ws = {n: np.ascontiguousarray(np.asarray(inputs[k], dtype=np.float32))
          for n, k in (("wq", "L0_attn_wq"), ("wk", "L0_attn_wk"),
                       ("wv", "L0_attn_wv"), ("wo", "L0_attn_wo"))}
    wgT = np.ascontiguousarray(np.asarray(inputs["L0_ffn_Wg"], dtype=np.float32).T)
    wuT = np.ascontiguousarray(np.asarray(inputs["L0_ffn_Wu"], dtype=np.float32).T)
    wdT = np.ascontiguousarray(np.asarray(inputs["L0_ffn_Wd"], dtype=np.float32).T)
    agt = np.ascontiguousarray(np.asarray(inputs["L0_ffn_ag"], dtype=np.float32).reshape(86, 128).T)
    aut = np.ascontiguousarray(np.asarray(inputs["L0_ffn_au"], dtype=np.float32).reshape(86, 128).T)
    adt = np.ascontiguousarray(np.asarray(inputs["L0_ffn_ad"], dtype=np.float32).reshape(32, 128).T)
    pswap = np.zeros((128, 128), np.float32)
    for i in range(64):
        pswap[2 * i, 2 * i + 1] = 1.0
        pswap[2 * i + 1, 2 * i] = 1.0
    ident = np.eye(128, dtype=np.float32)
    consts = np.zeros((128, 16), np.float32)
    consts[:, 0] = -C_EXP
    consts[:, 1] = EPS

    inv = 1.0 / 10000.0 ** (np.arange(0, 128, 2)[:64].astype(np.float32) / 128.0)
    cls = {}
    for h in (0, 1):
        pos = np.concatenate([np.arange(h * 1024, (h + 1) * 1024),
                              np.arange((1 - h) * 1024, (2 - h) * 1024)]).astype(np.float32)
        f = np.outer(inv, pos)
        ropec = np.repeat(np.cos(f), 2, axis=0).astype(np.float32)
        ropes = np.repeat(np.sin(f), 2, axis=0).astype(np.float32)
        ropes[0::2] *= -1.0
        gq = h * 1024 + np.arange(1024)
        masks = np.zeros((28, 128, 512), np.float32)
        idx = 0
        for rb in range(2):
            rows = gq[512 * rb:512 * (rb + 1)]
            for kt in KTS[rb]:
                keys = pos[128 * kt:128 * (kt + 1)]
                masks[idx] = (keys[:, None] <= rows[None, :]).astype(np.float32)
                idx += 1
        cls[h] = (ropec, ropes, masks)

    in_maps = []
    for c in range(8):
        b, h = c // 2, c % 2
        own = x[b, h * 1024:(h + 1) * 1024]
        par = x[b, (1 - h) * 1024:(2 - h) * 1024]
        xbt = np.ascontiguousarray(np.concatenate([own, par], 0).T)
        ropec, ropes, masks = cls[h]
        in_maps.append({
            "xbt": xbt, "wq": ws["wq"], "wk": ws["wk"], "wv": ws["wv"],
            "wo": ws["wo"], "wgT": wgT, "wuT": wuT, "wdT": wdT,
            "agt": agt, "aut": aut, "adt": adt,
            "ropec": ropec, "ropes": ropes, "masks": masks,
            "pswap": pswap, "ident": ident, "consts": consts,
        })
    return in_maps


def kernel(**inputs):
    nc = build(dbg=False)
    in_maps = stage_inputs(inputs)
    res = run_bass_kernel_spmd(nc, in_maps, core_ids=list(range(8)))
    out = np.empty((4, 2048, 4096), np.float32)
    for c in range(8):
        b, h = c // 2, c % 2
        out[b, h * 1024:(h + 1) * 1024, :] = res.results[c]["y_t"].T
    return out
